# version 12
# baseline (speedup 1.0000x reference)
import contextlib
import numpy as np
import ml_dtypes
import concourse.bass as bass
import concourse.mybir as mybir

F32 = mybir.dt.float32
BF16 = mybir.dt.bfloat16
I32 = mybir.dt.int32
AF = mybir.ActivationFunctionType
AL = mybir.AluOpType
AX = mybir.AxisListType

FULL = dict(NTILES=32, OWN0=16, D=1024, I=2816, ITILE=256, SWEEP=4, GROUP=2)
TINY = dict(NTILES=2, OWN0=1, D=1024, I=2816, ITILE=256, SWEEP=2, GROUP=1)
MID = dict(NTILES=8, OWN0=4, D=1024, I=2816, ITILE=256, SWEEP=4, GROUP=2)


def build_nc(cfg, phases=("prep", "ffn1", "attn", "ffn2")):
    NT, OWN0, D, I = cfg["NTILES"], cfg["OWN0"], cfg["D"], cfg["I"]
    ITILE, SWEEP, GROUP = cfg["ITILE"], cfg["SWEEP"], cfg["GROUP"]
    DCH, ICH = D // 128, I // 128
    NIT = I // ITILE
    NOWN = NT - OWN0
    IBLK = ITILE // 128
    SCRW = D

    nc = bass.Bass("TRN2", target_bir_lowering=False, debug=False)

    d_x = nc.dram_tensor("xall", [NT * 128, D], F32, kind="ExternalInput").ap()
    d_out = nc.dram_tensor("yout", [NOWN * 128, D], F32, kind="ExternalOutput").ap()
    d_wg = nc.dram_tensor("wg", [I, D], F32, kind="ExternalInput").ap()
    d_wu = nc.dram_tensor("wu", [I, D], F32, kind="ExternalInput").ap()
    d_wd = nc.dram_tensor("wd", [D, I], F32, kind="ExternalInput").ap()
    d_awq = nc.dram_tensor("awq", [D, 128], F32, kind="ExternalInput").ap()
    d_awg = nc.dram_tensor("awg", [D, 128], F32, kind="ExternalInput").ap()
    d_awkvab = nc.dram_tensor("awkvab", [D, 258], F32, kind="ExternalInput").ap()
    d_awo = nc.dram_tensor("awo", [128, D], F32, kind="ExternalInput").ap()
    d_bkvab = nc.dram_tensor("bkvab", [128, 258], F32, kind="ExternalInput").ap()
    d_bq = nc.dram_tensor("bqcol", [128, 1], F32, kind="ExternalInput").ap()
    d_bg = nc.dram_tensor("bgcol", [128, 1], F32, kind="ExternalInput").ap()
    d_ba = nc.dram_tensor("bacol", [128, 1], F32, kind="ExternalInput").ap()
    d_bb = nc.dram_tensor("bbcol", [128, 1], F32, kind="ExternalInput").ap()
    d_bo = nc.dram_tensor("borep", [128, D], F32, kind="ExternalInput").ap()
    d_cut = nc.dram_tensor("c_ut", [128, 128], F32, kind="ExternalInput").ap()
    d_cid = nc.dram_tensor("c_id", [128, 128], F32, kind="ExternalInput").ap()
    d_cidb = nc.dram_tensor("c_idb", [128, 128], BF16, kind="ExternalInput").ap()
    d_caiu = nc.dram_tensor("c_aiu", [128, 128], F32, kind="ExternalInput").ap()
    d_cones = nc.dram_tensor("c_ones1", [1, 128], F32, kind="ExternalInput").ap()

    d_x1ctx = nc.dram_tensor("x1ctx", [max(OWN0, 1) * 128, D], F32, kind="Internal").ap()
    d_x1own = nc.dram_tensor("x1own", [NOWN * 128, D], F32, kind="Internal").ap()
    d_x2own = nc.dram_tensor("x2own", [NOWN * 128, D], F32, kind="Internal").ap()
    d_stg = [nc.dram_tensor(f"stash_g{h}", [DCH, ICH, 128, 128], BF16,
                            kind="Internal").ap() for h in range(2)]
    d_stu = [nc.dram_tensor(f"stash_u{h}", [DCH, ICH, 128, 128], BF16,
                            kind="Internal").ap() for h in range(2)]
    d_std = [nc.dram_tensor(f"stash_d{h}", [ICH, DCH, 128, 128], BF16,
                            kind="Internal").ap() for h in range(2)]

    es = contextlib.ExitStack()
    with es:
        def sb(name, shape, dt=F32):
            return es.enter_context(nc.sbuf_tensor(name, shape, dt)).ap()

        c_ut = sb("c_ut_s", [128, 128])
        c_id = sb("c_id_s", [128, 128])
        c_aiu = sb("c_aiu_s", [128, 128])
        c_ones = sb("c_ones_s", [1, 128])
        w_awq = sb("w_awq", [128, DCH, 128])
        w_awg = sb("w_awg", [128, DCH, 128])
        w_awkvab = sb("w_awkvab", [128, DCH, 258])
        w_awo = sb("w_awo", [128, D])
        b_kvab = sb("b_kvab_s", [128, 258])
        b_q = sb("b_q_s", [128, 1])
        b_g = sb("b_g_s", [128, 1])
        b_a = sb("b_a_s", [128, 1])
        b_b = sb("b_b_s", [128, 1])
        b_o = sb("b_o_s", [128, D])
        c_eps = sb("c_eps", [128, 1])

        xin = [sb(f"xin{i}", [128, SCRW]) for i in range(2)]
        scr1 = sb("scr1", [128, SCRW])
        scr2 = sb("scr2", [128, SCRW])
        scr3 = sb("scr3", [128, SCRW])
        scr4 = sb("scr4", [128, SCRW])
        sm_m = sb("sm_m", [128, SCRW // 16])
        sm_s = sb("sm_s", [128, SCRW // 16])
        sm_r = sb("sm_r", [128, SCRW // 16])
        xqT = [[sb(f"xqT{t}_{h}", [128, DCH, 128], BF16) for h in range(2)]
               for t in range(SWEEP)]
        wcol = [[sb(f"wcol{i}_{h}", [128, 2, DCH, ITILE], BF16) for h in range(2)]
                for i in range(2)]
        c_idb = sb("c_idb_s", [128, 128], BF16)
        sp_h = sb("sp_h", [128, SCRW], BF16)
        sp_l = sb("sp_l", [128, SCRW], BF16)
        shared2k = sb("shared2k", [128, 2048])
        hb = [sb(f"hb{i}", [128, SWEEP * ITILE]) for i in range(2)]
        hb.append(shared2k[:, 0:SWEEP * ITILE])
        hsilu = [sb(f"hsilu{i}", [128, ITILE]) for i in range(2)]
        hq_m = sb("hq_m", [128, SWEEP * ITILE // 16])
        hq_s = sb("hq_s", [128, SWEEP * ITILE // 16])
        hq_r = sb("hq_r", [128, SWEEP * ITILE // 16])
        hqT = [[sb(f"hqT{t}_{h}", [128, ICH, 128], BF16) for h in range(2)]
               for t in range(SWEEP)]
        wdn = [[sb(f"wdn{i}_{h}", [128, 512], BF16) for h in range(2)]
               for i in range(2)]
        xres = [sb(f"xres{i}", [128, 512]) for i in range(2)]
        stg1 = [sb(f"stg1_{i}", [128, 512]) for i in range(3)]

        a_xh2T = shared2k.rearrange("p (d n) -> p d n", n=GROUP * 128)
        a_K = sb("a_K", [128, GROUP, 128])
        a_V = sb("a_V", [128, GROUP, 128])
        a_QT = sb("a_QT", [128, GROUP, 128])
        a_GT = sb("a_GT", [128, GROUP, 128])
        a_acol = sb("a_acol", [128, GROUP])
        a_bcol = sb("a_bcol", [128, GROUP])
        a_KT = sb("a_KT", [128, 128])
        a_VT = sb("a_VT", [128, 128])
        a_la = sb("a_la", [128, 1])
        a_cum = sb("a_cum", [128, 1])
        a_cumrow = sb("a_cumrow", [1, 128])
        a_cumrep = sb("a_cumrep", [128, 128])
        a_gam = sb("a_gam", [128, 1])
        a_gamL = sb("a_gamL", [128, 1])
        a_wgt = sb("a_wgt", [128, 1])
        a_negb = sb("a_negb", [128, 1])
        a_gamrep = sb("a_gamrep", [128, 128])
        a_gtincl = sb("a_gtincl", [128, 128])
        a_kk = sb("a_kk", [128, 128])
        s_na = [sb(f"s_na{i}", [128, 128]) for i in range(2)]
        s_nb = [sb(f"s_nb{i}", [128, 128]) for i in range(2)]
        a_gk = sb("a_gk", [128, 128])
        a_kw = sb("a_kw", [128, 128])
        a_Z = sb("a_Z", [128, 128])
        a_Wm = sb("a_Wm", [128, 128])
        a_PF = sb("a_PF", [128, 128])
        a_WT = sb("a_WT", [128, 128])
        a_ET = sb("a_ET", [128, 128])
        a_F = sb("a_F", [128, 128])
        a_FT = sb("a_FT", [128, 128])
        a_M = [sb(f"a_M{i}", [128, 128]) for i in range(2)]
        a_MT = [sb(f"a_MT{i}", [128, 128]) for i in range(2)]
        a_MQ = sb("a_MQ", [128, 128])
        a_BT = sb("a_BT", [128, 128])
        a_OT = sb("a_OT", [128, 128])
        a_t1 = sb("a_t1", [128, 128])
        a_t2 = sb("a_t2", [128, 128])

        ps = [es.enter_context(nc.psum_tensor(f"ps{i}", [128, 512], F32)).ap()
              for i in range(8)]

        sems, cnt = {}, {}

        def sem(name):
            if name not in sems:
                sems[name] = es.enter_context(nc.semaphore(name))
                cnt[name] = 0
            return sems[name]

        def inc(instr, name, amt=1):
            s = sem(name)
            cnt[name] += amt
            instr.then_inc(s, amt)
            return cnt[name]

        def wait(eng, name, val=None):
            if name not in cnt and val is None:
                return
            s = sem(name)
            v = cnt[name] if val is None else val
            if v > 0:
                eng.wait_ge(s, v)

        V, A, T, G, SY = nc.vector, nc.scalar, nc.tensor, nc.gpsimd, nc.sync

        lastuse = {}

        def guard(eng, key):
            if key in lastuse:
                sname, val = lastuse[key]
                wait(eng, sname, val)

        def used(key, sname):
            sem(sname)
            lastuse[key] = (sname, cnt[sname])

        def load(out, in_, s=None, key=None):
            if key is not None:
                guard(SY, key)
            if s is None:
                s = "ld_" + (key if key is not None else "c")
            return inc(SY.dma_start(out=out, in_=in_), s, 16)

        def store(out, in_, slot):
            return inc(A.dma_start(out=out, in_=in_), f"st{slot}", 16)

        def wait_all_stores(eng):
            for i in range(4):
                wait(eng, f"st{i}")

        with nc.Block():
            load(c_ut, d_cut); load(c_id, d_cid); load(c_aiu, d_caiu)
            load(c_idb, d_cidb)
            load(c_ones, d_cones)
            load(w_awq, d_awq.rearrange("(c p) n -> p c n", p=128))
            load(w_awg, d_awg.rearrange("(c p) n -> p c n", p=128))
            load(w_awkvab, d_awkvab.rearrange("(c p) n -> p c n", p=128))
            load(w_awo, d_awo)
            load(b_kvab, d_bkvab); load(b_q, d_bq); load(b_g, d_bg)
            load(b_a, d_ba); load(b_b, d_bb); load(b_o, d_bo)
            CONST_LD = cnt["ld_c"]
            for eng in (V, A, T):
                wait(eng, "ld_c", CONST_LD)
            V.memset(c_eps, 1e-5)
            V.drain()

            def quant(src, out, F, mb, ssb, rb, t1, t2, t3, srckey=None):
                nb = F // 16
                v3 = src.rearrange("p (b k) -> p b k", k=16) if src.shape[1] == F \
                    else src
                V.tensor_reduce(out=mb[:, :nb], in_=v3, axis=AX.X, op=AL.max,
                                apply_absolute_value=True)
                V.drain()
                V.tensor_scalar_mul(out=ssb[:, :nb], in0=mb[:, :nb],
                                    scalar1=float(np.float32(1.0) / np.float32(6.0)))
                V.drain()
                V.tensor_scalar_max(out=ssb[:, :nb], in0=ssb[:, :nb], scalar1=1e-12)
                V.drain()
                V.tensor_scalar_mul(out=mb[:, :nb], in0=ssb[:, :nb], scalar1=0.5)
                V.drain()
                V.reciprocal(out=rb[:, :nb], in_=mb[:, :nb])
                V.drain()
                pv = inc(V.engine_nop(), "v")
                wait(A, "v", pv)
                ai = A.activation(out=t1[:, :F], in_=src, func=AF.Abs)
                pa = inc(ai, "a")
                wait(V, "a", pa)
                rb3 = rb[:, :nb].broadcast_to([128, nb, 16])
                t1v = t1[:, :F].rearrange("p (b k) -> p b k", k=16)
                V.tensor_tensor(out=t1v, in0=t1v, in1=rb3, op=AL.mult)
                V.drain()
                q2b = t3.bitcast(BF16)[:, 0:F]
                tb = t3.bitcast(BF16)[:, F:2 * F]
                V.tensor_scalar(out=q2b, in0=t1[:, :F], scalar1=0.5, scalar2=1.0,
                                op0=AL.is_gt, op1=AL.mult)
                V.drain()
                for thr, w in ((1.5, 1.0), (2.5, 1.0), (3.5, 1.0),
                               (5.0, 2.0), (7.0, 2.0), (10.0, 4.0)):
                    V.tensor_scalar(out=tb, in0=t1[:, :F], scalar1=thr,
                                    scalar2=w, op0=AL.is_gt, op1=AL.mult)
                    V.drain()
                    V.tensor_add(out=q2b, in0=q2b, in1=tb)
                    V.drain()
                mb3 = mb[:, :nb].broadcast_to([128, nb, 16])
                t2v = t2[:, :F].rearrange("p (b k) -> p b k", k=16)
                V.tensor_tensor(out=t2v, in0=q2b.rearrange("p (b k) -> p b k", k=16),
                                in1=mb3, op=AL.mult)
                V.drain()
                V.tensor_copy(out=sp_h[:, :F], in_=t2[:, :F])
                V.drain()
                V.tensor_tensor(out=sp_l[:, :F], in0=t2[:, :F], in1=sp_h[:, :F],
                                op=AL.subtract)
                V.drain()
                pv = inc(V.engine_nop(), "v")
                sgb = t1.bitcast(BF16)[:, 0:F]
                wait(A, "v", pv)
                ai = A.activation(out=sgb, in_=src, func=AF.Sign)
                pa = inc(ai, "a")
                if srckey is not None:
                    used(srckey, "a")
                wait(G, "a", pa)
                wait(G, "v", pv)
                G.tensor_mul(out=sp_h[:, :F], in0=sp_h[:, :F], in1=sgb)
                G.tensor_mul(out=sp_l[:, :F], in0=sp_l[:, :F], in1=sgb)
                G.drain()
                inc(G.engine_nop(), "g")
                used("scr1sign", "g")
                return pv

            tp_state = {"n": 0}

            def transpose128(src_ap, dst_ap, ident=None):
                bk = 6 + (tp_state["n"] % 2)
                tp_state["n"] += 1
                key = f"bank{bk}"
                if ident is None:
                    pst = ps[bk][:, 0:128]
                else:
                    pst = ps[bk].bitcast(BF16)[:, 0:128]
                guard(T, key)
                mm = T.transpose(pst, src_ap, c_id if ident is None else ident)
                pt = inc(mm, "tpm")
                wait(A, "tpm", pt)
                inc(A.copy(out=dst_ap, in_=pst), "tpe")
                used(key, "tpe")
                return mm

            def transpose_hl(colslice, dsth, dstl):
                transpose128(sp_h[:, colslice], dsth, ident=c_idb)
                return transpose128(sp_l[:, colslice], dstl, ident=c_idb)

            def ln_tile(src, out, srckey=None):
                V.tensor_reduce(out=sm_m[:, 0:1], in_=src[:, 0:D], axis=AX.X, op=AL.add)
                V.drain()
                V.tensor_scalar_mul(out=sm_m[:, 0:1], in0=sm_m[:, 0:1], scalar1=1.0 / D)
                V.drain()
                guard(V, "scr1sign")
                V.tensor_scalar(out=scr1[:, 0:D], in0=src[:, 0:D], scalar1=sm_m[:, 0:1],
                                scalar2=None, op0=AL.subtract)
                V.drain()
                pv = inc(V.engine_nop(), "v")
                if srckey is not None:
                    used(srckey, "v")
                wait(A, "v", pv)
                A.activation(out=scr2[:, 0:D], in_=scr1[:, 0:D], func=AF.Square,
                             accum_out=sm_s[:, 0:1])
                A.drain()
                ai = A.activation(out=sm_r[:, 0:1], in_=sm_s[:, 0:1], func=AF.Sqrt,
                                  scale=1.0 / D, bias=c_eps)
                pa = inc(ai, "a")
                wait(V, "a", pa)
                V.reciprocal(out=sm_r[:, 1:2], in_=sm_r[:, 0:1])
                V.drain()
                V.tensor_scalar_mul(out=out[:, 0:D], in0=scr1[:, 0:D],
                                    scalar1=sm_r[:, 1:2])
                V.drain()
                return inc(V.engine_nop(), "v")

            stgb = [sb(f"stgb{i}", [128, 128], BF16) for i in range(2)]

            def prep_rowtile(dram_ap, Fq, blk_dst, rt2):
                b = xin[rt2 % 2]
                lvs = f"ld_xin{rt2 % 2}"
                lv = load(b[:, :Fq], dram_ap, key=f"xin{rt2 % 2}")
                wait(V, lvs, lv)
                wait(V, "tpm")
                quant(b[:, :Fq], scr4[:, :Fq], Fq, sm_m, sm_s, sm_r,
                      scr1, scr2, scr3, srckey=f"xin{rt2 % 2}")
                wait(T, "v")
                wait(T, "g")
                for bi in range(Fq // 128):
                    for h in range(2):
                        sl = h % 2
                        sgb = stgb[sl]
                        guard(A, f"stg{sl}")
                        src = (sp_h if h == 0 else sp_l)[:, bi * 128:(bi + 1) * 128]
                        transpose128(src, sgb, ident=c_idb)
                        wait(A, "tpe")
                        store(blk_dst(h, bi), sgb, sl)
                        used(f"stg{sl}", f"st{sl}")

            rt2 = 0
            for rt in (range(ICH) if "prep" in phases else ()):
                prep_rowtile(d_wg[rt * 128:(rt + 1) * 128, :], D,
                             lambda h, bi, rt=rt: d_stg[h][bi, rt], rt2); rt2 += 1
            for rt in (range(ICH) if "prep" in phases else ()):
                prep_rowtile(d_wu[rt * 128:(rt + 1) * 128, :], D,
                             lambda h, bi, rt=rt: d_stu[h][bi, rt], rt2); rt2 += 1
            for rt in (range(DCH) if "prep" in phases else ()):
                c0 = 0
                while c0 < I:
                    cw = min(D, I - c0)
                    prep_rowtile(d_wd[rt * 128:(rt + 1) * 128, c0:c0 + cw], cw,
                                 lambda h, bi, rt=rt, c0=c0:
                                 d_std[h][c0 // 128 + bi, rt], rt2)
                    rt2 += 1
                    c0 += cw
            wait_all_stores(SY)

            def emit_ffn(tiles, src_x, src_res, dst_of):
                nsw = (len(tiles) + SWEEP - 1) // SWEEP

                def emit_pre(stiles):
                    wait(A, "t")
                    for si, t in enumerate(stiles):
                        b = xin[si % 2]
                        lvs = f"ld_xin{si % 2}"
                        lv = load(b[:, 0:D], src_x(t), key=f"xin{si % 2}")
                        wait(V, lvs, lv)
                        ln_tile(b, scr4, srckey=f"xin{si % 2}")
                        wait(V, "tpm")
                        quant(scr4[:, 0:D], scr4[:, 0:D], D, sm_m, sm_s, sm_r,
                              scr1, scr2, scr3)
                        wait(T, "v")
                        wait(T, "g")
                        for dc in range(DCH):
                            transpose_hl(slice(dc * 128, (dc + 1) * 128),
                                         xqT[si][0][:, dc, :], xqT[si][1][:, dc, :])
                        wait(V, "tpe")
                    wait(T, "tpe")

                def emit_gateup(stiles):
                    ns = len(stiles)
                    pending_it = []

                    def flush_itile():
                        it0, hs0, ns0 = pending_it.pop(0)
                        F0 = ns0 * ITILE
                        wait(V, "g")
                        wait(A, "g")
                        wait(V, "tpm")
                        quant(hb[hs0][:, :F0], hb[hs0][:, :F0], F0,
                              hq_m, hq_s, hq_r, scr1, scr2, scr3)
                        wait(T, "v")
                        wait(T, "g")
                        for si0 in range(ns0):
                            for bi in range(IBLK):
                                col = si0 * ITILE + bi * 128
                                transpose_hl(slice(col, col + 128),
                                             hqT[si0][0][:, it0 * IBLK + bi, :],
                                             hqT[si0][1][:, it0 * IBLK + bi, :])
                        used(f"hb{hs0}", "tpm")

                    def load_wcol(slot, it0):
                        for mi, stash in enumerate((d_stg, d_stu)):
                            for h in range(2):
                                v5 = wcol[slot][h].rearrange(
                                    "p m d (b n) -> p m d b n", b=IBLK)
                                for bb in range(IBLK):
                                    load(v5[:, mi, :, bb, :],
                                         stash[h][:, it0 * IBLK + bb]
                                         .rearrange("d p n -> p d n"),
                                         s=f"wld{slot}", key=f"wcol{slot}")

                    for it in range(NIT):
                        if it == 0:
                            load_wcol(0, 0)
                        if it + 1 < NIT:
                            load_wcol((it + 1) % 2, it + 1)
                        wb = wcol[it % 2]
                        wait(T, f"wld{it % 2}")
                        hs = it % 3
                        for si in range(ns):
                            pp = ps[si % 2]
                            guard(T, f"bank{si % 2}")
                            passes = ((0, 0), (0, 1), (1, 0))
                            np_ = len(passes)
                            for mi2 in range(2):
                                reg = pp[:, mi2 * ITILE:(mi2 + 1) * ITILE]
                                k = 0
                                for xh_, wh_ in passes:
                                    for dc in range(DCH):
                                        mm = T.matmul(
                                            reg, xqT[si][xh_][:, dc, :],
                                            wb[wh_][:, mi2, dc, :],
                                            start=(k == 0),
                                            stop=(k == np_ * DCH - 1))
                                        k += 1
                            pt = inc(mm, "t")
                            if si == ns - 1:
                                used(f"wcol{it % 2}", "t")
                            wait(A, "t", pt)
                            hp = si % 2
                            guard(A, f"hsilu{hp}")
                            A.activation(out=hsilu[hp], in_=pp[:, 0:ITILE],
                                         func=AF.Silu)
                            hu = xres[hp]
                            guard(A, f"xres{hp}")
                            ai = A.copy(out=hu[:, 0:ITILE], in_=pp[:, ITILE:2 * ITILE])
                            pa = inc(ai, "a")
                            used(f"bank{si % 2}", "a")
                            wait(G, "a", pa)
                            if si == 0:
                                guard(G, f"hb{hs}")
                            G.tensor_mul(out=hb[hs][:, si * ITILE:(si + 1) * ITILE],
                                         in0=hsilu[hp], in1=hu[:, 0:ITILE])
                            G.drain()
                            pg = inc(G.engine_nop(), "g")
                            used(f"xres{hp}", "g")
                            used(f"hsilu{hp}", "g")
                        pending_it.append((it, hs, ns))
                        if len(pending_it) >= 3:
                            flush_itile()
                    while pending_it:
                        flush_itile()
                    wait(T, "tpe")

                def emit_down(stiles):
                    ns = len(stiles)
                    for hf0 in range(0, ns, 3):
                        htiles = stiles[hf0:hf0 + 3]
                        for dt in range(D // 512):
                            pts = []

                            def load_wdn(ic0):
                                sl = ic0 % 2
                                for h in range(2):
                                    load(wdn[sl][h].rearrange("p (b n) -> p b n", b=4),
                                         d_std[h][ic0, dt * 4:(dt + 1) * 4]
                                         .rearrange("b p n -> p b n"),
                                         s=f"dld{sl}", key=f"wdn{sl}")

                            load_wdn(0)
                            for ic in range(ICH):
                                wbD = wdn[ic % 2]
                                lv = cnt[f"dld{ic % 2}"]
                                if ic + 1 < ICH:
                                    load_wdn(ic + 1)
                                wait(T, f"dld{ic % 2}", lv)
                                mms = []
                                for si2 in range(len(htiles)):
                                    si = hf0 + si2
                                    pd = ps[2 + si2]
                                    if ic == 0:
                                        guard(T, f"bank{2 + si2}")
                                    for pi, (xh_, wh_) in enumerate(
                                            ((0, 0), (0, 1), (1, 0))):
                                        mms.append(T.matmul(
                                            pd[:, :512], hqT[si][xh_][:, ic, :],
                                            wbD[wh_], start=(ic == 0 and pi == 0),
                                            stop=(ic == ICH - 1 and pi == 2)))
                                inc(mms[-1], f"dfree{ic % 2}")
                                used(f"wdn{ic % 2}", f"dfree{ic % 2}")
                                if ic == ICH - 1:
                                    for si2 in range(len(htiles)):
                                        m_ = mms[si2 * 3 + 2]
                                        if si2 == len(htiles) - 1:
                                            pts.append((f"dfree{ic % 2}",
                                                        cnt[f"dfree{ic % 2}"]))
                                        else:
                                            pts.append(("t", inc(m_, "t")))
                            for si2, t in enumerate(htiles):
                                pd = ps[2 + si2]
                                xr = xres[si2 % 2]
                                lvs = f"ld_xres{si2 % 2}"
                                lv = load(xr, src_res(t)[:, dt * 512:(dt + 1) * 512],
                                          key=f"xres{si2 % 2}")
                                wait(A, pts[si2][0], pts[si2][1])
                                sl = si2
                                sgb = stg1[sl]
                                guard(A, f"stg{sl}")
                                ai = A.activation(out=sgb, in_=pd[:, :512],
                                                  func=AF.Copy, scale=0.5)
                                pa = inc(ai, "a")
                                used(f"bank{2 + si2}", "a")
                                wait(G, "a", pa)
                                wait(G, lvs, lv)
                                G.tensor_add(out=sgb, in0=sgb, in1=xr)
                                G.drain()
                                pg = inc(G.engine_nop(), "g")
                                used(f"xres{si2 % 2}", "g")
                                wait(A, "g", pg)
                                store(dst_of(t)[:, dt * 512:(dt + 1) * 512], sgb, sl)
                                used(f"stg{sl}", f"st{sl}")

                for sw in range(nsw):
                    stiles = stiles_of = tiles[sw * SWEEP:(sw + 1) * SWEEP]
                    if sw == 0:
                        emit_pre(stiles)
                    emit_gateup(stiles)
                    if sw + 1 < nsw:
                        emit_pre(tiles[(sw + 1) * SWEEP:(sw + 2) * SWEEP])
                    emit_down(stiles)
                wait_all_stores(SY)

            if "ffn1" in phases: emit_ffn(
                tiles=list(range(NT)),
                src_x=lambda t: d_x[t * 128:(t + 1) * 128, :],
                src_res=lambda t: d_x[t * 128:(t + 1) * 128, :],
                dst_of=lambda t: (d_x1ctx[t * 128:(t + 1) * 128, :] if t < OWN0 else
                                  d_x1own[(t - OWN0) * 128:(t - OWN0 + 1) * 128, :]),
            )

            def x1_src(t):
                return (d_x1ctx[t * 128:(t + 1) * 128, :] if t < OWN0 else
                        d_x1own[(t - OWN0) * 128:(t - OWN0 + 1) * 128, :])

            V.memset(a_M[0], 0.0)
            V.memset(a_MT[0], 0.0)
            V.drain()
            inc(V.engine_nop(), "v")
            mcur = {"i": 0}

            cs_state = {"n": 0}

            def cslot():
                bk = 2 + (cs_state["n"] % 2)
                cs_state["n"] += 1
                key = f"bank{bk}"
                pst = ps[bk][:, 0:128]
                guard(T, key)
                return pst, key

            def cmm(lhsT, rhs, dst, extra=None, n=128):
                pst, key = cslot()
                pm = inc(T.matmul(pst[:, 0:n], lhsT, rhs, start=True, stop=True), "csm")
                wait(V, "csm", pm)
                if extra is None:
                    V.tensor_copy(out=dst, in_=pst[:, 0:n])
                else:
                    extra(dst, pst[:, 0:n])
                V.drain()
                inc(V.engine_nop(), "v")
                used(key, "v")

            def emit_chunk(si, own, t_own):
                nx = 1 - mcur["i"]
                Mc, MTc = a_M[mcur["i"]], a_MT[mcur["i"]]
                Mn, MTn = a_M[nx], a_MT[nx]
                acol = a_acol[:, si:si + 1]
                bcol = a_bcol[:, si:si + 1]
                Kt = a_K[:, si, :]
                Vt = a_V[:, si, :]
                QTt = a_QT[:, si, :]
                GTt = a_GT[:, si, :]

                wait(A, "v", cnt["v"])
                ai = ai = ai = ai = ai = ai = ai = A.activation(out=a_la, in_=acol, func=AF.Ln)
                pa = inc(ai, "a")
                wait(T, "a", pa)
                cmm(c_ut, a_la, a_cum, n=1)
                wait(T, "v")
                pst, key = cslot()
                ptr = inc(T.transpose(pst[0:1, 0:128], a_cum, c_id), "csm")
                wait(V, "csm", ptr)
                V.tensor_copy(out=a_cumrow, in_=pst[0:1, 0:128])
                V.drain()
                pv = inc(V.engine_nop(), "v")
                used(key, "v")
                wait(T, "v", pv)
                cmm(c_ones, a_cumrow, a_cumrep)
                pv = cnt["v"]
                wait(A, "v", pv)
                A.activation(out=a_gam, in_=a_cum, func=AF.Exp)
                ai = A.activation(out=a_gamL, in_=a_cumrep[:, 127:128], func=AF.Exp)
                pa = inc(ai, "a")
                wait(V, "a", pa)
                V.tensor_sub(out=a_wgt, in0=a_cumrep[:, 127:128], in1=a_cum)
                V.drain()
                pv = inc(V.engine_nop(), "v")
                wait(A, "v", pv)
                ai = A.activation(out=a_wgt, in_=a_wgt, func=AF.Exp)
                pa = inc(ai, "a")
                wait(V, "a", pa)
                V.tensor_mul(out=a_wgt, in0=a_wgt, in1=bcol)
                V.tensor_scalar_mul(out=a_negb, in0=bcol, scalar1=-1.0)
                V.drain()
                V.tensor_scalar(out=a_t1, in0=a_cumrep, scalar1=a_cum, scalar2=None,
                                op0=AL.subtract)
                V.drain()
                V.tensor_add(out=a_t1, in0=a_t1, in1=c_aiu)
                V.drain()
                pv = inc(V.engine_nop(), "v")
                wait(A, "v", pv)
                ai = A.activation(out=a_gtincl, in_=a_t1, func=AF.Exp)
                if own:
                    ai = A.activation(out=a_gamrep, in_=a_cumrep, func=AF.Exp)
                pa = inc(ai, "a")
                wait(V, "a", pa)
                V.tensor_sub(out=a_t1, in0=a_gtincl, in1=c_id)
                V.drain()
                pv = inc(V.engine_nop(), "v")

                wait(T, "v", pv)
                transpose128(Kt, a_KT)
                if own:
                    transpose128(Vt, a_VT)
                wait(T, "tpe")
                cmm(a_KT, a_KT, a_kk)
                V.tensor_mul(out=s_nb[0], in0=a_t1, in1=a_kk)
                V.drain()
                V.tensor_scalar_mul(out=s_nb[0], in0=s_nb[0], scalar1=a_negb)
                V.tensor_scalar_mul(out=a_gk, in0=Kt, scalar1=a_gam)
                V.tensor_scalar_mul(out=a_kw, in0=Kt, scalar1=a_wgt)
                V.drain()
                pv = inc(V.engine_nop(), "v")
                wait(T, "v", pv)
                transpose128(s_nb[0], s_na[0])
                wait(V, "tpe")

                V.tensor_copy(out=a_Z, in_=a_kw)
                if own:
                    V.tensor_copy(out=a_Wm, in_=a_gk)
                V.drain()
                pv = inc(V.engine_nop(), "v")
                if own:
                    wait(T, "v", pv)
                    wait(T, "tpe")
                    cmm(a_VT, QTt, a_PF)
                cur = 0
                for m in range(7):
                    wait(T, "v")
                    cmm(s_na[cur], a_Z, a_Z,
                        extra=lambda d, p: V.tensor_add(out=d, in0=d, in1=p))
                    if own:
                        cmm(s_nb[cur], a_Wm, a_Wm,
                            extra=lambda d, p: V.tensor_add(out=d, in0=d, in1=p))
                        cmm(s_nb[cur], a_PF, a_PF,
                            extra=lambda d, p: V.tensor_add(out=d, in0=d, in1=p))
                    if m < 6:
                        wait(T, "v")
                        cmm(s_nb[cur], s_na[cur], s_na[1 - cur])
                        cmm(s_na[cur], s_nb[cur], s_nb[1 - cur])
                        cur = 1 - cur

                wait(T, "v")
                cmm(a_gk, a_Z, a_ET)
                cmm(a_Z, Vt, a_F)
                cmm(Vt, a_Z, a_FT)
                if own:
                    transpose128(a_Wm, a_WT)

                if own:
                    wait(T, "v")
                    cmm(MTc, QTt, a_MQ)
                    wait(T, "v")
                    wait(T, "tpe")
                    cmm(a_WT, a_MQ, a_BT,
                        extra=lambda d, p: V.scalar_tensor_tensor(
                            out=d, in0=p, scalar=-1.0, in1=a_PF,
                            op0=AL.mult, op1=AL.add))
                    V.tensor_mul(out=a_BT, in0=a_BT, in1=a_gtincl)
                    V.drain()
                    V.tensor_scalar_mul(out=a_BT, in0=a_BT, scalar1=bcol)
                    V.drain()
                    pv = inc(V.engine_nop(), "v")
                    wait(T, "v", pv)
                    cmm(Kt, a_BT, a_t2)
                    V.tensor_mul(out=a_t1, in0=a_MQ, in1=a_gamrep)
                    V.drain()
                    V.tensor_add(out=a_t1, in0=a_t1, in1=a_t2)
                    V.drain()
                    V.tensor_mul(out=a_OT, in0=a_t1, in1=GTt)
                    V.drain()
                    pv = inc(V.engine_nop(), "v")
                    wait(T, "v", pv)
                    for half in range(D // 512):
                        guard(T, "bank5")
                        mmO = T.matmul(ps[5][:, 0:512], a_OT,
                                       w_awo[:, half * 512:(half + 1) * 512],
                                       start=True, stop=True)
                        pt = inc(mmO, "t")
                        xr = xres[half % 2]
                        lvs = f"ld_xres{half % 2}"
                        lv = load(xr, d_x1own[t_own * 128:(t_own + 1) * 128,
                                              half * 512:(half + 1) * 512],
                                  key=f"xres{half % 2}")
                        wait(V, lvs, lv)
                        wait(V, "t", pt)
                        sgb = stg1[half]
                        guard(V, f"stg{half}")
                        V.tensor_add(out=sgb, in0=ps[5][:, 0:512], in1=xr)
                        V.drain()
                        V.tensor_add(out=sgb, in0=sgb,
                                     in1=b_o[:, half * 512:(half + 1) * 512])
                        V.drain()
                        pv = inc(V.engine_nop(), "v")
                        used("bank5", "v")
                        used(f"xres{half % 2}", "v")
                        wait(A, "v", pv)
                        store(d_x2own[t_own * 128:(t_own + 1) * 128,
                                      half * 512:(half + 1) * 512], sgb, half)
                        used(f"stg{half}", f"st{half}")

                wait(T, "v")
                cmm(a_ET, Mc, a_t1,
                    extra=lambda d, p: V.scalar_tensor_tensor(
                        out=d, in0=p, scalar=-1.0, in1=a_F, op0=AL.mult, op1=AL.add))
                cmm(Mc, a_ET, a_t2,
                    extra=lambda d, p: V.scalar_tensor_tensor(
                        out=d, in0=p, scalar=-1.0, in1=a_FT, op0=AL.mult, op1=AL.add))
                V.scalar_tensor_tensor(out=Mn, in0=Mc, scalar=a_gamL, in1=a_t1,
                                       op0=AL.mult, op1=AL.add)
                V.scalar_tensor_tensor(out=MTn, in0=MTc, scalar=a_gamL, in1=a_t2,
                                       op0=AL.mult, op1=AL.add)
                V.drain()
                inc(V.engine_nop(), "v")
                mcur["i"] = nx

            ngroups = NT // GROUP if "attn" in phases else 0
            for g in range(ngroups):
                gtiles = list(range(g * GROUP, (g + 1) * GROUP))
                own_g = gtiles[0] >= OWN0
                wait(A, "t")
                wait(A, "tpm")
                wait(V, "t")
                for si, t in enumerate(gtiles):
                    b = xin[si % 2]
                    lvs = f"ld_xin{si % 2}"
                    lv = load(b[:, 0:D], x1_src(t), key=f"xin{si % 2}")
                    wait(V, lvs, lv)
                    ln_tile(b, scr4, srckey=f"xin{si % 2}")
                    wait(T, "v")
                    for dc in range(DCH):
                        transpose128(scr4[:, dc * 128:(dc + 1) * 128],
                                     a_xh2T[:, dc, si * 128:(si + 1) * 128])
                    wait(V, "tpe")
                wait(T, "tpe")

                for si, t in enumerate(gtiles):
                    pp = ps[si % 2]
                    guard(T, f"bank{si % 2}")
                    for dc in range(DCH):
                        mm = T.matmul(pp[:, 0:258],
                                      a_xh2T[:, dc, si * 128:(si + 1) * 128],
                                      w_awkvab[:, dc, :], start=(dc == 0),
                                      stop=(dc == DCH - 1))
                    pt = inc(mm, "t")
                    wait(V, "t", pt)
                    V.tensor_add(out=scr1[:, 0:256], in0=pp[:, 0:256],
                                 in1=b_kvab[:, 0:256])
                    V.drain()
                    pv = inc(V.engine_nop(), "v")
                    wait(A, "v", pv)
                    A.activation(out=scr2[:, 0:128], in_=scr1[:, 0:128], func=AF.Square,
                                 accum_out=sm_s[:, 0:1])
                    A.drain()
                    A.activation(out=sm_s[:, 1:2], in_=sm_s[:, 0:1], func=AF.Sqrt)
                    A.activation(out=a_acol[:, si:si + 1], in_=pp[:, 256:257],
                                 func=AF.Sigmoid, bias=b_a)
                    ai = A.activation(out=a_bcol[:, si:si + 1], in_=pp[:, 257:258],
                                      func=AF.Sigmoid, bias=b_b)
                    pa = inc(ai, "a")
                    used(f"bank{si % 2}", "a")
                    wait(V, "a", pa)
                    V.tensor_scalar_max(out=sm_s[:, 1:2], in0=sm_s[:, 1:2],
                                        scalar1=1e-12)
                    V.drain()
                    pv = inc(V.engine_nop(), "v")
                    wait(A, "v", pv)
                    V.reciprocal(out=sm_r[:, 0:1], in_=sm_s[:, 1:2])
                    V.drain()
                    V.tensor_scalar_mul(out=a_K[:, si, :], in0=scr1[:, 0:128],
                                        scalar1=sm_r[:, 0:1])
                    V.tensor_copy(out=a_V[:, si, :], in_=scr1[:, 128:256])
                    V.drain()
                    inc(V.engine_nop(), "v")
                if own_g:
                    for si, t in enumerate(gtiles):
                        pp = ps[si % 2]
                        guard(T, f"bank{si % 2}")
                        for dc in range(DCH):
                            T.matmul(pp[:, 0:128], w_awq[:, dc, :],
                                     a_xh2T[:, dc, si * 128:(si + 1) * 128],
                                     start=(dc == 0), stop=(dc == DCH - 1))
                        for dc in range(DCH):
                            mm = T.matmul(pp[:, 128:256], w_awg[:, dc, :],
                                          a_xh2T[:, dc, si * 128:(si + 1) * 128],
                                          start=(dc == 0), stop=(dc == DCH - 1))
                        pt = inc(mm, "t")
                        wait(V, "t", pt)
                        V.tensor_scalar(out=a_QT[:, si, :], in0=pp[:, 0:128],
                                        scalar1=b_q, scalar2=None, op0=AL.add)
                        V.drain()
                        pv = inc(V.engine_nop(), "v")
                        wait(A, "v", pv)
                        ai = A.activation(out=a_GT[:, si, :], in_=pp[:, 128:256],
                                          func=AF.Sigmoid, bias=b_g)
                        pa = inc(ai, "a")
                        used(f"bank{si % 2}", "a")
                        wait(V, "a", pa)

                for si, t in enumerate(gtiles):
                    emit_chunk(si, own_g, (t - OWN0) if own_g else 0)

            wait_all_stores(SY)
            if "ffn2" in phases: emit_ffn(
                tiles=list(range(NOWN)),
                src_x=lambda t: d_x2own[t * 128:(t + 1) * 128, :],
                src_res=lambda t: d_x2own[t * 128:(t + 1) * 128, :],
                dst_of=lambda t: d_out[t * 128:(t + 1) * 128, :],
            )

            if "ffn2" not in phases:
                src_dump = d_x2own if "attn" in phases else d_x1own
                for t in range(0, NOWN * 128, 128):
                    load(xin[0][:, 0:D], src_dump[t:t + 128, :],
                         s="dbg_ld", key="xin0")
                    wait(A, "dbg_ld")
                    store(d_out[t:t + 128, :], xin[0][:, 0:D], 0)
                    used("xin0", "st0")
                    wait(SY, "st0")
            wait_all_stores(SY)

    return nc


def make_host_inputs(cfg, ins):
    NT, OWN0, D, I = cfg["NTILES"], cfg["OWN0"], cfg["D"], cfg["I"]
    NOWN = NT - OWN0
    x = np.ascontiguousarray(np.asarray(ins["L0_attn_x"], dtype=np.float32))
    B = x.shape[0]
    Wk = np.asarray(ins["L0_attn_Wk"], np.float32)
    Wv = np.asarray(ins["L0_attn_Wv"], np.float32)
    Wa = np.asarray(ins["L0_attn_Wa"], np.float32)
    Wb = np.asarray(ins["L0_attn_Wb"], np.float32)
    wkvab = np.ascontiguousarray(np.concatenate([Wk, Wv, Wa, Wb], axis=1))
    bk = np.asarray(ins["L0_attn_bk"], np.float32)
    bv = np.asarray(ins["L0_attn_bv"], np.float32)
    bkvab = np.ascontiguousarray(
        np.broadcast_to(np.concatenate([bk, bv, [0.0, 0.0]]).astype(np.float32)[None, :],
                        (128, 258)))
    common = {
        "wg": np.ascontiguousarray(np.asarray(ins["L0_ffn_w_gate"], np.float32)),
        "wu": np.ascontiguousarray(np.asarray(ins["L0_ffn_w_up"], np.float32)),
        "wd": np.ascontiguousarray(np.asarray(ins["L0_ffn_w_down"], np.float32)),
        "awq": np.ascontiguousarray(np.asarray(ins["L0_attn_Wq"], np.float32)),
        "awg": np.ascontiguousarray(np.asarray(ins["L0_attn_Wg"], np.float32)),
        "awkvab": wkvab,
        "awo": np.ascontiguousarray(np.asarray(ins["L0_attn_Wo"], np.float32)),
        "bkvab": bkvab,
        "bqcol": np.ascontiguousarray(np.asarray(ins["L0_attn_bq"], np.float32)[:, None]),
        "bgcol": np.ascontiguousarray(np.asarray(ins["L0_attn_bg"], np.float32)[:, None]),
        "bacol": np.ascontiguousarray(
            np.broadcast_to(np.asarray(ins["L0_attn_ba"], np.float32), (128,))[:, None]),
        "bbcol": np.ascontiguousarray(
            np.broadcast_to(np.asarray(ins["L0_attn_bb"], np.float32), (128,))[:, None]),
        "borep": np.ascontiguousarray(
            np.broadcast_to(np.asarray(ins["L0_attn_bo"], np.float32)[None, :], (128, D))),
        "c_ut": np.triu(np.ones((128, 128), np.float32)),
        "c_id": np.eye(128, dtype=np.float32),
        "c_idb": np.eye(128, dtype=np.float32).astype(ml_dtypes.bfloat16),
        "c_aiu": ((np.triu(np.ones((128, 128), np.float32)) - 1.0) * 1e9).astype(np.float32),
        "c_ones1": np.ones((1, 128), np.float32),
    }
    half = OWN0 * 128
    in_maps = []
    for c in range(2 * B):
        bi, hi = c // 2, c % 2
        xa = np.zeros((NT * 128, D), np.float32)
        if hi == 0:
            xa[half:half + NOWN * 128] = x[bi, 0:NOWN * 128]
        else:
            xa[0:half] = x[bi, 0:half]
            xa[half:half + NOWN * 128] = x[bi, half:half + NOWN * 128]
        in_maps.append(dict(common, xall=np.ascontiguousarray(xa)))
    return in_maps


def assemble(cfg, results, B, T):
    NOWN = cfg["NTILES"] - cfg["OWN0"]
    D = cfg["D"]
    out = np.zeros((B, T, D), np.float32)
    half = cfg["OWN0"] * 128
    for c, r in enumerate(results):
        bi, hi = c // 2, c % 2
        if hi == 0:
            out[bi, 0:NOWN * 128] = r["yout"]
        else:
            out[bi, half:half + NOWN * 128] = r["yout"]
    return out


from concourse.bass_utils import run_bass_kernel_spmd

_NC_CACHE = {}


def kernel(**inputs):
    cfg = FULL
    if "nc" not in _NC_CACHE:
        _NC_CACHE["nc"] = build_nc(cfg)
    nc = _NC_CACHE["nc"]
    in_maps = make_host_inputs(cfg, inputs)
    res = run_bass_kernel_spmd(nc, in_maps, core_ids=list(range(8)))
    x = np.asarray(inputs["L0_attn_x"])
    return assemble(cfg, res.results, x.shape[0], x.shape[1])


# version 13
# speedup vs baseline: 1.1304x; 1.1304x over previous
import contextlib
import numpy as np
import ml_dtypes
import concourse.bass as bass
import concourse.mybir as mybir

F32 = mybir.dt.float32
BF16 = mybir.dt.bfloat16
I32 = mybir.dt.int32
AF = mybir.ActivationFunctionType
AL = mybir.AluOpType
AX = mybir.AxisListType

FULL = dict(NTILES=32, OWN0=16, D=1024, I=2816, ITILE=256, SWEEP=4, GROUP=2)
TINY = dict(NTILES=2, OWN0=1, D=1024, I=2816, ITILE=256, SWEEP=2, GROUP=1)
MID = dict(NTILES=8, OWN0=4, D=1024, I=2816, ITILE=256, SWEEP=4, GROUP=2)


def build_nc(cfg, phases=("prep", "ffn1", "attn", "ffn2")):
    NT, OWN0, D, I = cfg["NTILES"], cfg["OWN0"], cfg["D"], cfg["I"]
    ITILE, SWEEP, GROUP = cfg["ITILE"], cfg["SWEEP"], cfg["GROUP"]
    DCH, ICH = D // 128, I // 128
    NIT = I // ITILE
    NOWN = NT - OWN0
    IBLK = ITILE // 128
    SCRW = D

    nc = bass.Bass("TRN2", target_bir_lowering=False, debug=False)

    d_x = nc.dram_tensor("xall", [NT * 128, D], F32, kind="ExternalInput").ap()
    d_out = nc.dram_tensor("yout", [NOWN * 128, D], F32, kind="ExternalOutput").ap()
    d_wg = nc.dram_tensor("wg", [I, D], F32, kind="ExternalInput").ap()
    d_wu = nc.dram_tensor("wu", [I, D], F32, kind="ExternalInput").ap()
    d_wd = nc.dram_tensor("wd", [D, I], F32, kind="ExternalInput").ap()
    d_awq = nc.dram_tensor("awq", [D, 128], F32, kind="ExternalInput").ap()
    d_awg = nc.dram_tensor("awg", [D, 128], F32, kind="ExternalInput").ap()
    d_awkvab = nc.dram_tensor("awkvab", [D, 258], F32, kind="ExternalInput").ap()
    d_awo = nc.dram_tensor("awo", [128, D], F32, kind="ExternalInput").ap()
    d_bkvab = nc.dram_tensor("bkvab", [128, 258], F32, kind="ExternalInput").ap()
    d_bq = nc.dram_tensor("bqcol", [128, 1], F32, kind="ExternalInput").ap()
    d_bg = nc.dram_tensor("bgcol", [128, 1], F32, kind="ExternalInput").ap()
    d_ba = nc.dram_tensor("bacol", [128, 1], F32, kind="ExternalInput").ap()
    d_bb = nc.dram_tensor("bbcol", [128, 1], F32, kind="ExternalInput").ap()
    d_bo = nc.dram_tensor("borep", [128, D], F32, kind="ExternalInput").ap()
    d_cut = nc.dram_tensor("c_ut", [128, 128], F32, kind="ExternalInput").ap()
    d_cid = nc.dram_tensor("c_id", [128, 128], F32, kind="ExternalInput").ap()
    d_cidb = nc.dram_tensor("c_idb", [128, 128], BF16, kind="ExternalInput").ap()
    d_caiu = nc.dram_tensor("c_aiu", [128, 128], F32, kind="ExternalInput").ap()
    d_cones = nc.dram_tensor("c_ones1", [1, 128], F32, kind="ExternalInput").ap()

    d_x1ctx = nc.dram_tensor("x1ctx", [max(OWN0, 1) * 128, D], F32, kind="Internal").ap()
    d_x1own = nc.dram_tensor("x1own", [NOWN * 128, D], F32, kind="Internal").ap()
    d_x2own = nc.dram_tensor("x2own", [NOWN * 128, D], F32, kind="Internal").ap()
    d_stg = [nc.dram_tensor(f"stash_g{h}", [DCH, ICH, 128, 128], BF16,
                            kind="Internal").ap() for h in range(2)]
    d_stu = [nc.dram_tensor(f"stash_u{h}", [DCH, ICH, 128, 128], BF16,
                            kind="Internal").ap() for h in range(2)]
    d_std = [nc.dram_tensor(f"stash_d{h}", [ICH, DCH, 128, 128], BF16,
                            kind="Internal").ap() for h in range(2)]

    es = contextlib.ExitStack()
    with es:
        def sb(name, shape, dt=F32):
            return es.enter_context(nc.sbuf_tensor(name, shape, dt)).ap()

        c_ut = sb("c_ut_s", [128, 128])
        c_id = sb("c_id_s", [128, 128])
        c_aiu = sb("c_aiu_s", [128, 128])
        c_ones = sb("c_ones_s", [1, 128])
        w_awq = sb("w_awq", [128, DCH, 128])
        w_awg = sb("w_awg", [128, DCH, 128])
        w_awkvab = sb("w_awkvab", [128, DCH, 258])
        w_awo = sb("w_awo", [128, D])
        b_kvab = sb("b_kvab_s", [128, 258])
        b_q = sb("b_q_s", [128, 1])
        b_g = sb("b_g_s", [128, 1])
        b_a = sb("b_a_s", [128, 1])
        b_b = sb("b_b_s", [128, 1])
        b_o = sb("b_o_s", [128, D])
        c_eps = sb("c_eps", [128, 1])

        xin = [sb(f"xin{i}", [128, SCRW]) for i in range(2)]
        scr1 = sb("scr1", [128, SCRW])
        scr2 = sb("scr2", [128, SCRW])
        scr3 = sb("scr3", [128, SCRW])
        scr4 = sb("scr4", [128, SCRW])
        sm_m = sb("sm_m", [128, SCRW // 16])
        sm_s = sb("sm_s", [128, SCRW // 16])
        sm_r = sb("sm_r", [128, SCRW // 16])
        xqT = [[sb(f"xqT{t}_{h}", [128, DCH, 128], BF16) for h in range(2)]
               for t in range(SWEEP)]
        wcol = [[sb(f"wcol{i}_{h}", [128, 2, DCH, ITILE], BF16) for h in range(2)]
                for i in range(2)]
        c_idb = sb("c_idb_s", [128, 128], BF16)
        sp_h = sb("sp_h", [128, SCRW], BF16)
        sp_l = sb("sp_l", [128, SCRW], BF16)
        shared2k = sb("shared2k", [128, 2048])
        hb = [sb(f"hb{i}", [128, SWEEP * ITILE]) for i in range(2)]
        hb.append(shared2k[:, 0:SWEEP * ITILE])
        hsilu = [sb(f"hsilu{i}", [128, ITILE]) for i in range(2)]
        hq_m = sb("hq_m", [128, SWEEP * ITILE // 16])
        hq_s = sb("hq_s", [128, SWEEP * ITILE // 16])
        hq_r = sb("hq_r", [128, SWEEP * ITILE // 16])
        hqT = [[sb(f"hqT{t}_{h}", [128, ICH, 128], BF16) for h in range(2)]
               for t in range(SWEEP)]
        wdn = [[sb(f"wdn{i}_{h}", [128, 512], BF16) for h in range(2)]
               for i in range(2)]
        xres = [sb(f"xres{i}", [128, 512]) for i in range(2)]
        stg1 = [sb(f"stg1_{i}", [128, 512]) for i in range(3)]

        a_xh2T = shared2k.rearrange("p (d n) -> p d n", n=GROUP * 128)
        a_K = sb("a_K", [128, GROUP, 128])
        a_V = sb("a_V", [128, GROUP, 128])
        a_QT = sb("a_QT", [128, GROUP, 128])
        a_GT = sb("a_GT", [128, GROUP, 128])
        a_acol = sb("a_acol", [128, GROUP])
        a_bcol = sb("a_bcol", [128, GROUP])
        a_KT = sb("a_KT", [128, 128])
        a_VT = sb("a_VT", [128, 128])
        a_la = sb("a_la", [128, 1])
        a_cum = sb("a_cum", [128, 1])
        a_cumrow = sb("a_cumrow", [1, 128])
        a_cumrep = sb("a_cumrep", [128, 128])
        a_gam = sb("a_gam", [128, 1])
        a_gamL = sb("a_gamL", [128, 1])
        a_wgt = sb("a_wgt", [128, 1])
        a_negb = sb("a_negb", [128, 1])
        a_gamrep = sb("a_gamrep", [128, 128])
        a_gtincl = sb("a_gtincl", [128, 128])
        a_kk = sb("a_kk", [128, 128])
        s_na = [sb(f"s_na{i}", [128, 128]) for i in range(2)]
        s_nb = [sb(f"s_nb{i}", [128, 128]) for i in range(2)]
        a_gk = sb("a_gk", [128, 128])
        a_kw = sb("a_kw", [128, 128])
        a_Z = sb("a_Z", [128, 128])
        a_Wm = sb("a_Wm", [128, 128])
        a_PF = sb("a_PF", [128, 128])
        a_WT = sb("a_WT", [128, 128])
        a_ET = sb("a_ET", [128, 128])
        a_F = sb("a_F", [128, 128])
        a_FT = sb("a_FT", [128, 128])
        a_M = [sb(f"a_M{i}", [128, 128]) for i in range(2)]
        a_MT = [sb(f"a_MT{i}", [128, 128]) for i in range(2)]
        a_MQ = sb("a_MQ", [128, 128])
        a_BT = sb("a_BT", [128, 128])
        a_OT = sb("a_OT", [128, 128])
        a_t1 = sb("a_t1", [128, 128])
        a_t2 = sb("a_t2", [128, 128])

        ps = [es.enter_context(nc.psum_tensor(f"ps{i}", [128, 512], F32)).ap()
              for i in range(8)]

        sems, cnt = {}, {}

        def sem(name):
            if name not in sems:
                sems[name] = es.enter_context(nc.semaphore(name))
                cnt[name] = 0
            return sems[name]

        def inc(instr, name, amt=1):
            s = sem(name)
            cnt[name] += amt
            instr.then_inc(s, amt)
            return cnt[name]

        def wait(eng, name, val=None):
            if name not in cnt and val is None:
                return
            s = sem(name)
            v = cnt[name] if val is None else val
            if v > 0:
                eng.wait_ge(s, v)

        V, A, T, G, SY = nc.vector, nc.scalar, nc.tensor, nc.gpsimd, nc.sync

        lastuse = {}

        def guard(eng, key):
            if key in lastuse:
                sname, val = lastuse[key]
                wait(eng, sname, val)

        def used(key, sname):
            sem(sname)
            lastuse[key] = (sname, cnt[sname])

        def load(out, in_, s=None, key=None):
            if key is not None:
                guard(SY, key)
            if s is None:
                s = "ld_" + (key if key is not None else "c")
            return inc(SY.dma_start(out=out, in_=in_), s, 16)

        def store(out, in_, slot):
            return inc(A.dma_start(out=out, in_=in_), f"st{slot}", 16)

        def wait_all_stores(eng):
            for i in range(4):
                wait(eng, f"st{i}")

        with nc.Block():
            load(c_ut, d_cut); load(c_id, d_cid); load(c_aiu, d_caiu)
            load(c_idb, d_cidb)
            load(c_ones, d_cones)
            load(w_awq, d_awq.rearrange("(c p) n -> p c n", p=128))
            load(w_awg, d_awg.rearrange("(c p) n -> p c n", p=128))
            load(w_awkvab, d_awkvab.rearrange("(c p) n -> p c n", p=128))
            load(w_awo, d_awo)
            load(b_kvab, d_bkvab); load(b_q, d_bq); load(b_g, d_bg)
            load(b_a, d_ba); load(b_b, d_bb); load(b_o, d_bo)
            CONST_LD = cnt["ld_c"]
            for eng in (V, A, T):
                wait(eng, "ld_c", CONST_LD)
            V.memset(c_eps, 1e-5)
            V.drain()

            SH_MUL = float(np.float32(1.0) / np.float32(6.0)) * 0.5
            SH_MIN = float(np.float32(1e-12)) * 0.5

            def quant(src, out, F, mb, ssb, rb, t1, t2, t3, srckey=None):
                nb = F // 16
                v3 = src.rearrange("p (b k) -> p b k", k=16) if src.shape[1] == F \
                    else src
                guard(V, "scr1sign")
                guard(V, "scr2g")
                V.tensor_reduce(out=mb[:, :nb], in_=v3, axis=AX.X, op=AL.max,
                                apply_absolute_value=True)
                V.tensor_tensor(out=t1[:, :F], in0=src, in1=src, op=AL.mult)
                V.drain()
                V.tensor_scalar(out=ssb[:, :nb], in0=mb[:, :nb], scalar1=SH_MUL,
                                scalar2=SH_MIN, op0=AL.mult, op1=AL.max)
                V.drain()
                V.reciprocal(out=rb[:, :nb], in_=ssb[:, :nb])
                V.drain()
                V.tensor_tensor(out=mb[:, :nb], in0=rb[:, :nb], in1=rb[:, :nb],
                                op=AL.mult)
                V.drain()
                mb3 = mb[:, :nb].broadcast_to([128, nb, 16])
                t1v = t1[:, :F].rearrange("p (b k) -> p b k", k=16)
                V.tensor_tensor(out=t1v, in0=t1v, in1=mb3, op=AL.mult)
                V.drain()
                c1 = t3.bitcast(BF16)[:, 0:F]
                c2 = t3.bitcast(BF16)[:, F:2 * F]
                c3 = t2.bitcast(BF16)[:, 0:F]
                c4 = t2.bitcast(BF16)[:, F:2 * F]

                def cmp(dst, thr2, w):
                    V.tensor_scalar(out=dst, in0=t1[:, :F], scalar1=thr2,
                                    scalar2=w, op0=AL.is_gt, op1=AL.mult)

                cmp(c1, 0.25, 1.0)
                cmp(c2, 2.25, 1.0)
                cmp(c3, 6.25, 1.0)
                cmp(c4, 12.25, 1.0)
                V.drain()
                V.tensor_add(out=c1, in0=c1, in1=c2)
                V.tensor_add(out=c3, in0=c3, in1=c4)
                V.drain()
                cmp(c2, 25.0, 2.0)
                cmp(c4, 49.0, 2.0)
                V.drain()
                V.tensor_add(out=c1, in0=c1, in1=c3)
                V.tensor_add(out=c2, in0=c2, in1=c4)
                V.drain()
                cmp(c3, 100.0, 4.0)
                V.tensor_add(out=c1, in0=c1, in1=c2)
                V.drain()
                V.tensor_add(out=c1, in0=c1, in1=c3)
                V.drain()
                sh3 = ssb[:, :nb].broadcast_to([128, nb, 16])
                t2v = t2[:, :F].rearrange("p (b k) -> p b k", k=16)
                V.tensor_tensor(out=t2v, in0=c1.rearrange("p (b k) -> p b k", k=16),
                                in1=sh3, op=AL.mult)
                V.drain()
                pv = inc(V.engine_nop(), "v")
                sgb = t1.bitcast(BF16)[:, 0:F]
                wait(A, "v", pv)
                A.copy(out=sp_h[:, :F], in_=t2[:, :F])
                ai = A.activation(out=sgb, in_=src, func=AF.Sign)
                pa = inc(ai, "a")
                if srckey is not None:
                    used(srckey, "a")
                wait(G, "a", pa)
                G.tensor_tensor(out=sp_l[:, :F], in0=t2[:, :F], in1=sp_h[:, :F],
                                op=AL.subtract)
                G.drain()
                G.tensor_mul(out=sp_h[:, :F], in0=sp_h[:, :F], in1=sgb)
                G.tensor_mul(out=sp_l[:, :F], in0=sp_l[:, :F], in1=sgb)
                G.drain()
                inc(G.engine_nop(), "g")
                used("scr1sign", "g")
                used("scr2g", "g")
                return pv

            tp_state = {"n": 0}

            def transpose128(src_ap, dst_ap, ident=None):
                bk = 6 + (tp_state["n"] % 2)
                tp_state["n"] += 1
                key = f"bank{bk}"
                if ident is None:
                    pst = ps[bk][:, 0:128]
                else:
                    pst = ps[bk].bitcast(BF16)[:, 0:128]
                guard(T, key)
                mm = T.transpose(pst, src_ap, c_id if ident is None else ident)
                pt = inc(mm, "tpm")
                wait(A, "tpm", pt)
                inc(A.copy(out=dst_ap, in_=pst), "tpe")
                used(key, "tpe")
                return mm

            def transpose_hl(colslice, dsth, dstl):
                transpose128(sp_h[:, colslice], dsth, ident=c_idb)
                return transpose128(sp_l[:, colslice], dstl, ident=c_idb)

            def transpose_batch(pairs):
                assert len(pairs) <= 8
                bk = 6 + (tp_state["n"] % 2)
                tp_state["n"] += 1
                key = f"bank{bk}"
                pst = ps[bk].bitcast(BF16)
                guard(T, key)
                for i, (src, _) in enumerate(pairs):
                    mm = T.transpose(pst[:, i * 128:(i + 1) * 128], src, c_idb)
                pt = inc(mm, "tpm")
                wait(A, "tpm", pt)
                for i, (_, dst) in enumerate(pairs):
                    ai = A.copy(out=dst, in_=pst[:, i * 128:(i + 1) * 128])
                inc(ai, "tpe")
                used(key, "tpe")

            def transpose_batch_wide(srcs, dst):
                assert len(srcs) <= 8
                bk = 6 + (tp_state["n"] % 2)
                tp_state["n"] += 1
                key = f"bank{bk}"
                pst = ps[bk].bitcast(BF16)
                guard(T, key)
                for i, src in enumerate(srcs):
                    mm = T.transpose(pst[:, i * 128:(i + 1) * 128], src, c_idb)
                pt = inc(mm, "tpm")
                wait(A, "tpm", pt)
                inc(A.copy(out=dst, in_=pst[:, 0:len(srcs) * 128]), "tpe")
                used(key, "tpe")

            def ln_tile(src, out, srckey=None):
                V.tensor_reduce(out=sm_m[:, 0:1], in_=src[:, 0:D], axis=AX.X, op=AL.add)
                V.drain()
                V.tensor_scalar_mul(out=sm_m[:, 0:1], in0=sm_m[:, 0:1], scalar1=1.0 / D)
                V.drain()
                guard(V, "scr1sign")
                V.tensor_scalar(out=scr1[:, 0:D], in0=src[:, 0:D], scalar1=sm_m[:, 0:1],
                                scalar2=None, op0=AL.subtract)
                V.drain()
                pv = inc(V.engine_nop(), "v")
                if srckey is not None:
                    used(srckey, "v")
                wait(A, "v", pv)
                guard(A, "scr2g")
                A.activation(out=scr2[:, 0:D], in_=scr1[:, 0:D], func=AF.Square,
                             accum_out=sm_s[:, 0:1])
                A.drain()
                ai = A.activation(out=sm_r[:, 0:1], in_=sm_s[:, 0:1], func=AF.Sqrt,
                                  scale=1.0 / D, bias=c_eps)
                pa = inc(ai, "a")
                wait(V, "a", pa)
                V.reciprocal(out=sm_r[:, 1:2], in_=sm_r[:, 0:1])
                V.drain()
                V.tensor_scalar_mul(out=out[:, 0:D], in0=scr1[:, 0:D],
                                    scalar1=sm_r[:, 1:2])
                V.drain()
                return inc(V.engine_nop(), "v")

            stgb = [sb(f"stgb{i}", [128, 1024], BF16) for i in range(2)]

            def prep_rowtile(dram_ap, Fq, wide_dst, rt2):
                b = xin[rt2 % 2]
                lvs = f"ld_xin{rt2 % 2}"
                lv = load(b[:, :Fq], dram_ap, key=f"xin{rt2 % 2}")
                wait(V, lvs, lv)
                wait(V, "tpm")
                quant(b[:, :Fq], scr4[:, :Fq], Fq, sm_m, sm_s, sm_r,
                      scr1, scr2, scr3, srckey=f"xin{rt2 % 2}")
                wait(T, "v")
                wait(T, "g")
                nbk = Fq // 128
                for h in range(2):
                    sgb = stgb[h]
                    guard(A, f"stg{h}")
                    sp = sp_h if h == 0 else sp_l
                    transpose_batch_wide(
                        [sp[:, bi * 128:(bi + 1) * 128] for bi in range(nbk)],
                        sgb[:, 0:nbk * 128])
                    wait(A, "tpe")
                    store(wide_dst(h, nbk),
                          sgb[:, 0:nbk * 128].rearrange("p (b n) -> p b n", n=128), h)
                    used(f"stg{h}", f"st{h}")

            rt2 = 0
            for rt in (range(ICH) if "prep" in phases else ()):
                prep_rowtile(d_wg[rt * 128:(rt + 1) * 128, :], D,
                             lambda h, nbk, rt=rt:
                             d_stg[h][0:nbk, rt].rearrange("b p n -> p b n"), rt2)
                rt2 += 1
            for rt in (range(ICH) if "prep" in phases else ()):
                prep_rowtile(d_wu[rt * 128:(rt + 1) * 128, :], D,
                             lambda h, nbk, rt=rt:
                             d_stu[h][0:nbk, rt].rearrange("b p n -> p b n"), rt2)
                rt2 += 1
            for rt in (range(DCH) if "prep" in phases else ()):
                c0 = 0
                while c0 < I:
                    cw = min(D, I - c0)
                    prep_rowtile(d_wd[rt * 128:(rt + 1) * 128, c0:c0 + cw], cw,
                                 lambda h, nbk, rt=rt, c0=c0:
                                 d_std[h][c0 // 128:c0 // 128 + nbk, rt]
                                 .rearrange("b p n -> p b n"), rt2)
                    rt2 += 1
                    c0 += cw
            wait_all_stores(SY)

            def emit_ffn(tiles, src_x, src_res, dst_of):
                nsw = (len(tiles) + SWEEP - 1) // SWEEP

                def emit_pre(stiles):
                    wait(A, "t")
                    for si, t in enumerate(stiles):
                        b = xin[si % 2]
                        lvs = f"ld_xin{si % 2}"
                        lv = load(b[:, 0:D], src_x(t), key=f"xin{si % 2}")
                        wait(V, lvs, lv)
                        ln_tile(b, scr4, srckey=f"xin{si % 2}")
                        wait(V, "tpm")
                        quant(scr4[:, 0:D], scr4[:, 0:D], D, sm_m, sm_s, sm_r,
                              scr1, scr2, scr3)
                        wait(T, "v")
                        wait(T, "g")
                        transpose_batch_wide(
                            [sp_h[:, dc * 128:(dc + 1) * 128] for dc in range(DCH)],
                            xqT[si][0].rearrange("p a n -> p (a n)"))
                        transpose_batch_wide(
                            [sp_l[:, dc * 128:(dc + 1) * 128] for dc in range(DCH)],
                            xqT[si][1].rearrange("p a n -> p (a n)"))
                        wait(V, "tpe")
                    wait(T, "tpe")

                def emit_gateup(stiles):
                    ns = len(stiles)
                    pending_it = []

                    def flush_itile():
                        it0, hs0, ns0 = pending_it.pop(0)
                        F0 = ns0 * ITILE
                        wait(V, "g")
                        wait(A, "g")
                        wait(V, "tpm")
                        quant(hb[hs0][:, :F0], hb[hs0][:, :F0], F0,
                              hq_m, hq_s, hq_r, scr1, scr2, scr3)
                        wait(T, "v")
                        wait(T, "g")
                        pairs_h, pairs_l = [], []
                        for si0 in range(ns0):
                            for bi in range(IBLK):
                                col = si0 * ITILE + bi * 128
                                pairs_h.append((sp_h[:, col:col + 128],
                                                hqT[si0][0][:, it0 * IBLK + bi, :]))
                                pairs_l.append((sp_l[:, col:col + 128],
                                                hqT[si0][1][:, it0 * IBLK + bi, :]))
                        transpose_batch(pairs_h)
                        transpose_batch(pairs_l)
                        used(f"hb{hs0}", "tpm")

                    def load_wcol(slot, it0):
                        for mi, stash in enumerate((d_stg, d_stu)):
                            for h in range(2):
                                v5 = wcol[slot][h].rearrange(
                                    "p m d (b n) -> p m d b n", b=IBLK)
                                for bb in range(IBLK):
                                    load(v5[:, mi, :, bb, :],
                                         stash[h][:, it0 * IBLK + bb]
                                         .rearrange("d p n -> p d n"),
                                         s=f"wld{slot}", key=f"wcol{slot}")

                    for it in range(NIT):
                        if it == 0:
                            load_wcol(0, 0)
                        if it + 1 < NIT:
                            load_wcol((it + 1) % 2, it + 1)
                        wb = wcol[it % 2]
                        wait(T, f"wld{it % 2}")
                        hs = it % 3
                        for si in range(ns):
                            pp = ps[si % 2]
                            guard(T, f"bank{si % 2}")
                            passes = ((0, 0), (0, 1), (1, 0))
                            np_ = len(passes)
                            for mi2 in range(2):
                                reg = pp[:, mi2 * ITILE:(mi2 + 1) * ITILE]
                                k = 0
                                for xh_, wh_ in passes:
                                    for dc in range(DCH):
                                        mm = T.matmul(
                                            reg, xqT[si][xh_][:, dc, :],
                                            wb[wh_][:, mi2, dc, :],
                                            start=(k == 0),
                                            stop=(k == np_ * DCH - 1))
                                        k += 1
                            pt = inc(mm, "t")
                            if si == ns - 1:
                                used(f"wcol{it % 2}", "t")
                            wait(A, "t", pt)
                            hp = si % 2
                            guard(A, f"hsilu{hp}")
                            A.activation(out=hsilu[hp], in_=pp[:, 0:ITILE],
                                         func=AF.Silu)
                            hu = xres[hp]
                            guard(A, f"xres{hp}")
                            ai = A.copy(out=hu[:, 0:ITILE], in_=pp[:, ITILE:2 * ITILE])
                            pa = inc(ai, "a")
                            used(f"bank{si % 2}", "a")
                            wait(G, "a", pa)
                            if si == 0:
                                guard(G, f"hb{hs}")
                            G.tensor_mul(out=hb[hs][:, si * ITILE:(si + 1) * ITILE],
                                         in0=hsilu[hp], in1=hu[:, 0:ITILE])
                            G.drain()
                            pg = inc(G.engine_nop(), "g")
                            used(f"xres{hp}", "g")
                            used(f"hsilu{hp}", "g")
                        pending_it.append((it, hs, ns))
                        if len(pending_it) >= 3:
                            flush_itile()
                    while pending_it:
                        flush_itile()
                    wait(T, "tpe")

                def emit_down(stiles):
                    ns = len(stiles)
                    for hf0 in range(0, ns, 3):
                        htiles = stiles[hf0:hf0 + 3]
                        for dt in range(D // 512):
                            pts = []

                            def load_wdn(ic0):
                                sl = ic0 % 2
                                for h in range(2):
                                    load(wdn[sl][h].rearrange("p (b n) -> p b n", b=4),
                                         d_std[h][ic0, dt * 4:(dt + 1) * 4]
                                         .rearrange("b p n -> p b n"),
                                         s=f"dld{sl}", key=f"wdn{sl}")

                            load_wdn(0)
                            for ic in range(ICH):
                                wbD = wdn[ic % 2]
                                lv = cnt[f"dld{ic % 2}"]
                                if ic + 1 < ICH:
                                    load_wdn(ic + 1)
                                wait(T, f"dld{ic % 2}", lv)
                                mms = []
                                for si2 in range(len(htiles)):
                                    si = hf0 + si2
                                    pd = ps[2 + si2]
                                    if ic == 0:
                                        guard(T, f"bank{2 + si2}")
                                    for pi, (xh_, wh_) in enumerate(
                                            ((0, 0), (0, 1), (1, 0))):
                                        mms.append(T.matmul(
                                            pd[:, :512], hqT[si][xh_][:, ic, :],
                                            wbD[wh_], start=(ic == 0 and pi == 0),
                                            stop=(ic == ICH - 1 and pi == 2)))
                                inc(mms[-1], f"dfree{ic % 2}")
                                used(f"wdn{ic % 2}", f"dfree{ic % 2}")
                                if ic == ICH - 1:
                                    for si2 in range(len(htiles)):
                                        m_ = mms[si2 * 3 + 2]
                                        if si2 == len(htiles) - 1:
                                            pts.append((f"dfree{ic % 2}",
                                                        cnt[f"dfree{ic % 2}"]))
                                        else:
                                            pts.append(("t", inc(m_, "t")))
                            for si2, t in enumerate(htiles):
                                pd = ps[2 + si2]
                                xr = xres[si2 % 2]
                                lvs = f"ld_xres{si2 % 2}"
                                lv = load(xr, src_res(t)[:, dt * 512:(dt + 1) * 512],
                                          key=f"xres{si2 % 2}")
                                wait(A, pts[si2][0], pts[si2][1])
                                sl = si2
                                sgb = stg1[sl]
                                guard(A, f"stg{sl}")
                                ai = A.activation(out=sgb, in_=pd[:, :512],
                                                  func=AF.Copy, scale=0.5)
                                pa = inc(ai, "a")
                                used(f"bank{2 + si2}", "a")
                                wait(G, "a", pa)
                                wait(G, lvs, lv)
                                G.tensor_add(out=sgb, in0=sgb, in1=xr)
                                G.drain()
                                pg = inc(G.engine_nop(), "g")
                                used(f"xres{si2 % 2}", "g")
                                wait(A, "g", pg)
                                store(dst_of(t)[:, dt * 512:(dt + 1) * 512], sgb, sl)
                                used(f"stg{sl}", f"st{sl}")

                for sw in range(nsw):
                    stiles = stiles_of = tiles[sw * SWEEP:(sw + 1) * SWEEP]
                    if sw == 0:
                        emit_pre(stiles)
                    emit_gateup(stiles)
                    if sw + 1 < nsw:
                        emit_pre(tiles[(sw + 1) * SWEEP:(sw + 2) * SWEEP])
                    emit_down(stiles)
                wait_all_stores(SY)

            if "ffn1" in phases: emit_ffn(
                tiles=list(range(NT)),
                src_x=lambda t: d_x[t * 128:(t + 1) * 128, :],
                src_res=lambda t: d_x[t * 128:(t + 1) * 128, :],
                dst_of=lambda t: (d_x1ctx[t * 128:(t + 1) * 128, :] if t < OWN0 else
                                  d_x1own[(t - OWN0) * 128:(t - OWN0 + 1) * 128, :]),
            )

            def x1_src(t):
                return (d_x1ctx[t * 128:(t + 1) * 128, :] if t < OWN0 else
                        d_x1own[(t - OWN0) * 128:(t - OWN0 + 1) * 128, :])

            V.memset(a_M[0], 0.0)
            V.memset(a_MT[0], 0.0)
            V.drain()
            inc(V.engine_nop(), "v")
            mcur = {"i": 0}

            cs_state = {"n": 0}

            def cslot():
                bk = 2 + (cs_state["n"] % 2)
                cs_state["n"] += 1
                key = f"bank{bk}"
                pst = ps[bk][:, 0:128]
                guard(T, key)
                return pst, key

            def cmm(lhsT, rhs, dst, extra=None, n=128):
                pst, key = cslot()
                pm = inc(T.matmul(pst[:, 0:n], lhsT, rhs, start=True, stop=True), "csm")
                wait(V, "csm", pm)
                if extra is None:
                    V.tensor_copy(out=dst, in_=pst[:, 0:n])
                else:
                    extra(dst, pst[:, 0:n])
                V.drain()
                inc(V.engine_nop(), "v")
                used(key, "v")

            def emit_chunk(si, own, t_own):
                nx = 1 - mcur["i"]
                Mc, MTc = a_M[mcur["i"]], a_MT[mcur["i"]]
                Mn, MTn = a_M[nx], a_MT[nx]
                acol = a_acol[:, si:si + 1]
                bcol = a_bcol[:, si:si + 1]
                Kt = a_K[:, si, :]
                Vt = a_V[:, si, :]
                QTt = a_QT[:, si, :]
                GTt = a_GT[:, si, :]

                wait(A, "v", cnt["v"])
                ai = ai = ai = ai = ai = ai = ai = A.activation(out=a_la, in_=acol, func=AF.Ln)
                pa = inc(ai, "a")
                wait(T, "a", pa)
                cmm(c_ut, a_la, a_cum, n=1)
                wait(T, "v")
                pst, key = cslot()
                ptr = inc(T.transpose(pst[0:1, 0:128], a_cum, c_id), "csm")
                wait(V, "csm", ptr)
                V.tensor_copy(out=a_cumrow, in_=pst[0:1, 0:128])
                V.drain()
                pv = inc(V.engine_nop(), "v")
                used(key, "v")
                wait(T, "v", pv)
                cmm(c_ones, a_cumrow, a_cumrep)
                pv = cnt["v"]
                wait(A, "v", pv)
                A.activation(out=a_gam, in_=a_cum, func=AF.Exp)
                ai = A.activation(out=a_gamL, in_=a_cumrep[:, 127:128], func=AF.Exp)
                pa = inc(ai, "a")
                wait(V, "a", pa)
                V.tensor_sub(out=a_wgt, in0=a_cumrep[:, 127:128], in1=a_cum)
                V.drain()
                pv = inc(V.engine_nop(), "v")
                wait(A, "v", pv)
                ai = A.activation(out=a_wgt, in_=a_wgt, func=AF.Exp)
                pa = inc(ai, "a")
                wait(V, "a", pa)
                V.tensor_mul(out=a_wgt, in0=a_wgt, in1=bcol)
                V.tensor_scalar_mul(out=a_negb, in0=bcol, scalar1=-1.0)
                V.drain()
                V.tensor_scalar(out=a_t1, in0=a_cumrep, scalar1=a_cum, scalar2=None,
                                op0=AL.subtract)
                V.drain()
                V.tensor_add(out=a_t1, in0=a_t1, in1=c_aiu)
                V.drain()
                pv = inc(V.engine_nop(), "v")
                wait(A, "v", pv)
                ai = A.activation(out=a_gtincl, in_=a_t1, func=AF.Exp)
                if own:
                    ai = A.activation(out=a_gamrep, in_=a_cumrep, func=AF.Exp)
                pa = inc(ai, "a")
                wait(V, "a", pa)
                V.tensor_sub(out=a_t1, in0=a_gtincl, in1=c_id)
                V.drain()
                pv = inc(V.engine_nop(), "v")

                wait(T, "v", pv)
                transpose128(Kt, a_KT)
                if own:
                    transpose128(Vt, a_VT)
                wait(T, "tpe")
                cmm(a_KT, a_KT, a_kk)
                V.tensor_mul(out=s_nb[0], in0=a_t1, in1=a_kk)
                V.drain()
                V.tensor_scalar_mul(out=s_nb[0], in0=s_nb[0], scalar1=a_negb)
                V.tensor_scalar_mul(out=a_gk, in0=Kt, scalar1=a_gam)
                V.tensor_scalar_mul(out=a_kw, in0=Kt, scalar1=a_wgt)
                V.drain()
                pv = inc(V.engine_nop(), "v")
                wait(T, "v", pv)
                transpose128(s_nb[0], s_na[0])
                wait(V, "tpe")

                V.tensor_copy(out=a_Z, in_=a_kw)
                if own:
                    V.tensor_copy(out=a_Wm, in_=a_gk)
                V.drain()
                pv = inc(V.engine_nop(), "v")
                if own:
                    wait(T, "v", pv)
                    wait(T, "tpe")
                    cmm(a_VT, QTt, a_PF)
                cur = 0
                for m in range(7):
                    wait(T, "v")
                    cmm(s_na[cur], a_Z, a_Z,
                        extra=lambda d, p: V.tensor_add(out=d, in0=d, in1=p))
                    if own:
                        cmm(s_nb[cur], a_Wm, a_Wm,
                            extra=lambda d, p: V.tensor_add(out=d, in0=d, in1=p))
                        cmm(s_nb[cur], a_PF, a_PF,
                            extra=lambda d, p: V.tensor_add(out=d, in0=d, in1=p))
                    if m < 6:
                        wait(T, "v")
                        cmm(s_nb[cur], s_na[cur], s_na[1 - cur])
                        cmm(s_na[cur], s_nb[cur], s_nb[1 - cur])
                        cur = 1 - cur

                wait(T, "v")
                cmm(a_gk, a_Z, a_ET)
                cmm(a_Z, Vt, a_F)
                cmm(Vt, a_Z, a_FT)
                if own:
                    transpose128(a_Wm, a_WT)

                if own:
                    wait(T, "v")
                    cmm(MTc, QTt, a_MQ)
                    wait(T, "v")
                    wait(T, "tpe")
                    cmm(a_WT, a_MQ, a_BT,
                        extra=lambda d, p: V.scalar_tensor_tensor(
                            out=d, in0=p, scalar=-1.0, in1=a_PF,
                            op0=AL.mult, op1=AL.add))
                    V.tensor_mul(out=a_BT, in0=a_BT, in1=a_gtincl)
                    V.drain()
                    V.tensor_scalar_mul(out=a_BT, in0=a_BT, scalar1=bcol)
                    V.drain()
                    pv = inc(V.engine_nop(), "v")
                    wait(T, "v", pv)
                    cmm(Kt, a_BT, a_t2)
                    V.tensor_mul(out=a_t1, in0=a_MQ, in1=a_gamrep)
                    V.drain()
                    V.tensor_add(out=a_t1, in0=a_t1, in1=a_t2)
                    V.drain()
                    V.tensor_mul(out=a_OT, in0=a_t1, in1=GTt)
                    V.drain()
                    pv = inc(V.engine_nop(), "v")
                    wait(T, "v", pv)
                    for half in range(D // 512):
                        guard(T, "bank5")
                        mmO = T.matmul(ps[5][:, 0:512], a_OT,
                                       w_awo[:, half * 512:(half + 1) * 512],
                                       start=True, stop=True)
                        pt = inc(mmO, "t")
                        xr = xres[half % 2]
                        lvs = f"ld_xres{half % 2}"
                        lv = load(xr, d_x1own[t_own * 128:(t_own + 1) * 128,
                                              half * 512:(half + 1) * 512],
                                  key=f"xres{half % 2}")
                        wait(V, lvs, lv)
                        wait(V, "t", pt)
                        sgb = stg1[half]
                        guard(V, f"stg{half}")
                        V.tensor_add(out=sgb, in0=ps[5][:, 0:512], in1=xr)
                        V.drain()
                        V.tensor_add(out=sgb, in0=sgb,
                                     in1=b_o[:, half * 512:(half + 1) * 512])
                        V.drain()
                        pv = inc(V.engine_nop(), "v")
                        used("bank5", "v")
                        used(f"xres{half % 2}", "v")
                        wait(A, "v", pv)
                        store(d_x2own[t_own * 128:(t_own + 1) * 128,
                                      half * 512:(half + 1) * 512], sgb, half)
                        used(f"stg{half}", f"st{half}")

                wait(T, "v")
                cmm(a_ET, Mc, a_t1,
                    extra=lambda d, p: V.scalar_tensor_tensor(
                        out=d, in0=p, scalar=-1.0, in1=a_F, op0=AL.mult, op1=AL.add))
                cmm(Mc, a_ET, a_t2,
                    extra=lambda d, p: V.scalar_tensor_tensor(
                        out=d, in0=p, scalar=-1.0, in1=a_FT, op0=AL.mult, op1=AL.add))
                V.scalar_tensor_tensor(out=Mn, in0=Mc, scalar=a_gamL, in1=a_t1,
                                       op0=AL.mult, op1=AL.add)
                V.scalar_tensor_tensor(out=MTn, in0=MTc, scalar=a_gamL, in1=a_t2,
                                       op0=AL.mult, op1=AL.add)
                V.drain()
                inc(V.engine_nop(), "v")
                mcur["i"] = nx

            ngroups = NT // GROUP if "attn" in phases else 0
            for g in range(ngroups):
                gtiles = list(range(g * GROUP, (g + 1) * GROUP))
                own_g = gtiles[0] >= OWN0
                wait(A, "t")
                wait(A, "tpm")
                wait(V, "t")
                for si, t in enumerate(gtiles):
                    b = xin[si % 2]
                    lvs = f"ld_xin{si % 2}"
                    lv = load(b[:, 0:D], x1_src(t), key=f"xin{si % 2}")
                    wait(V, lvs, lv)
                    ln_tile(b, scr4, srckey=f"xin{si % 2}")
                    wait(T, "v")
                    for dc in range(DCH):
                        transpose128(scr4[:, dc * 128:(dc + 1) * 128],
                                     a_xh2T[:, dc, si * 128:(si + 1) * 128])
                    wait(V, "tpe")
                wait(T, "tpe")

                for si, t in enumerate(gtiles):
                    pp = ps[si % 2]
                    guard(T, f"bank{si % 2}")
                    for dc in range(DCH):
                        mm = T.matmul(pp[:, 0:258],
                                      a_xh2T[:, dc, si * 128:(si + 1) * 128],
                                      w_awkvab[:, dc, :], start=(dc == 0),
                                      stop=(dc == DCH - 1))
                    pt = inc(mm, "t")
                    wait(V, "t", pt)
                    V.tensor_add(out=scr1[:, 0:256], in0=pp[:, 0:256],
                                 in1=b_kvab[:, 0:256])
                    V.drain()
                    pv = inc(V.engine_nop(), "v")
                    wait(A, "v", pv)
                    A.activation(out=scr2[:, 0:128], in_=scr1[:, 0:128], func=AF.Square,
                                 accum_out=sm_s[:, 0:1])
                    A.drain()
                    A.activation(out=sm_s[:, 1:2], in_=sm_s[:, 0:1], func=AF.Sqrt)
                    A.activation(out=a_acol[:, si:si + 1], in_=pp[:, 256:257],
                                 func=AF.Sigmoid, bias=b_a)
                    ai = A.activation(out=a_bcol[:, si:si + 1], in_=pp[:, 257:258],
                                      func=AF.Sigmoid, bias=b_b)
                    pa = inc(ai, "a")
                    used(f"bank{si % 2}", "a")
                    wait(V, "a", pa)
                    V.tensor_scalar_max(out=sm_s[:, 1:2], in0=sm_s[:, 1:2],
                                        scalar1=1e-12)
                    V.drain()
                    pv = inc(V.engine_nop(), "v")
                    wait(A, "v", pv)
                    V.reciprocal(out=sm_r[:, 0:1], in_=sm_s[:, 1:2])
                    V.drain()
                    V.tensor_scalar_mul(out=a_K[:, si, :], in0=scr1[:, 0:128],
                                        scalar1=sm_r[:, 0:1])
                    V.tensor_copy(out=a_V[:, si, :], in_=scr1[:, 128:256])
                    V.drain()
                    inc(V.engine_nop(), "v")
                if own_g:
                    for si, t in enumerate(gtiles):
                        pp = ps[si % 2]
                        guard(T, f"bank{si % 2}")
                        for dc in range(DCH):
                            T.matmul(pp[:, 0:128], w_awq[:, dc, :],
                                     a_xh2T[:, dc, si * 128:(si + 1) * 128],
                                     start=(dc == 0), stop=(dc == DCH - 1))
                        for dc in range(DCH):
                            mm = T.matmul(pp[:, 128:256], w_awg[:, dc, :],
                                          a_xh2T[:, dc, si * 128:(si + 1) * 128],
                                          start=(dc == 0), stop=(dc == DCH - 1))
                        pt = inc(mm, "t")
                        wait(V, "t", pt)
                        V.tensor_scalar(out=a_QT[:, si, :], in0=pp[:, 0:128],
                                        scalar1=b_q, scalar2=None, op0=AL.add)
                        V.drain()
                        pv = inc(V.engine_nop(), "v")
                        wait(A, "v", pv)
                        ai = A.activation(out=a_GT[:, si, :], in_=pp[:, 128:256],
                                          func=AF.Sigmoid, bias=b_g)
                        pa = inc(ai, "a")
                        used(f"bank{si % 2}", "a")
                        wait(V, "a", pa)

                for si, t in enumerate(gtiles):
                    emit_chunk(si, own_g, (t - OWN0) if own_g else 0)

            wait_all_stores(SY)
            if "ffn2" in phases: emit_ffn(
                tiles=list(range(NOWN)),
                src_x=lambda t: d_x2own[t * 128:(t + 1) * 128, :],
                src_res=lambda t: d_x2own[t * 128:(t + 1) * 128, :],
                dst_of=lambda t: d_out[t * 128:(t + 1) * 128, :],
            )

            if "ffn2" not in phases:
                src_dump = d_x2own if "attn" in phases else d_x1own
                for t in range(0, NOWN * 128, 128):
                    load(xin[0][:, 0:D], src_dump[t:t + 128, :],
                         s="dbg_ld", key="xin0")
                    wait(A, "dbg_ld")
                    store(d_out[t:t + 128, :], xin[0][:, 0:D], 0)
                    used("xin0", "st0")
                    wait(SY, "st0")
            wait_all_stores(SY)

    return nc


def make_host_inputs(cfg, ins):
    NT, OWN0, D, I = cfg["NTILES"], cfg["OWN0"], cfg["D"], cfg["I"]
    NOWN = NT - OWN0
    x = np.ascontiguousarray(np.asarray(ins["L0_attn_x"], dtype=np.float32))
    B = x.shape[0]
    Wk = np.asarray(ins["L0_attn_Wk"], np.float32)
    Wv = np.asarray(ins["L0_attn_Wv"], np.float32)
    Wa = np.asarray(ins["L0_attn_Wa"], np.float32)
    Wb = np.asarray(ins["L0_attn_Wb"], np.float32)
    wkvab = np.ascontiguousarray(np.concatenate([Wk, Wv, Wa, Wb], axis=1))
    bk = np.asarray(ins["L0_attn_bk"], np.float32)
    bv = np.asarray(ins["L0_attn_bv"], np.float32)
    bkvab = np.ascontiguousarray(
        np.broadcast_to(np.concatenate([bk, bv, [0.0, 0.0]]).astype(np.float32)[None, :],
                        (128, 258)))
    common = {
        "wg": np.ascontiguousarray(np.asarray(ins["L0_ffn_w_gate"], np.float32)),
        "wu": np.ascontiguousarray(np.asarray(ins["L0_ffn_w_up"], np.float32)),
        "wd": np.ascontiguousarray(np.asarray(ins["L0_ffn_w_down"], np.float32)),
        "awq": np.ascontiguousarray(np.asarray(ins["L0_attn_Wq"], np.float32)),
        "awg": np.ascontiguousarray(np.asarray(ins["L0_attn_Wg"], np.float32)),
        "awkvab": wkvab,
        "awo": np.ascontiguousarray(np.asarray(ins["L0_attn_Wo"], np.float32)),
        "bkvab": bkvab,
        "bqcol": np.ascontiguousarray(np.asarray(ins["L0_attn_bq"], np.float32)[:, None]),
        "bgcol": np.ascontiguousarray(np.asarray(ins["L0_attn_bg"], np.float32)[:, None]),
        "bacol": np.ascontiguousarray(
            np.broadcast_to(np.asarray(ins["L0_attn_ba"], np.float32), (128,))[:, None]),
        "bbcol": np.ascontiguousarray(
            np.broadcast_to(np.asarray(ins["L0_attn_bb"], np.float32), (128,))[:, None]),
        "borep": np.ascontiguousarray(
            np.broadcast_to(np.asarray(ins["L0_attn_bo"], np.float32)[None, :], (128, D))),
        "c_ut": np.triu(np.ones((128, 128), np.float32)),
        "c_id": np.eye(128, dtype=np.float32),
        "c_idb": np.eye(128, dtype=np.float32).astype(ml_dtypes.bfloat16),
        "c_aiu": ((np.triu(np.ones((128, 128), np.float32)) - 1.0) * 1e9).astype(np.float32),
        "c_ones1": np.ones((1, 128), np.float32),
    }
    half = OWN0 * 128
    in_maps = []
    for c in range(2 * B):
        bi, hi = c // 2, c % 2
        xa = np.zeros((NT * 128, D), np.float32)
        if hi == 0:
            xa[half:half + NOWN * 128] = x[bi, 0:NOWN * 128]
        else:
            xa[0:half] = x[bi, 0:half]
            xa[half:half + NOWN * 128] = x[bi, half:half + NOWN * 128]
        in_maps.append(dict(common, xall=np.ascontiguousarray(xa)))
    return in_maps


def assemble(cfg, results, B, T):
    NOWN = cfg["NTILES"] - cfg["OWN0"]
    D = cfg["D"]
    out = np.zeros((B, T, D), np.float32)
    half = cfg["OWN0"] * 128
    for c, r in enumerate(results):
        bi, hi = c // 2, c % 2
        if hi == 0:
            out[bi, 0:NOWN * 128] = r["yout"]
        else:
            out[bi, half:half + NOWN * 128] = r["yout"]
    return out


from concourse.bass_utils import run_bass_kernel_spmd

_NC_CACHE = {}


def kernel(**inputs):
    cfg = FULL
    if "nc" not in _NC_CACHE:
        _NC_CACHE["nc"] = build_nc(cfg)
    nc = _NC_CACHE["nc"]
    in_maps = make_host_inputs(cfg, inputs)
    res = run_bass_kernel_spmd(nc, in_maps, core_ids=list(range(8)))
    x = np.asarray(inputs["L0_attn_x"])
    return assemble(cfg, res.results, x.shape[0], x.shape[1])


# version 14
# speedup vs baseline: 1.2511x; 1.1068x over previous
import contextlib
import numpy as np
import ml_dtypes
import concourse.bass as bass
import concourse.mybir as mybir

F32 = mybir.dt.float32
BF16 = mybir.dt.bfloat16
I32 = mybir.dt.int32
AF = mybir.ActivationFunctionType
AL = mybir.AluOpType
AX = mybir.AxisListType

FULL = dict(NTILES=32, OWN0=16, D=1024, I=2816, ITILE=256, SWEEP=4, GROUP=2)
TINY = dict(NTILES=2, OWN0=1, D=1024, I=2816, ITILE=256, SWEEP=2, GROUP=1)
MID = dict(NTILES=8, OWN0=4, D=1024, I=2816, ITILE=256, SWEEP=4, GROUP=2)


def build_nc(cfg, phases=("prep", "ffn1", "attn", "ffn2")):
    NT, OWN0, D, I = cfg["NTILES"], cfg["OWN0"], cfg["D"], cfg["I"]
    ITILE, SWEEP, GROUP = cfg["ITILE"], cfg["SWEEP"], cfg["GROUP"]
    DCH, ICH = D // 128, I // 128
    NIT = I // ITILE
    NOWN = NT - OWN0
    IBLK = ITILE // 128
    SCRW = D

    nc = bass.Bass("TRN2", target_bir_lowering=False, debug=False)

    d_x = nc.dram_tensor("xall", [NT * 128, D], F32, kind="ExternalInput").ap()
    d_out = nc.dram_tensor("yout", [NOWN * 128, D], F32, kind="ExternalOutput").ap()
    d_wg = nc.dram_tensor("wg", [I, D], F32, kind="ExternalInput").ap()
    d_wu = nc.dram_tensor("wu", [I, D], F32, kind="ExternalInput").ap()
    d_wd = nc.dram_tensor("wd", [D, I], F32, kind="ExternalInput").ap()
    d_awq = nc.dram_tensor("awq", [D, 128], F32, kind="ExternalInput").ap()
    d_awg = nc.dram_tensor("awg", [D, 128], F32, kind="ExternalInput").ap()
    d_awkvab = nc.dram_tensor("awkvab", [D, 258], F32, kind="ExternalInput").ap()
    d_awo = nc.dram_tensor("awo", [128, D], F32, kind="ExternalInput").ap()
    d_bkvab = nc.dram_tensor("bkvab", [128, 258], F32, kind="ExternalInput").ap()
    d_bq = nc.dram_tensor("bqcol", [128, 1], F32, kind="ExternalInput").ap()
    d_bg = nc.dram_tensor("bgcol", [128, 1], F32, kind="ExternalInput").ap()
    d_ba = nc.dram_tensor("bacol", [128, 1], F32, kind="ExternalInput").ap()
    d_bb = nc.dram_tensor("bbcol", [128, 1], F32, kind="ExternalInput").ap()
    d_bo = nc.dram_tensor("borep", [128, D], F32, kind="ExternalInput").ap()
    d_cut = nc.dram_tensor("c_ut", [128, 128], F32, kind="ExternalInput").ap()
    d_cid = nc.dram_tensor("c_id", [128, 128], F32, kind="ExternalInput").ap()
    d_cidb = nc.dram_tensor("c_idb", [128, 128], BF16, kind="ExternalInput").ap()
    d_caiu = nc.dram_tensor("c_aiu", [128, 128], F32, kind="ExternalInput").ap()
    d_cones = nc.dram_tensor("c_ones1", [1, 128], F32, kind="ExternalInput").ap()

    d_x1ctx = nc.dram_tensor("x1ctx", [max(OWN0, 1) * 128, D], F32, kind="Internal").ap()
    d_x1own = nc.dram_tensor("x1own", [NOWN * 128, D], F32, kind="Internal").ap()
    d_x2own = nc.dram_tensor("x2own", [NOWN * 128, D], F32, kind="Internal").ap()
    d_stg = [nc.dram_tensor(f"stash_g{h}", [DCH, ICH, 128, 128], BF16,
                            kind="Internal").ap() for h in range(2)]
    d_stu = [nc.dram_tensor(f"stash_u{h}", [DCH, ICH, 128, 128], BF16,
                            kind="Internal").ap() for h in range(2)]
    d_std = [nc.dram_tensor(f"stash_d{h}", [ICH, DCH, 128, 128], BF16,
                            kind="Internal").ap() for h in range(2)]

    es = contextlib.ExitStack()
    with es:
        def sb(name, shape, dt=F32):
            return es.enter_context(nc.sbuf_tensor(name, shape, dt)).ap()

        c_ut = sb("c_ut_s", [128, 128])
        c_id = sb("c_id_s", [128, 128])
        c_aiu = sb("c_aiu_s", [128, 128])
        c_ones = sb("c_ones_s", [1, 128])
        w_awq = sb("w_awq", [128, DCH, 128])
        w_awg = sb("w_awg", [128, DCH, 128])
        w_awkvab = sb("w_awkvab", [128, DCH, 258])
        w_awo = sb("w_awo", [128, D])
        b_kvab = sb("b_kvab_s", [128, 258])
        b_q = sb("b_q_s", [128, 1])
        b_g = sb("b_g_s", [128, 1])
        b_a = sb("b_a_s", [128, 1])
        b_b = sb("b_b_s", [128, 1])
        b_o = sb("b_o_s", [128, D])
        c_eps = sb("c_eps", [128, 1])

        xin = [sb(f"xin{i}", [128, SCRW]) for i in range(2)]
        scr1 = sb("scr1", [128, SCRW])
        scr2 = sb("scr2", [128, SCRW])
        scr3 = sb("scr3", [128, SCRW])
        scr4 = sb("scr4", [128, SCRW])
        sm_m = sb("sm_m", [128, SCRW // 16])
        sm_s = sb("sm_s", [128, SCRW // 16])
        sm_r = sb("sm_r", [128, SCRW // 16])
        xqT = [[sb(f"xqT{t}_{h}", [128, DCH, 128], BF16) for h in range(2)]
               for t in range(SWEEP)]
        wcol = [[sb(f"wcol{i}_{h}", [128, 2, DCH, ITILE], BF16) for h in range(2)]
                for i in range(2)]
        c_idb = sb("c_idb_s", [128, 128], BF16)
        sp_h = sb("sp_h", [128, SCRW], BF16)
        sp_l = sb("sp_l", [128, SCRW], BF16)
        shared2k = sb("shared2k", [128, 2048])
        hb = [sb(f"hb{i}", [128, SWEEP * ITILE]) for i in range(2)]
        hb.append(shared2k[:, 0:SWEEP * ITILE])
        hsilu = [sb(f"hsilu{i}", [128, ITILE]) for i in range(2)]
        hq_m = sb("hq_m", [128, SWEEP * ITILE // 16])
        hq_s = sb("hq_s", [128, SWEEP * ITILE // 16])
        hq_r = sb("hq_r", [128, SWEEP * ITILE // 16])
        hqT = [[sb(f"hqT{t}_{h}", [128, ICH, 128], BF16) for h in range(2)]
               for t in range(SWEEP)]
        wdn = [[sb(f"wdn{i}_{h}", [128, 512], BF16) for h in range(2)]
               for i in range(3)]
        xres = [sb(f"xres{i}", [128, 512]) for i in range(2)]
        stg1 = [sb(f"stg1_{i}", [128, 512]) for i in range(4)]

        a_xh2T = shared2k.rearrange("p (d n) -> p d n", n=GROUP * 128)
        a_K = sb("a_K", [128, GROUP, 128])
        a_V = sb("a_V", [128, GROUP, 128])
        a_QT = sb("a_QT", [128, GROUP, 128])
        a_GT = sb("a_GT", [128, GROUP, 128])
        a_acol = sb("a_acol", [128, GROUP])
        a_bcol = sb("a_bcol", [128, GROUP])
        a_KT = sb("a_KT", [128, 128])
        a_VT = sb("a_VT", [128, 128])
        a_la = sb("a_la", [128, 1])
        a_cum = sb("a_cum", [128, 1])
        a_cumrow = sb("a_cumrow", [1, 128])
        a_cumrep = sb("a_cumrep", [128, 128])
        a_gam = sb("a_gam", [128, 1])
        a_gamL = sb("a_gamL", [128, 1])
        a_wgt = sb("a_wgt", [128, 1])
        a_negb = sb("a_negb", [128, 1])
        a_gamrep = sb("a_gamrep", [128, 128])
        a_gtincl = sb("a_gtincl", [128, 128])
        a_kk = sb("a_kk", [128, 128])
        s_na = [sb(f"s_na{i}", [128, 128]) for i in range(2)]
        s_nb = [sb(f"s_nb{i}", [128, 128]) for i in range(2)]
        a_gk = sb("a_gk", [128, 128])
        a_kw = sb("a_kw", [128, 128])
        a_Z = sb("a_Z", [128, 128])
        a_Wm = sb("a_Wm", [128, 128])
        a_PF = sb("a_PF", [128, 128])
        a_WT = sb("a_WT", [128, 128])
        a_ET = sb("a_ET", [128, 128])
        a_F = sb("a_F", [128, 128])
        a_FT = sb("a_FT", [128, 128])
        a_M = [sb(f"a_M{i}", [128, 128]) for i in range(2)]
        a_MT = [sb(f"a_MT{i}", [128, 128]) for i in range(2)]
        a_MQ = sb("a_MQ", [128, 128])
        a_BT = sb("a_BT", [128, 128])
        a_OT = sb("a_OT", [128, 128])
        a_t1 = sb("a_t1", [128, 128])
        a_t2 = sb("a_t2", [128, 128])

        ps = [es.enter_context(nc.psum_tensor(f"ps{i}", [128, 512], F32)).ap()
              for i in range(8)]

        sems, cnt = {}, {}

        def sem(name):
            if name not in sems:
                sems[name] = es.enter_context(nc.semaphore(name))
                cnt[name] = 0
            return sems[name]

        def inc(instr, name, amt=1):
            s = sem(name)
            cnt[name] += amt
            instr.then_inc(s, amt)
            return cnt[name]

        def wait(eng, name, val=None):
            if name not in cnt and val is None:
                return
            s = sem(name)
            v = cnt[name] if val is None else val
            if v > 0:
                eng.wait_ge(s, v)

        V, A, T, G, SY = nc.vector, nc.scalar, nc.tensor, nc.gpsimd, nc.sync

        lastuse = {}

        def guard(eng, key):
            if key in lastuse:
                sname, val = lastuse[key]
                wait(eng, sname, val)

        def used(key, sname):
            sem(sname)
            lastuse[key] = (sname, cnt[sname])

        def load(out, in_, s=None, key=None):
            if key is not None:
                guard(SY, key)
            if s is None:
                s = "ld_" + (key if key is not None else "c")
            return inc(SY.dma_start(out=out, in_=in_), s, 16)

        def store(out, in_, slot):
            return inc(A.dma_start(out=out, in_=in_), f"st{slot}", 16)

        def wait_all_stores(eng):
            for i in range(4):
                wait(eng, f"st{i}")

        with nc.Block():
            load(c_ut, d_cut); load(c_id, d_cid); load(c_aiu, d_caiu)
            load(c_idb, d_cidb)
            load(c_ones, d_cones)
            load(w_awq, d_awq.rearrange("(c p) n -> p c n", p=128))
            load(w_awg, d_awg.rearrange("(c p) n -> p c n", p=128))
            load(w_awkvab, d_awkvab.rearrange("(c p) n -> p c n", p=128))
            load(w_awo, d_awo)
            load(b_kvab, d_bkvab); load(b_q, d_bq); load(b_g, d_bg)
            load(b_a, d_ba); load(b_b, d_bb); load(b_o, d_bo)
            CONST_LD = cnt["ld_c"]
            for eng in (V, A, T):
                wait(eng, "ld_c", CONST_LD)
            V.memset(c_eps, 1e-5)
            V.drain()

            SH_MUL = float(np.float32(1.0) / np.float32(6.0)) * 0.5
            SH_MIN = float(np.float32(1e-12)) * 0.5

            def quant(src, out, F, mb, ssb, rb, t1, t2, t3, srckey=None):
                nb = F // 16
                v3 = src.rearrange("p (b k) -> p b k", k=16) if src.shape[1] == F \
                    else src
                guard(V, "scr1sign")
                guard(V, "scr2g")
                V.tensor_reduce(out=mb[:, :nb], in_=v3, axis=AX.X, op=AL.max,
                                apply_absolute_value=True)
                V.tensor_tensor(out=t1[:, :F], in0=src, in1=src, op=AL.mult)
                V.drain()
                V.tensor_scalar(out=ssb[:, :nb], in0=mb[:, :nb], scalar1=SH_MUL,
                                scalar2=SH_MIN, op0=AL.mult, op1=AL.max)
                V.drain()
                V.reciprocal(out=rb[:, :nb], in_=ssb[:, :nb])
                V.drain()
                V.tensor_tensor(out=mb[:, :nb], in0=rb[:, :nb], in1=rb[:, :nb],
                                op=AL.mult)
                V.drain()
                mb3 = mb[:, :nb].broadcast_to([128, nb, 16])
                t1v = t1[:, :F].rearrange("p (b k) -> p b k", k=16)
                V.tensor_tensor(out=t1v, in0=t1v, in1=mb3, op=AL.mult)
                V.drain()
                c1 = t3.bitcast(BF16)[:, 0:F]
                c2 = t3.bitcast(BF16)[:, F:2 * F]
                c3 = t2.bitcast(BF16)[:, 0:F]
                c4 = t2.bitcast(BF16)[:, F:2 * F]

                def cmp(dst, thr2, w):
                    V.tensor_scalar(out=dst, in0=t1[:, :F], scalar1=thr2,
                                    scalar2=w, op0=AL.is_gt, op1=AL.mult)

                cmp(c1, 0.25, 1.0)
                cmp(c2, 2.25, 1.0)
                cmp(c3, 6.25, 1.0)
                cmp(c4, 12.25, 1.0)
                V.drain()
                V.tensor_add(out=c1, in0=c1, in1=c2)
                V.tensor_add(out=c3, in0=c3, in1=c4)
                V.drain()
                cmp(c2, 25.0, 2.0)
                cmp(c4, 49.0, 2.0)
                V.drain()
                V.tensor_add(out=c1, in0=c1, in1=c3)
                V.tensor_add(out=c2, in0=c2, in1=c4)
                V.drain()
                cmp(c3, 100.0, 4.0)
                V.tensor_add(out=c1, in0=c1, in1=c2)
                V.drain()
                V.tensor_add(out=c1, in0=c1, in1=c3)
                V.drain()
                sh3 = ssb[:, :nb].broadcast_to([128, nb, 16])
                t2v = t2[:, :F].rearrange("p (b k) -> p b k", k=16)
                mm = V.tensor_tensor(out=t2v,
                                     in0=c1.rearrange("p (b k) -> p b k", k=16),
                                     in1=sh3, op=AL.mult)
                pv = inc(mm, "v")
                V.drain()
                sgb = t1.bitcast(BF16)[:, 0:F]
                wait(A, "v", pv)
                A.copy(out=sp_h[:, :F], in_=t2[:, :F])
                ai = A.activation(out=sgb, in_=src, func=AF.Sign)
                pa = inc(ai, "a")
                if srckey is not None:
                    used(srckey, "a")
                wait(G, "a", pa)
                G.tensor_tensor(out=sp_l[:, :F], in0=t2[:, :F], in1=sp_h[:, :F],
                                op=AL.subtract)
                G.drain()
                G.tensor_mul(out=sp_h[:, :F], in0=sp_h[:, :F], in1=sgb)
                gi = G.tensor_mul(out=sp_l[:, :F], in0=sp_l[:, :F], in1=sgb)
                inc(gi, "g")
                used("scr1sign", "g")
                used("scr2g", "g")
                return pv

            tp_state = {"n": 0}

            def transpose128(src_ap, dst_ap, ident=None):
                bk = 6 + (tp_state["n"] % 2)
                tp_state["n"] += 1
                key = f"bank{bk}"
                if ident is None:
                    pst = ps[bk][:, 0:128]
                else:
                    pst = ps[bk].bitcast(BF16)[:, 0:128]
                guard(T, key)
                mm = T.transpose(pst, src_ap, c_id if ident is None else ident)
                pt = inc(mm, "tpm")
                wait(A, "tpm", pt)
                inc(A.copy(out=dst_ap, in_=pst), "tpe")
                used(key, "tpe")
                return mm

            def transpose_hl(colslice, dsth, dstl):
                transpose128(sp_h[:, colslice], dsth, ident=c_idb)
                return transpose128(sp_l[:, colslice], dstl, ident=c_idb)

            def transpose_batch(pairs):
                assert len(pairs) <= 8
                bk = 6 + (tp_state["n"] % 2)
                tp_state["n"] += 1
                key = f"bank{bk}"
                pst = ps[bk].bitcast(BF16)
                guard(T, key)
                for i, (src, _) in enumerate(pairs):
                    mm = T.transpose(pst[:, i * 128:(i + 1) * 128], src, c_idb)
                pt = inc(mm, "tpm")
                wait(A, "tpm", pt)
                for i, (_, dst) in enumerate(pairs):
                    ai = A.copy(out=dst, in_=pst[:, i * 128:(i + 1) * 128])
                inc(ai, "tpe")
                used(key, "tpe")

            def transpose_batch_wide(srcs, dst):
                assert len(srcs) <= 8
                bk = 6 + (tp_state["n"] % 2)
                tp_state["n"] += 1
                key = f"bank{bk}"
                pst = ps[bk].bitcast(BF16)
                guard(T, key)
                for i, src in enumerate(srcs):
                    mm = T.transpose(pst[:, i * 128:(i + 1) * 128], src, c_idb)
                pt = inc(mm, "tpm")
                wait(A, "tpm", pt)
                inc(A.copy(out=dst, in_=pst[:, 0:len(srcs) * 128]), "tpe")
                used(key, "tpe")

            def ln_tile(src, out, srckey=None):
                V.tensor_reduce(out=sm_m[:, 0:1], in_=src[:, 0:D], axis=AX.X, op=AL.add)
                V.drain()
                V.tensor_scalar_mul(out=sm_m[:, 0:1], in0=sm_m[:, 0:1], scalar1=1.0 / D)
                V.drain()
                guard(V, "scr1sign")
                vi = V.tensor_scalar(out=scr1[:, 0:D], in0=src[:, 0:D],
                                     scalar1=sm_m[:, 0:1],
                                     scalar2=None, op0=AL.subtract)
                pv = inc(vi, "v")
                V.drain()
                if srckey is not None:
                    used(srckey, "v")
                wait(A, "v", pv)
                guard(A, "scr2g")
                A.activation(out=scr2[:, 0:D], in_=scr1[:, 0:D], func=AF.Square,
                             accum_out=sm_s[:, 0:1])
                A.drain()
                ai = A.activation(out=sm_r[:, 0:1], in_=sm_s[:, 0:1], func=AF.Sqrt,
                                  scale=1.0 / D, bias=c_eps)
                pa = inc(ai, "a")
                wait(V, "a", pa)
                V.reciprocal(out=sm_r[:, 1:2], in_=sm_r[:, 0:1])
                V.drain()
                vi = V.tensor_scalar_mul(out=out[:, 0:D], in0=scr1[:, 0:D],
                                         scalar1=sm_r[:, 1:2])
                pv = inc(vi, "v")
                V.drain()
                return pv

            stgb = [sb(f"stgb{i}", [128, 1024], BF16) for i in range(2)]

            def prep_rowtile(dram_ap, Fq, wide_dst, rt2):
                b = xin[rt2 % 2]
                lvs = f"ld_xin{rt2 % 2}"
                lv = load(b[:, :Fq], dram_ap, key=f"xin{rt2 % 2}")
                wait(V, lvs, lv)
                wait(V, "tpm")
                quant(b[:, :Fq], scr4[:, :Fq], Fq, sm_m, sm_s, sm_r,
                      scr1, scr2, scr3, srckey=f"xin{rt2 % 2}")
                wait(T, "g")
                nbk = Fq // 128
                for h in range(2):
                    sgb = stgb[h]
                    guard(A, f"stg{h}")
                    sp = sp_h if h == 0 else sp_l
                    transpose_batch_wide(
                        [sp[:, bi * 128:(bi + 1) * 128] for bi in range(nbk)],
                        sgb[:, 0:nbk * 128])
                    wait(A, "tpe")
                    store(wide_dst(h, nbk),
                          sgb[:, 0:nbk * 128].rearrange("p (b n) -> p b n", n=128), h)
                    used(f"stg{h}", f"st{h}")

            rt2 = 0
            for rt in (range(ICH) if "prep" in phases else ()):
                prep_rowtile(d_wg[rt * 128:(rt + 1) * 128, :], D,
                             lambda h, nbk, rt=rt:
                             d_stg[h][0:nbk, rt].rearrange("b p n -> p b n"), rt2)
                rt2 += 1
            for rt in (range(ICH) if "prep" in phases else ()):
                prep_rowtile(d_wu[rt * 128:(rt + 1) * 128, :], D,
                             lambda h, nbk, rt=rt:
                             d_stu[h][0:nbk, rt].rearrange("b p n -> p b n"), rt2)
                rt2 += 1
            for rt in (range(DCH) if "prep" in phases else ()):
                c0 = 0
                while c0 < I:
                    cw = min(D, I - c0)
                    prep_rowtile(d_wd[rt * 128:(rt + 1) * 128, c0:c0 + cw], cw,
                                 lambda h, nbk, rt=rt, c0=c0:
                                 d_std[h][c0 // 128:c0 // 128 + nbk, rt]
                                 .rearrange("b p n -> p b n"), rt2)
                    rt2 += 1
                    c0 += cw
            wait_all_stores(SY)

            def emit_ffn(tiles, src_x, src_res, dst_of):
                nsw = (len(tiles) + SWEEP - 1) // SWEEP

                def emit_pre(stiles):
                    wait(A, "t")
                    for si, t in enumerate(stiles):
                        b = xin[si % 2]
                        lvs = f"ld_xin{si % 2}"
                        lv = load(b[:, 0:D], src_x(t), key=f"xin{si % 2}")
                        wait(V, lvs, lv)
                        ln_tile(b, scr4, srckey=f"xin{si % 2}")
                        wait(V, "tpm")
                        quant(scr4[:, 0:D], scr4[:, 0:D], D, sm_m, sm_s, sm_r,
                              scr1, scr2, scr3)
                        wait(T, "g")
                        transpose_batch_wide(
                            [sp_h[:, dc * 128:(dc + 1) * 128] for dc in range(DCH)],
                            xqT[si][0].rearrange("p a n -> p (a n)"))
                        transpose_batch_wide(
                            [sp_l[:, dc * 128:(dc + 1) * 128] for dc in range(DCH)],
                            xqT[si][1].rearrange("p a n -> p (a n)"))
                        wait(V, "tpe")
                    wait(T, "tpe")

                def emit_gateup(stiles):
                    ns = len(stiles)
                    pending_it = []

                    def flush_itile():
                        it0, hs0, ns0 = pending_it.pop(0)
                        F0 = ns0 * ITILE
                        wait(V, "g")
                        wait(A, "g")
                        wait(V, "tpm")
                        quant(hb[hs0][:, :F0], hb[hs0][:, :F0], F0,
                              hq_m, hq_s, hq_r, scr1, scr2, scr3)
                        wait(T, "g")
                        pairs_h, pairs_l = [], []
                        for si0 in range(ns0):
                            for bi in range(IBLK):
                                col = si0 * ITILE + bi * 128
                                pairs_h.append((sp_h[:, col:col + 128],
                                                hqT[si0][0][:, it0 * IBLK + bi, :]))
                                pairs_l.append((sp_l[:, col:col + 128],
                                                hqT[si0][1][:, it0 * IBLK + bi, :]))
                        transpose_batch(pairs_h)
                        transpose_batch(pairs_l)
                        used(f"hb{hs0}", "tpm")

                    def load_wcol(slot, it0):
                        for mi, stash in enumerate((d_stg, d_stu)):
                            for h in range(2):
                                v5 = wcol[slot][h].rearrange(
                                    "p m d (b n) -> p m d b n", b=IBLK)
                                for bb in range(IBLK):
                                    load(v5[:, mi, :, bb, :],
                                         stash[h][:, it0 * IBLK + bb]
                                         .rearrange("d p n -> p d n"),
                                         s=f"wld{slot}", key=f"wcol{slot}")

                    for it in range(NIT):
                        if it == 0:
                            load_wcol(0, 0)
                        if it + 1 < NIT:
                            load_wcol((it + 1) % 2, it + 1)
                        wb = wcol[it % 2]
                        wait(T, f"wld{it % 2}")
                        hs = it % 3
                        for si in range(ns):
                            pp = ps[si % 2]
                            guard(T, f"bank{si % 2}")
                            passes = ((0, 0), (0, 1), (1, 0))
                            np_ = len(passes)
                            for mi2 in range(2):
                                reg = pp[:, mi2 * ITILE:(mi2 + 1) * ITILE]
                                k = 0
                                for xh_, wh_ in passes:
                                    for dc in range(DCH):
                                        mm = T.matmul(
                                            reg, xqT[si][xh_][:, dc, :],
                                            wb[wh_][:, mi2, dc, :],
                                            start=(k == 0),
                                            stop=(k == np_ * DCH - 1))
                                        k += 1
                            pt = inc(mm, "t")
                            if si == ns - 1:
                                used(f"wcol{it % 2}", "t")
                            wait(A, "t", pt)
                            hp = si % 2
                            guard(A, f"hsilu{hp}")
                            A.activation(out=hsilu[hp], in_=pp[:, 0:ITILE],
                                         func=AF.Silu)
                            hu = xres[hp]
                            guard(A, f"xres{hp}")
                            ai = A.copy(out=hu[:, 0:ITILE], in_=pp[:, ITILE:2 * ITILE])
                            pa = inc(ai, "a")
                            used(f"bank{si % 2}", "a")
                            wait(G, "a", pa)
                            if si == 0:
                                guard(G, f"hb{hs}")
                            gi = G.tensor_mul(out=hb[hs][:, si * ITILE:(si + 1) * ITILE],
                                              in0=hsilu[hp], in1=hu[:, 0:ITILE])
                            pg = inc(gi, "g")
                            used(f"xres{hp}", "g")
                            used(f"hsilu{hp}", "g")
                        pending_it.append((it, hs, ns))
                        if len(pending_it) >= 3:
                            flush_itile()
                    while pending_it:
                        flush_itile()
                    wait(T, "tpe")

                def emit_down(stiles):
                    ns = len(stiles)
                    if True:
                        htiles = stiles
                        for dt in range(D // 512):
                            pts = []

                            def load_wdn(ic0):
                                sl = ic0 % 3
                                for h in range(2):
                                    load(wdn[sl][h].rearrange("p (b n) -> p b n", b=4),
                                         d_std[h][ic0, dt * 4:(dt + 1) * 4]
                                         .rearrange("b p n -> p b n"),
                                         s=f"dld{sl}", key=f"wdn{sl}")

                            load_wdn(0)
                            load_wdn(1)
                            for ic in range(ICH):
                                wbD = wdn[ic % 3]
                                lv = cnt[f"dld{ic % 3}"]
                                if ic + 2 < ICH:
                                    load_wdn(ic + 2)
                                wait(T, f"dld{ic % 3}", lv)
                                mms = []
                                for si2 in range(len(htiles)):
                                    si = si2
                                    pd = ps[2 + si2]
                                    if ic == 0:
                                        guard(T, f"bank{2 + si2}")
                                    for pi, (xh_, wh_) in enumerate(
                                            ((0, 0), (0, 1), (1, 0))):
                                        mms.append(T.matmul(
                                            pd[:, :512], hqT[si][xh_][:, ic, :],
                                            wbD[wh_], start=(ic == 0 and pi == 0),
                                            stop=(ic == ICH - 1 and pi == 2)))
                                inc(mms[-1], f"dfree{ic % 3}")
                                used(f"wdn{ic % 3}", f"dfree{ic % 3}")
                                if ic == ICH - 1:
                                    for si2 in range(len(htiles)):
                                        m_ = mms[si2 * 3 + 2]
                                        if si2 == len(htiles) - 1:
                                            pts.append((f"dfree{ic % 3}",
                                                        cnt[f"dfree{ic % 3}"]))
                                        else:
                                            pts.append(("t", inc(m_, "t")))
                            for si2, t in enumerate(htiles):
                                pd = ps[2 + si2]
                                xr = xres[si2 % 2]
                                lvs = f"ld_xres{si2 % 2}"
                                lv = load(xr, src_res(t)[:, dt * 512:(dt + 1) * 512],
                                          key=f"xres{si2 % 2}")
                                wait(A, pts[si2][0], pts[si2][1])
                                sl = si2
                                sgb = stg1[sl]
                                guard(A, f"stg{sl}")
                                ai = A.activation(out=sgb, in_=pd[:, :512],
                                                  func=AF.Copy, scale=0.5)
                                pa = inc(ai, "a")
                                used(f"bank{2 + si2}", "a")
                                wait(G, "a", pa)
                                wait(G, lvs, lv)
                                pg = inc(G.tensor_add(out=sgb, in0=sgb, in1=xr), "g")
                                used(f"xres{si2 % 2}", "g")
                                wait(A, "g", pg)
                                store(dst_of(t)[:, dt * 512:(dt + 1) * 512], sgb, sl)
                                used(f"stg{sl}", f"st{sl}")

                for sw in range(nsw):
                    stiles = stiles_of = tiles[sw * SWEEP:(sw + 1) * SWEEP]
                    if sw == 0:
                        emit_pre(stiles)
                    emit_gateup(stiles)
                    if sw + 1 < nsw:
                        emit_pre(tiles[(sw + 1) * SWEEP:(sw + 2) * SWEEP])
                    emit_down(stiles)
                wait_all_stores(SY)

            if "ffn1" in phases: emit_ffn(
                tiles=list(range(NT)),
                src_x=lambda t: d_x[t * 128:(t + 1) * 128, :],
                src_res=lambda t: d_x[t * 128:(t + 1) * 128, :],
                dst_of=lambda t: (d_x1ctx[t * 128:(t + 1) * 128, :] if t < OWN0 else
                                  d_x1own[(t - OWN0) * 128:(t - OWN0 + 1) * 128, :]),
            )

            def x1_src(t):
                return (d_x1ctx[t * 128:(t + 1) * 128, :] if t < OWN0 else
                        d_x1own[(t - OWN0) * 128:(t - OWN0 + 1) * 128, :])

            V.memset(a_M[0], 0.0)
            V.memset(a_MT[0], 0.0)
            V.drain()
            inc(V.engine_nop(), "v")
            mcur = {"i": 0}

            cs_state = {"n": 0}

            def cslot():
                bk = 2 + (cs_state["n"] % 2)
                cs_state["n"] += 1
                key = f"bank{bk}"
                pst = ps[bk][:, 0:128]
                guard(T, key)
                return pst, key

            def cmm(lhsT, rhs, dst, extra=None, n=128):
                pst, key = cslot()
                pm = inc(T.matmul(pst[:, 0:n], lhsT, rhs, start=True, stop=True), "csm")
                wait(V, "csm", pm)
                if extra is None:
                    V.tensor_copy(out=dst, in_=pst[:, 0:n])
                else:
                    extra(dst, pst[:, 0:n])
                V.drain()
                inc(V.engine_nop(), "v")
                used(key, "v")

            def emit_chunk(si, own, t_own):
                nx = 1 - mcur["i"]
                Mc, MTc = a_M[mcur["i"]], a_MT[mcur["i"]]
                Mn, MTn = a_M[nx], a_MT[nx]
                acol = a_acol[:, si:si + 1]
                bcol = a_bcol[:, si:si + 1]
                Kt = a_K[:, si, :]
                Vt = a_V[:, si, :]
                QTt = a_QT[:, si, :]
                GTt = a_GT[:, si, :]

                wait(A, "v", cnt["v"])
                ai = ai = ai = ai = ai = ai = ai = A.activation(out=a_la, in_=acol, func=AF.Ln)
                pa = inc(ai, "a")
                wait(T, "a", pa)
                cmm(c_ut, a_la, a_cum, n=1)
                wait(T, "v")
                pst, key = cslot()
                ptr = inc(T.transpose(pst[0:1, 0:128], a_cum, c_id), "csm")
                wait(V, "csm", ptr)
                V.tensor_copy(out=a_cumrow, in_=pst[0:1, 0:128])
                V.drain()
                pv = inc(V.engine_nop(), "v")
                used(key, "v")
                wait(T, "v", pv)
                cmm(c_ones, a_cumrow, a_cumrep)
                pv = cnt["v"]
                wait(A, "v", pv)
                A.activation(out=a_gam, in_=a_cum, func=AF.Exp)
                ai = A.activation(out=a_gamL, in_=a_cumrep[:, 127:128], func=AF.Exp)
                pa = inc(ai, "a")
                wait(V, "a", pa)
                V.tensor_sub(out=a_wgt, in0=a_cumrep[:, 127:128], in1=a_cum)
                V.drain()
                pv = inc(V.engine_nop(), "v")
                wait(A, "v", pv)
                ai = A.activation(out=a_wgt, in_=a_wgt, func=AF.Exp)
                pa = inc(ai, "a")
                wait(V, "a", pa)
                V.tensor_mul(out=a_wgt, in0=a_wgt, in1=bcol)
                V.tensor_scalar_mul(out=a_negb, in0=bcol, scalar1=-1.0)
                V.drain()
                V.tensor_scalar(out=a_t1, in0=a_cumrep, scalar1=a_cum, scalar2=None,
                                op0=AL.subtract)
                V.drain()
                V.tensor_add(out=a_t1, in0=a_t1, in1=c_aiu)
                V.drain()
                pv = inc(V.engine_nop(), "v")
                wait(A, "v", pv)
                ai = A.activation(out=a_gtincl, in_=a_t1, func=AF.Exp)
                if own:
                    ai = A.activation(out=a_gamrep, in_=a_cumrep, func=AF.Exp)
                pa = inc(ai, "a")
                wait(V, "a", pa)
                V.tensor_sub(out=a_t1, in0=a_gtincl, in1=c_id)
                V.drain()
                pv = inc(V.engine_nop(), "v")

                wait(T, "v", pv)
                transpose128(Kt, a_KT)
                if own:
                    transpose128(Vt, a_VT)
                wait(T, "tpe")
                cmm(a_KT, a_KT, a_kk)
                V.tensor_mul(out=s_nb[0], in0=a_t1, in1=a_kk)
                V.drain()
                V.tensor_scalar_mul(out=s_nb[0], in0=s_nb[0], scalar1=a_negb)
                V.tensor_scalar_mul(out=a_gk, in0=Kt, scalar1=a_gam)
                V.tensor_scalar_mul(out=a_kw, in0=Kt, scalar1=a_wgt)
                V.drain()
                pv = inc(V.engine_nop(), "v")
                wait(T, "v", pv)
                transpose128(s_nb[0], s_na[0])
                wait(V, "tpe")

                V.tensor_copy(out=a_Z, in_=a_kw)
                if own:
                    V.tensor_copy(out=a_Wm, in_=a_gk)
                V.drain()
                pv = inc(V.engine_nop(), "v")
                if own:
                    wait(T, "v", pv)
                    wait(T, "tpe")
                    cmm(a_VT, QTt, a_PF)
                cur = 0
                for m in range(7):
                    wait(T, "v")
                    cmm(s_na[cur], a_Z, a_Z,
                        extra=lambda d, p: V.tensor_add(out=d, in0=d, in1=p))
                    if own:
                        cmm(s_nb[cur], a_Wm, a_Wm,
                            extra=lambda d, p: V.tensor_add(out=d, in0=d, in1=p))
                        cmm(s_nb[cur], a_PF, a_PF,
                            extra=lambda d, p: V.tensor_add(out=d, in0=d, in1=p))
                    if m < 6:
                        wait(T, "v")
                        cmm(s_nb[cur], s_na[cur], s_na[1 - cur])
                        cmm(s_na[cur], s_nb[cur], s_nb[1 - cur])
                        cur = 1 - cur

                wait(T, "v")
                cmm(a_gk, a_Z, a_ET)
                cmm(a_Z, Vt, a_F)
                cmm(Vt, a_Z, a_FT)
                if own:
                    transpose128(a_Wm, a_WT)

                if own:
                    wait(T, "v")
                    cmm(MTc, QTt, a_MQ)
                    wait(T, "v")
                    wait(T, "tpe")
                    cmm(a_WT, a_MQ, a_BT,
                        extra=lambda d, p: V.scalar_tensor_tensor(
                            out=d, in0=p, scalar=-1.0, in1=a_PF,
                            op0=AL.mult, op1=AL.add))
                    V.tensor_mul(out=a_BT, in0=a_BT, in1=a_gtincl)
                    V.drain()
                    V.tensor_scalar_mul(out=a_BT, in0=a_BT, scalar1=bcol)
                    V.drain()
                    pv = inc(V.engine_nop(), "v")
                    wait(T, "v", pv)
                    cmm(Kt, a_BT, a_t2)
                    V.tensor_mul(out=a_t1, in0=a_MQ, in1=a_gamrep)
                    V.drain()
                    V.tensor_add(out=a_t1, in0=a_t1, in1=a_t2)
                    V.drain()
                    V.tensor_mul(out=a_OT, in0=a_t1, in1=GTt)
                    V.drain()
                    pv = inc(V.engine_nop(), "v")
                    wait(T, "v", pv)
                    for half in range(D // 512):
                        guard(T, "bank5")
                        mmO = T.matmul(ps[5][:, 0:512], a_OT,
                                       w_awo[:, half * 512:(half + 1) * 512],
                                       start=True, stop=True)
                        pt = inc(mmO, "t")
                        xr = xres[half % 2]
                        lvs = f"ld_xres{half % 2}"
                        lv = load(xr, d_x1own[t_own * 128:(t_own + 1) * 128,
                                              half * 512:(half + 1) * 512],
                                  key=f"xres{half % 2}")
                        wait(V, lvs, lv)
                        wait(V, "t", pt)
                        sgb = stg1[half]
                        guard(V, f"stg{half}")
                        V.tensor_add(out=sgb, in0=ps[5][:, 0:512], in1=xr)
                        V.drain()
                        V.tensor_add(out=sgb, in0=sgb,
                                     in1=b_o[:, half * 512:(half + 1) * 512])
                        V.drain()
                        pv = inc(V.engine_nop(), "v")
                        used("bank5", "v")
                        used(f"xres{half % 2}", "v")
                        wait(A, "v", pv)
                        store(d_x2own[t_own * 128:(t_own + 1) * 128,
                                      half * 512:(half + 1) * 512], sgb, half)
                        used(f"stg{half}", f"st{half}")

                wait(T, "v")
                cmm(a_ET, Mc, a_t1,
                    extra=lambda d, p: V.scalar_tensor_tensor(
                        out=d, in0=p, scalar=-1.0, in1=a_F, op0=AL.mult, op1=AL.add))
                cmm(Mc, a_ET, a_t2,
                    extra=lambda d, p: V.scalar_tensor_tensor(
                        out=d, in0=p, scalar=-1.0, in1=a_FT, op0=AL.mult, op1=AL.add))
                V.scalar_tensor_tensor(out=Mn, in0=Mc, scalar=a_gamL, in1=a_t1,
                                       op0=AL.mult, op1=AL.add)
                V.scalar_tensor_tensor(out=MTn, in0=MTc, scalar=a_gamL, in1=a_t2,
                                       op0=AL.mult, op1=AL.add)
                V.drain()
                inc(V.engine_nop(), "v")
                mcur["i"] = nx

            ngroups = NT // GROUP if "attn" in phases else 0
            for g in range(ngroups):
                gtiles = list(range(g * GROUP, (g + 1) * GROUP))
                own_g = gtiles[0] >= OWN0
                wait(A, "t")
                wait(A, "tpm")
                wait(V, "t")
                for si, t in enumerate(gtiles):
                    b = xin[si % 2]
                    lvs = f"ld_xin{si % 2}"
                    lv = load(b[:, 0:D], x1_src(t), key=f"xin{si % 2}")
                    wait(V, lvs, lv)
                    ln_tile(b, scr4, srckey=f"xin{si % 2}")
                    wait(T, "v")
                    for dc in range(DCH):
                        transpose128(scr4[:, dc * 128:(dc + 1) * 128],
                                     a_xh2T[:, dc, si * 128:(si + 1) * 128])
                    wait(V, "tpe")
                wait(T, "tpe")

                for si, t in enumerate(gtiles):
                    pp = ps[si % 2]
                    guard(T, f"bank{si % 2}")
                    for dc in range(DCH):
                        mm = T.matmul(pp[:, 0:258],
                                      a_xh2T[:, dc, si * 128:(si + 1) * 128],
                                      w_awkvab[:, dc, :], start=(dc == 0),
                                      stop=(dc == DCH - 1))
                    pt = inc(mm, "t")
                    wait(V, "t", pt)
                    V.tensor_add(out=scr1[:, 0:256], in0=pp[:, 0:256],
                                 in1=b_kvab[:, 0:256])
                    V.drain()
                    pv = inc(V.engine_nop(), "v")
                    wait(A, "v", pv)
                    A.activation(out=scr2[:, 0:128], in_=scr1[:, 0:128], func=AF.Square,
                                 accum_out=sm_s[:, 0:1])
                    A.drain()
                    A.activation(out=sm_s[:, 1:2], in_=sm_s[:, 0:1], func=AF.Sqrt)
                    A.activation(out=a_acol[:, si:si + 1], in_=pp[:, 256:257],
                                 func=AF.Sigmoid, bias=b_a)
                    ai = A.activation(out=a_bcol[:, si:si + 1], in_=pp[:, 257:258],
                                      func=AF.Sigmoid, bias=b_b)
                    pa = inc(ai, "a")
                    used(f"bank{si % 2}", "a")
                    wait(V, "a", pa)
                    V.tensor_scalar_max(out=sm_s[:, 1:2], in0=sm_s[:, 1:2],
                                        scalar1=1e-12)
                    V.drain()
                    pv = inc(V.engine_nop(), "v")
                    wait(A, "v", pv)
                    V.reciprocal(out=sm_r[:, 0:1], in_=sm_s[:, 1:2])
                    V.drain()
                    V.tensor_scalar_mul(out=a_K[:, si, :], in0=scr1[:, 0:128],
                                        scalar1=sm_r[:, 0:1])
                    V.tensor_copy(out=a_V[:, si, :], in_=scr1[:, 128:256])
                    V.drain()
                    inc(V.engine_nop(), "v")
                if own_g:
                    for si, t in enumerate(gtiles):
                        pp = ps[si % 2]
                        guard(T, f"bank{si % 2}")
                        for dc in range(DCH):
                            T.matmul(pp[:, 0:128], w_awq[:, dc, :],
                                     a_xh2T[:, dc, si * 128:(si + 1) * 128],
                                     start=(dc == 0), stop=(dc == DCH - 1))
                        for dc in range(DCH):
                            mm = T.matmul(pp[:, 128:256], w_awg[:, dc, :],
                                          a_xh2T[:, dc, si * 128:(si + 1) * 128],
                                          start=(dc == 0), stop=(dc == DCH - 1))
                        pt = inc(mm, "t")
                        wait(V, "t", pt)
                        V.tensor_scalar(out=a_QT[:, si, :], in0=pp[:, 0:128],
                                        scalar1=b_q, scalar2=None, op0=AL.add)
                        V.drain()
                        pv = inc(V.engine_nop(), "v")
                        wait(A, "v", pv)
                        ai = A.activation(out=a_GT[:, si, :], in_=pp[:, 128:256],
                                          func=AF.Sigmoid, bias=b_g)
                        pa = inc(ai, "a")
                        used(f"bank{si % 2}", "a")
                        wait(V, "a", pa)

                for si, t in enumerate(gtiles):
                    emit_chunk(si, own_g, (t - OWN0) if own_g else 0)

            wait_all_stores(SY)
            if "ffn2" in phases: emit_ffn(
                tiles=list(range(NOWN)),
                src_x=lambda t: d_x2own[t * 128:(t + 1) * 128, :],
                src_res=lambda t: d_x2own[t * 128:(t + 1) * 128, :],
                dst_of=lambda t: d_out[t * 128:(t + 1) * 128, :],
            )

            if "ffn2" not in phases:
                src_dump = d_x2own if "attn" in phases else d_x1own
                for t in range(0, NOWN * 128, 128):
                    load(xin[0][:, 0:D], src_dump[t:t + 128, :],
                         s="dbg_ld", key="xin0")
                    wait(A, "dbg_ld")
                    store(d_out[t:t + 128, :], xin[0][:, 0:D], 0)
                    used("xin0", "st0")
                    wait(SY, "st0")
            wait_all_stores(SY)

    return nc


def make_host_inputs(cfg, ins):
    NT, OWN0, D, I = cfg["NTILES"], cfg["OWN0"], cfg["D"], cfg["I"]
    NOWN = NT - OWN0
    x = np.ascontiguousarray(np.asarray(ins["L0_attn_x"], dtype=np.float32))
    B = x.shape[0]
    Wk = np.asarray(ins["L0_attn_Wk"], np.float32)
    Wv = np.asarray(ins["L0_attn_Wv"], np.float32)
    Wa = np.asarray(ins["L0_attn_Wa"], np.float32)
    Wb = np.asarray(ins["L0_attn_Wb"], np.float32)
    wkvab = np.ascontiguousarray(np.concatenate([Wk, Wv, Wa, Wb], axis=1))
    bk = np.asarray(ins["L0_attn_bk"], np.float32)
    bv = np.asarray(ins["L0_attn_bv"], np.float32)
    bkvab = np.ascontiguousarray(
        np.broadcast_to(np.concatenate([bk, bv, [0.0, 0.0]]).astype(np.float32)[None, :],
                        (128, 258)))
    common = {
        "wg": np.ascontiguousarray(np.asarray(ins["L0_ffn_w_gate"], np.float32)),
        "wu": np.ascontiguousarray(np.asarray(ins["L0_ffn_w_up"], np.float32)),
        "wd": np.ascontiguousarray(np.asarray(ins["L0_ffn_w_down"], np.float32)),
        "awq": np.ascontiguousarray(np.asarray(ins["L0_attn_Wq"], np.float32)),
        "awg": np.ascontiguousarray(np.asarray(ins["L0_attn_Wg"], np.float32)),
        "awkvab": wkvab,
        "awo": np.ascontiguousarray(np.asarray(ins["L0_attn_Wo"], np.float32)),
        "bkvab": bkvab,
        "bqcol": np.ascontiguousarray(np.asarray(ins["L0_attn_bq"], np.float32)[:, None]),
        "bgcol": np.ascontiguousarray(np.asarray(ins["L0_attn_bg"], np.float32)[:, None]),
        "bacol": np.ascontiguousarray(
            np.broadcast_to(np.asarray(ins["L0_attn_ba"], np.float32), (128,))[:, None]),
        "bbcol": np.ascontiguousarray(
            np.broadcast_to(np.asarray(ins["L0_attn_bb"], np.float32), (128,))[:, None]),
        "borep": np.ascontiguousarray(
            np.broadcast_to(np.asarray(ins["L0_attn_bo"], np.float32)[None, :], (128, D))),
        "c_ut": np.triu(np.ones((128, 128), np.float32)),
        "c_id": np.eye(128, dtype=np.float32),
        "c_idb": np.eye(128, dtype=np.float32).astype(ml_dtypes.bfloat16),
        "c_aiu": ((np.triu(np.ones((128, 128), np.float32)) - 1.0) * 1e9).astype(np.float32),
        "c_ones1": np.ones((1, 128), np.float32),
    }
    half = OWN0 * 128
    in_maps = []
    for c in range(2 * B):
        bi, hi = c // 2, c % 2
        xa = np.zeros((NT * 128, D), np.float32)
        if hi == 0:
            xa[half:half + NOWN * 128] = x[bi, 0:NOWN * 128]
        else:
            xa[0:half] = x[bi, 0:half]
            xa[half:half + NOWN * 128] = x[bi, half:half + NOWN * 128]
        in_maps.append(dict(common, xall=np.ascontiguousarray(xa)))
    return in_maps


def assemble(cfg, results, B, T):
    NOWN = cfg["NTILES"] - cfg["OWN0"]
    D = cfg["D"]
    out = np.zeros((B, T, D), np.float32)
    half = cfg["OWN0"] * 128
    for c, r in enumerate(results):
        bi, hi = c // 2, c % 2
        if hi == 0:
            out[bi, 0:NOWN * 128] = r["yout"]
        else:
            out[bi, half:half + NOWN * 128] = r["yout"]
    return out


from concourse.bass_utils import run_bass_kernel_spmd

_NC_CACHE = {}


def kernel(**inputs):
    cfg = FULL
    if "nc" not in _NC_CACHE:
        _NC_CACHE["nc"] = build_nc(cfg)
    nc = _NC_CACHE["nc"]
    in_maps = make_host_inputs(cfg, inputs)
    res = run_bass_kernel_spmd(nc, in_maps, core_ids=list(range(8)))
    x = np.asarray(inputs["L0_attn_x"])
    return assemble(cfg, res.results, x.shape[0], x.shape[1])


# version 15
# speedup vs baseline: 1.2848x; 1.0269x over previous
import contextlib
import numpy as np
import ml_dtypes
import concourse.bass as bass
import concourse.mybir as mybir

F32 = mybir.dt.float32
BF16 = mybir.dt.bfloat16
I32 = mybir.dt.int32
AF = mybir.ActivationFunctionType
AL = mybir.AluOpType
AX = mybir.AxisListType

FULL = dict(NTILES=32, OWN0=16, D=1024, I=2816, ITILE=256, SWEEP=4, GROUP=2)
TINY = dict(NTILES=2, OWN0=1, D=1024, I=2816, ITILE=256, SWEEP=2, GROUP=1)
MID = dict(NTILES=8, OWN0=4, D=1024, I=2816, ITILE=256, SWEEP=4, GROUP=2)


def build_nc(cfg, phases=("prep", "ffn1", "attn", "ffn2")):
    NT, OWN0, D, I = cfg["NTILES"], cfg["OWN0"], cfg["D"], cfg["I"]
    ITILE, SWEEP, GROUP = cfg["ITILE"], cfg["SWEEP"], cfg["GROUP"]
    DCH, ICH = D // 128, I // 128
    NIT = I // ITILE
    NOWN = NT - OWN0
    IBLK = ITILE // 128
    SCRW = D
    SLOTW = SWEEP * ITILE
    QW = max(2 * D, 2 * SLOTW)

    nc = bass.Bass("TRN2", target_bir_lowering=False, debug=False)

    d_x = nc.dram_tensor("xall", [NT * 128, D], F32, kind="ExternalInput").ap()
    d_out = nc.dram_tensor("yout", [NOWN * 128, D], F32, kind="ExternalOutput").ap()
    d_wg = nc.dram_tensor("wg", [I, D], F32, kind="ExternalInput").ap()
    d_wu = nc.dram_tensor("wu", [I, D], F32, kind="ExternalInput").ap()
    d_wd = nc.dram_tensor("wd", [D, I], F32, kind="ExternalInput").ap()
    d_awq = nc.dram_tensor("awq", [D, 128], F32, kind="ExternalInput").ap()
    d_awg = nc.dram_tensor("awg", [D, 128], F32, kind="ExternalInput").ap()
    d_awkvab = nc.dram_tensor("awkvab", [D, 258], F32, kind="ExternalInput").ap()
    d_awo = nc.dram_tensor("awo", [128, D], F32, kind="ExternalInput").ap()
    d_bkvab = nc.dram_tensor("bkvab", [128, 258], F32, kind="ExternalInput").ap()
    d_bq = nc.dram_tensor("bqcol", [128, 1], F32, kind="ExternalInput").ap()
    d_bg = nc.dram_tensor("bgcol", [128, 1], F32, kind="ExternalInput").ap()
    d_ba = nc.dram_tensor("bacol", [128, 1], F32, kind="ExternalInput").ap()
    d_bb = nc.dram_tensor("bbcol", [128, 1], F32, kind="ExternalInput").ap()
    d_bo = nc.dram_tensor("borep", [128, D], F32, kind="ExternalInput").ap()
    d_cut = nc.dram_tensor("c_ut", [128, 128], F32, kind="ExternalInput").ap()
    d_cid = nc.dram_tensor("c_id", [128, 128], F32, kind="ExternalInput").ap()
    d_cidb = nc.dram_tensor("c_idb", [128, 128], BF16, kind="ExternalInput").ap()
    d_caiu = nc.dram_tensor("c_aiu", [128, 128], F32, kind="ExternalInput").ap()
    d_cones = nc.dram_tensor("c_ones1", [1, 128], F32, kind="ExternalInput").ap()

    d_x1ctx = nc.dram_tensor("x1ctx", [max(OWN0, 1) * 128, D], F32, kind="Internal").ap()
    d_x1own = nc.dram_tensor("x1own", [NOWN * 128, D], F32, kind="Internal").ap()
    d_x2own = nc.dram_tensor("x2own", [NOWN * 128, D], F32, kind="Internal").ap()
    d_stg = [nc.dram_tensor(f"stash_g{h}", [DCH, ICH, 128, 128], BF16,
                            kind="Internal").ap() for h in range(2)]
    d_stu = [nc.dram_tensor(f"stash_u{h}", [DCH, ICH, 128, 128], BF16,
                            kind="Internal").ap() for h in range(2)]
    d_std = [nc.dram_tensor(f"stash_d{h}", [ICH, DCH, 128, 128], BF16,
                            kind="Internal").ap() for h in range(2)]

    es = contextlib.ExitStack()
    with es:
        def sb(name, shape, dt=F32):
            return es.enter_context(nc.sbuf_tensor(name, shape, dt)).ap()

        c_ut = sb("c_ut_s", [128, 128])
        c_id = sb("c_id_s", [128, 128])
        c_aiu = sb("c_aiu_s", [128, 128])
        c_ones = sb("c_ones_s", [1, 128])
        w_awq = sb("w_awq", [128, DCH, 128])
        w_awg = sb("w_awg", [128, DCH, 128])
        w_awkvab = sb("w_awkvab", [128, DCH, 258])
        w_awo = sb("w_awo", [128, D])
        b_kvab = sb("b_kvab_s", [128, 258])
        b_q = sb("b_q_s", [128, 1])
        b_g = sb("b_g_s", [128, 1])
        b_a = sb("b_a_s", [128, 1])
        b_b = sb("b_b_s", [128, 1])
        b_o = sb("b_o_s", [128, D])
        c_eps = sb("c_eps", [128, 1])

        xin = [sb("xin0", [128, QW]), sb("xin1", [128, SCRW])]
        scr1 = sb("scr1", [128, QW])
        scr2 = sb("scr2", [128, QW])
        scr3 = sb("scr3", [128, QW])
        scr4 = sb("scr4", [128, SCRW])
        sm_m = sb("sm_m", [128, QW // 16])
        sm_s = sb("sm_s", [128, QW // 16])
        sm_r = sb("sm_r", [128, QW // 16])
        xqT = [[sb(f"xqT{t}_{h}", [128, DCH, 128], BF16) for h in range(2)]
               for t in range(SWEEP)]
        wcol = [[sb(f"wcol{i}_{h}", [128, 2, DCH, ITILE], BF16) for h in range(2)]
                for i in range(2)]
        c_idb = sb("c_idb_s", [128, 128], BF16)
        sp_h = sb("sp_h", [128, QW], BF16)
        sp_l = sb("sp_l", [128, QW], BF16)
        hb_all = sb("hb_all", [128, 4 * SLOTW])
        hb = [hb_all[:, i * SLOTW:(i + 1) * SLOTW] for i in range(4)]
        hsilu = [sb(f"hsilu{i}", [128, ITILE]) for i in range(2)]
        hq_m, hq_s, hq_r = sm_m, sm_s, sm_r
        hqT = [[sb(f"hqT{t}_{h}", [128, ICH, 128], BF16) for h in range(2)]
               for t in range(SWEEP)]
        wdn = [[sb(f"wdn{i}_{h}", [128, 512], BF16) for h in range(2)]
               for i in range(3)]
        xres = [sb(f"xres{i}", [128, 512]) for i in range(2)]
        stg1 = [sb(f"stg1_{i}", [128, 512]) for i in range(4)]

        a_xh2T = hb_all[:, 0:DCH * GROUP * 128].rearrange(
            "p (d n) -> p d n", n=GROUP * 128)
        hpool = [hqT[si][h].bitcast(F32).rearrange("p a n -> p (a n)")
                 for si in (0, 1) for h in (0, 1)]
        hp_off = [0, 0, 0, 0]

        def carve(pi, w):
            c0 = hp_off[pi]
            hp_off[pi] = c0 + w
            v = hpool[pi][:, c0:c0 + w]
            assert hp_off[pi] <= ICH * 64, "hqT pool overflow"
            return v

        a_K = carve(0, GROUP * 128).rearrange("p (g n) -> p g n", n=128)
        a_V = carve(0, GROUP * 128).rearrange("p (g n) -> p g n", n=128)
        a_QT = carve(0, GROUP * 128).rearrange("p (g n) -> p g n", n=128)
        a_GT = carve(0, GROUP * 128).rearrange("p (g n) -> p g n", n=128)
        a_KT = carve(0, 128)
        a_VT = carve(0, 128)
        a_acol = sb("a_acol", [128, GROUP])
        a_bcol = sb("a_bcol", [128, GROUP])
        a_la = sb("a_la", [128, 1])
        a_cum = sb("a_cum", [128, 1])
        a_cumrow = sb("a_cumrow", [1, 128])
        a_gam = sb("a_gam", [128, 1])
        a_gamL = sb("a_gamL", [128, 1])
        a_wgt = sb("a_wgt", [128, 1])
        a_negb = sb("a_negb", [128, 1])
        a_cumrep = carve(1, 128)
        a_gamrep = carve(1, 128)
        a_gtincl = carve(1, 128)
        a_kk = carve(1, 128)
        s_na = [carve(1, 128) for i in range(2)]
        s_nb = [carve(1, 128) for i in range(2)]
        a_gk = carve(1, 128)
        a_kw = carve(1, 128)
        a_Z = carve(1, 128)
        a_Wm = carve(2, 128)
        a_PF = carve(2, 128)
        a_WT = carve(2, 128)
        a_ET = carve(2, 128)
        a_F = carve(2, 128)
        a_FT = carve(2, 128)
        a_M = [carve(2, 128) for i in range(2)]
        a_MT = [carve(2, 128) for i in range(2)]
        a_MQ = carve(3, 128)
        a_BT = carve(3, 128)
        a_OT = carve(3, 128)
        a_t1 = carve(3, 128)
        a_t2 = carve(3, 128)

        ps = [es.enter_context(nc.psum_tensor(f"ps{i}", [128, 512], F32)).ap()
              for i in range(8)]

        sems, cnt = {}, {}

        def sem(name):
            if name not in sems:
                sems[name] = es.enter_context(nc.semaphore(name))
                cnt[name] = 0
            return sems[name]

        def inc(instr, name, amt=1):
            s = sem(name)
            cnt[name] += amt
            instr.then_inc(s, amt)
            return cnt[name]

        def wait(eng, name, val=None):
            if name not in cnt and val is None:
                return
            s = sem(name)
            v = cnt[name] if val is None else val
            if v > 0:
                eng.wait_ge(s, v)

        V, A, T, G, SY = nc.vector, nc.scalar, nc.tensor, nc.gpsimd, nc.sync

        lastuse = {}

        def guard(eng, key):
            if key in lastuse:
                sname, val = lastuse[key]
                wait(eng, sname, val)

        def used(key, sname):
            sem(sname)
            lastuse[key] = (sname, cnt[sname])

        def load(out, in_, s=None, key=None):
            if key is not None:
                guard(SY, key)
            if s is None:
                s = "ld_" + (key if key is not None else "c")
            return inc(SY.dma_start(out=out, in_=in_), s, 16)

        def store(out, in_, slot):
            return inc(A.dma_start(out=out, in_=in_), f"st{slot}", 16)

        def wait_all_stores(eng):
            for i in range(4):
                wait(eng, f"st{i}")

        with nc.Block():
            load(c_ut, d_cut); load(c_id, d_cid); load(c_aiu, d_caiu)
            load(c_idb, d_cidb)
            load(c_ones, d_cones)
            load(w_awq, d_awq.rearrange("(c p) n -> p c n", p=128))
            load(w_awg, d_awg.rearrange("(c p) n -> p c n", p=128))
            load(w_awkvab, d_awkvab.rearrange("(c p) n -> p c n", p=128))
            load(w_awo, d_awo)
            load(b_kvab, d_bkvab); load(b_q, d_bq); load(b_g, d_bg)
            load(b_a, d_ba); load(b_b, d_bb); load(b_o, d_bo)
            CONST_LD = cnt["ld_c"]
            for eng in (V, A, T):
                wait(eng, "ld_c", CONST_LD)
            V.memset(c_eps, 1e-5)
            V.drain()

            SH_MUL = float(np.float32(1.0) / np.float32(6.0)) * 0.5
            SH_MIN = float(np.float32(1e-12)) * 0.5

            def quant(src, out, F, mb, ssb, rb, t1, t2, t3, srckey=None):
                nb = F // 16
                v3 = src.rearrange("p (b k) -> p b k", k=16) if src.shape[1] == F \
                    else src
                guard(V, "scr1sign")
                guard(V, "scr2g")
                V.tensor_reduce(out=mb[:, :nb], in_=v3, axis=AX.X, op=AL.max,
                                apply_absolute_value=True)
                V.tensor_tensor(out=t1[:, :F], in0=src, in1=src, op=AL.mult)
                V.drain()
                V.tensor_scalar(out=ssb[:, :nb], in0=mb[:, :nb], scalar1=SH_MUL,
                                scalar2=SH_MIN, op0=AL.mult, op1=AL.max)
                V.drain()
                V.reciprocal(out=rb[:, :nb], in_=ssb[:, :nb])
                V.drain()
                V.tensor_tensor(out=mb[:, :nb], in0=rb[:, :nb], in1=rb[:, :nb],
                                op=AL.mult)
                V.drain()
                mb3 = mb[:, :nb].broadcast_to([128, nb, 16])
                t1v = t1[:, :F].rearrange("p (b k) -> p b k", k=16)
                V.tensor_tensor(out=t1v, in0=t1v, in1=mb3, op=AL.mult)
                V.drain()
                c1 = t3.bitcast(BF16)[:, 0:F]
                c2 = t3.bitcast(BF16)[:, F:2 * F]
                c3 = t2.bitcast(BF16)[:, 0:F]
                c4 = t2.bitcast(BF16)[:, F:2 * F]

                def cmp(dst, thr2, w):
                    V.tensor_scalar(out=dst, in0=t1[:, :F], scalar1=thr2,
                                    scalar2=w, op0=AL.is_gt, op1=AL.mult)

                cmp(c1, 0.25, 1.0)
                cmp(c2, 2.25, 1.0)
                cmp(c3, 6.25, 1.0)
                cmp(c4, 12.25, 1.0)
                V.drain()
                V.tensor_add(out=c1, in0=c1, in1=c2)
                V.tensor_add(out=c3, in0=c3, in1=c4)
                V.drain()
                cmp(c2, 25.0, 2.0)
                cmp(c4, 49.0, 2.0)
                V.drain()
                V.tensor_add(out=c1, in0=c1, in1=c3)
                V.tensor_add(out=c2, in0=c2, in1=c4)
                V.drain()
                cmp(c3, 100.0, 4.0)
                V.tensor_add(out=c1, in0=c1, in1=c2)
                V.drain()
                V.tensor_add(out=c1, in0=c1, in1=c3)
                V.drain()
                sh3 = ssb[:, :nb].broadcast_to([128, nb, 16])
                t2v = t2[:, :F].rearrange("p (b k) -> p b k", k=16)
                mm = V.tensor_tensor(out=t2v,
                                     in0=c1.rearrange("p (b k) -> p b k", k=16),
                                     in1=sh3, op=AL.mult)
                pv = inc(mm, "v")
                V.drain()
                sgb = t1.bitcast(BF16)[:, 0:F]
                wait(A, "v", pv)
                A.copy(out=sp_h[:, :F], in_=t2[:, :F])
                ai = A.activation(out=sgb, in_=src, func=AF.Sign)
                pa = inc(ai, "a")
                if srckey is not None:
                    used(srckey, "a")
                wait(G, "a", pa)
                G.tensor_tensor(out=sp_l[:, :F], in0=t2[:, :F], in1=sp_h[:, :F],
                                op=AL.subtract)
                G.drain()
                G.tensor_mul(out=sp_h[:, :F], in0=sp_h[:, :F], in1=sgb)
                gi = G.tensor_mul(out=sp_l[:, :F], in0=sp_l[:, :F], in1=sgb)
                inc(gi, "g")
                used("scr1sign", "g")
                used("scr2g", "g")
                return pv

            tp_state = {"n": 0}

            def transpose128(src_ap, dst_ap, ident=None):
                bk = 6 + (tp_state["n"] % 2)
                tp_state["n"] += 1
                key = f"bank{bk}"
                if ident is None:
                    pst = ps[bk][:, 0:128]
                else:
                    pst = ps[bk].bitcast(BF16)[:, 0:128]
                guard(T, key)
                mm = T.transpose(pst, src_ap, c_id if ident is None else ident)
                pt = inc(mm, "tpm")
                wait(A, "tpm", pt)
                inc(A.copy(out=dst_ap, in_=pst), "tpe")
                used(key, "tpe")
                return mm

            def transpose_hl(colslice, dsth, dstl):
                transpose128(sp_h[:, colslice], dsth, ident=c_idb)
                return transpose128(sp_l[:, colslice], dstl, ident=c_idb)

            def transpose_batch(pairs):
                assert len(pairs) <= 8
                bk = 6 + (tp_state["n"] % 2)
                tp_state["n"] += 1
                key = f"bank{bk}"
                pst = ps[bk].bitcast(BF16)
                guard(T, key)
                for i, (src, _) in enumerate(pairs):
                    mm = T.transpose(pst[:, i * 128:(i + 1) * 128], src, c_idb)
                pt = inc(mm, "tpm")
                wait(A, "tpm", pt)
                for i, (_, dst) in enumerate(pairs):
                    ai = A.copy(out=dst, in_=pst[:, i * 128:(i + 1) * 128])
                inc(ai, "tpe")
                used(key, "tpe")

            def transpose_batch_wide(srcs, dst):
                assert len(srcs) <= 8
                bk = 6 + (tp_state["n"] % 2)
                tp_state["n"] += 1
                key = f"bank{bk}"
                pst = ps[bk].bitcast(BF16)
                guard(T, key)
                for i, src in enumerate(srcs):
                    mm = T.transpose(pst[:, i * 128:(i + 1) * 128], src, c_idb)
                pt = inc(mm, "tpm")
                wait(A, "tpm", pt)
                inc(A.copy(out=dst, in_=pst[:, 0:len(srcs) * 128]), "tpe")
                used(key, "tpe")

            def ln_tile(src, out, srckey=None):
                V.tensor_reduce(out=sm_m[:, 0:1], in_=src[:, 0:D], axis=AX.X, op=AL.add)
                V.drain()
                V.tensor_scalar_mul(out=sm_m[:, 0:1], in0=sm_m[:, 0:1], scalar1=1.0 / D)
                V.drain()
                guard(V, "scr1sign")
                vi = V.tensor_scalar(out=scr1[:, 0:D], in0=src[:, 0:D],
                                     scalar1=sm_m[:, 0:1],
                                     scalar2=None, op0=AL.subtract)
                pv = inc(vi, "v")
                V.drain()
                if srckey is not None:
                    used(srckey, "v")
                wait(A, "v", pv)
                guard(A, "scr2g")
                A.activation(out=scr2[:, 0:D], in_=scr1[:, 0:D], func=AF.Square,
                             accum_out=sm_s[:, 0:1])
                A.drain()
                ai = A.activation(out=sm_r[:, 0:1], in_=sm_s[:, 0:1], func=AF.Sqrt,
                                  scale=1.0 / D, bias=c_eps)
                pa = inc(ai, "a")
                wait(V, "a", pa)
                V.reciprocal(out=sm_r[:, 1:2], in_=sm_r[:, 0:1])
                V.drain()
                vi = V.tensor_scalar_mul(out=out[:, 0:D], in0=scr1[:, 0:D],
                                         scalar1=sm_r[:, 1:2])
                pv = inc(vi, "v")
                V.drain()
                return pv

            stgb = [hqT[-1][h].rearrange("p a n -> p (a n)")[:, 0:1024]
                    for h in range(2)]

            def prep_group(segs, rt2):
                b = xin[0]
                lvs = "ld_xin0"
                off = 0
                for dram_ap, Fq, _ in segs:
                    lv = load(b[:, off:off + Fq], dram_ap, key="xin0")
                    off += Fq
                Ftot = off
                wait(V, lvs, lv)
                wait(V, "tpm")
                quant(b[:, :Ftot], b[:, :Ftot], Ftot, sm_m, sm_s, sm_r,
                      scr1, scr2, scr3, srckey="xin0")
                wait(T, "g")
                off = 0
                for _, Fq, wdst in segs:
                    nbk = Fq // 128
                    for h in range(2):
                        sp = sp_h if h == 0 else sp_l
                        for p0 in range(0, nbk, 8):
                            pw = min(8, nbk - p0)
                            sgb = stgb[h]
                            guard(A, f"stg{h}")
                            transpose_batch_wide(
                                [sp[:, off + (p0 + j) * 128:off + (p0 + j + 1) * 128]
                                 for j in range(pw)],
                                sgb[:, 0:pw * 128])
                            wait(A, "tpe")
                            store(wdst(h, p0, pw),
                                  sgb[:, 0:pw * 128]
                                  .rearrange("p (b n) -> p b n", n=128), h)
                            used(f"stg{h}", f"st{h}")
                    off += Fq

            rt2 = 0
            if "prep" in phases:
                groups = []
                for stash, wsrc in ((d_stg, d_wg), (d_stu, d_wu)):
                    for r0 in range(0, ICH, 2):
                        groups.append([
                            (wsrc[rt * 128:(rt + 1) * 128, :], D,
                             lambda h, p0, pw, rt=rt, stash=stash:
                             stash[h][p0:p0 + pw, rt].rearrange("b p n -> p b n"))
                            for rt in (r0, r0 + 1) if rt < ICH])
                for rt in range(DCH):
                    c0 = 0
                    while c0 < I:
                        cw = min(2 * D, I - c0)
                        groups.append([
                            (d_wd[rt * 128:(rt + 1) * 128, c0:c0 + cw], cw,
                             lambda h, p0, pw, rt=rt, c0=c0:
                             d_std[h][c0 // 128 + p0:c0 // 128 + p0 + pw, rt]
                             .rearrange("b p n -> p b n"))])
                        c0 += cw
                for g in groups:
                    prep_group(g, rt2)
                    rt2 += 1
            wait_all_stores(SY)
            wait_all_stores(A)

            def emit_ffn(tiles, src_x, src_res, dst_of):
                nsw = (len(tiles) + SWEEP - 1) // SWEEP

                def emit_pre(stiles):
                    wait(A, "t")
                    for si, t in enumerate(stiles):
                        b = xin[si % 2]
                        lvs = f"ld_xin{si % 2}"
                        lv = load(b[:, 0:D], src_x(t), key=f"xin{si % 2}")
                        wait(V, lvs, lv)
                        ln_tile(b, scr4, srckey=f"xin{si % 2}")
                        wait(V, "tpm")
                        quant(scr4[:, 0:D], scr4[:, 0:D], D, sm_m, sm_s, sm_r,
                              scr1, scr2, scr3)
                        wait(T, "g")
                        transpose_batch_wide(
                            [sp_h[:, dc * 128:(dc + 1) * 128] for dc in range(DCH)],
                            xqT[si][0].rearrange("p a n -> p (a n)"))
                        transpose_batch_wide(
                            [sp_l[:, dc * 128:(dc + 1) * 128] for dc in range(DCH)],
                            xqT[si][1].rearrange("p a n -> p (a n)"))
                        wait(V, "tpe")
                    wait(T, "tpe")

                def emit_gateup(stiles):
                    ns = len(stiles)
                    pending_it = []

                    def flush_items(items):
                        hs0 = items[0][1]
                        F0 = sum(nsX * ITILE for _, _, nsX in items)
                        src = hb_all[:, hs0 * SLOTW:hs0 * SLOTW + F0]
                        wait(V, "g")
                        wait(A, "g")
                        wait(V, "tpm")
                        quant(src, src, F0, hq_m, hq_s, hq_r, scr1, scr2, scr3)
                        wait(T, "g")
                        pairs_h, pairs_l = [], []
                        for k, (itX, hsX, nsX) in enumerate(items):
                            for si0 in range(nsX):
                                for bi in range(IBLK):
                                    col = k * SLOTW + si0 * ITILE + bi * 128
                                    pairs_h.append((sp_h[:, col:col + 128],
                                                    hqT[si0][0][:, itX * IBLK + bi, :]))
                                    pairs_l.append((sp_l[:, col:col + 128],
                                                    hqT[si0][1][:, itX * IBLK + bi, :]))
                        for pp_ in (pairs_h, pairs_l):
                            for k0 in range(0, len(pp_), 8):
                                transpose_batch(pp_[k0:k0 + 8])
                        for _, hsX, _ in items:
                            used(f"hb{hsX}", "tpm")

                    def load_wcol(slot, it0):
                        for mi, stash in enumerate((d_stg, d_stu)):
                            for h in range(2):
                                v5 = wcol[slot][h].rearrange(
                                    "p m d (b n) -> p m d b n", b=IBLK)
                                for bb in range(IBLK):
                                    load(v5[:, mi, :, bb, :],
                                         stash[h][:, it0 * IBLK + bb]
                                         .rearrange("d p n -> p d n"),
                                         s=f"wld{slot}", key=f"wcol{slot}")

                    for it in range(NIT):
                        if it == 0:
                            load_wcol(0, 0)
                        if it + 1 < NIT:
                            load_wcol((it + 1) % 2, it + 1)
                        wb = wcol[it % 2]
                        wait(T, f"wld{it % 2}")
                        hs = it % 4
                        for si in range(ns):
                            pp = ps[si % 2]
                            guard(T, f"bank{si % 2}")
                            passes = ((0, 0), (0, 1), (1, 0))
                            np_ = len(passes)
                            for mi2 in range(2):
                                reg = pp[:, mi2 * ITILE:(mi2 + 1) * ITILE]
                                k = 0
                                for xh_, wh_ in passes:
                                    for dc in range(DCH):
                                        mm = T.matmul(
                                            reg, xqT[si][xh_][:, dc, :],
                                            wb[wh_][:, mi2, dc, :],
                                            start=(k == 0),
                                            stop=(k == np_ * DCH - 1))
                                        k += 1
                            pt = inc(mm, "t")
                            if si == ns - 1:
                                used(f"wcol{it % 2}", "t")
                            wait(A, "t", pt)
                            hp = si % 2
                            guard(A, f"hsilu{hp}")
                            A.activation(out=hsilu[hp], in_=pp[:, 0:ITILE],
                                         func=AF.Silu)
                            hu = xres[hp]
                            guard(A, f"xres{hp}")
                            ai = A.copy(out=hu[:, 0:ITILE], in_=pp[:, ITILE:2 * ITILE])
                            pa = inc(ai, "a")
                            used(f"bank{si % 2}", "a")
                            wait(G, "a", pa)
                            if si == 0:
                                guard(G, f"hb{hs}")
                            gi = G.tensor_mul(out=hb[hs][:, si * ITILE:(si + 1) * ITILE],
                                              in0=hsilu[hp], in1=hu[:, 0:ITILE])
                            pg = inc(gi, "g")
                            used(f"xres{hp}", "g")
                            used(f"hsilu{hp}", "g")
                        pending_it.append((it, hs, ns))
                        if len(pending_it) >= 4:
                            flush_items([pending_it.pop(0), pending_it.pop(0)])
                    while pending_it:
                        if (len(pending_it) >= 2 and pending_it[0][1] % 2 == 0
                                and pending_it[1][1] == pending_it[0][1] + 1):
                            flush_items([pending_it.pop(0), pending_it.pop(0)])
                        else:
                            flush_items([pending_it.pop(0)])
                    wait(T, "tpe")

                def emit_down(stiles):
                    ns = len(stiles)
                    if True:
                        htiles = stiles
                        for dt in range(D // 512):
                            pts = []

                            def load_wdn(ic0):
                                sl = ic0 % 3
                                for h in range(2):
                                    load(wdn[sl][h].rearrange("p (b n) -> p b n", b=4),
                                         d_std[h][ic0, dt * 4:(dt + 1) * 4]
                                         .rearrange("b p n -> p b n"),
                                         s=f"dld{sl}", key=f"wdn{sl}")

                            load_wdn(0)
                            load_wdn(1)
                            for ic in range(ICH):
                                wbD = wdn[ic % 3]
                                lv = cnt[f"dld{ic % 3}"]
                                if ic + 2 < ICH:
                                    load_wdn(ic + 2)
                                wait(T, f"dld{ic % 3}", lv)
                                mms = []
                                for si2 in range(len(htiles)):
                                    si = si2
                                    pd = ps[2 + si2]
                                    if ic == 0:
                                        guard(T, f"bank{2 + si2}")
                                    for pi, (xh_, wh_) in enumerate(
                                            ((0, 0), (0, 1), (1, 0))):
                                        mms.append(T.matmul(
                                            pd[:, :512], hqT[si][xh_][:, ic, :],
                                            wbD[wh_], start=(ic == 0 and pi == 0),
                                            stop=(ic == ICH - 1 and pi == 2)))
                                inc(mms[-1], f"dfree{ic % 3}")
                                used(f"wdn{ic % 3}", f"dfree{ic % 3}")
                                if ic == ICH - 1:
                                    for si2 in range(len(htiles)):
                                        m_ = mms[si2 * 3 + 2]
                                        if si2 == len(htiles) - 1:
                                            pts.append((f"dfree{ic % 3}",
                                                        cnt[f"dfree{ic % 3}"]))
                                        else:
                                            pts.append(("t", inc(m_, "t")))
                            for si2, t in enumerate(htiles):
                                pd = ps[2 + si2]
                                xr = xres[si2 % 2]
                                lvs = f"ld_xres{si2 % 2}"
                                lv = load(xr, src_res(t)[:, dt * 512:(dt + 1) * 512],
                                          key=f"xres{si2 % 2}")
                                wait(A, pts[si2][0], pts[si2][1])
                                sl = si2
                                sgb = stg1[sl]
                                guard(A, f"stg{sl}")
                                ai = A.activation(out=sgb, in_=pd[:, :512],
                                                  func=AF.Copy, scale=0.5)
                                pa = inc(ai, "a")
                                used(f"bank{2 + si2}", "a")
                                wait(G, "a", pa)
                                wait(G, lvs, lv)
                                pg = inc(G.tensor_add(out=sgb, in0=sgb, in1=xr), "g")
                                used(f"xres{si2 % 2}", "g")
                                wait(A, "g", pg)
                                store(dst_of(t)[:, dt * 512:(dt + 1) * 512], sgb, sl)
                                used(f"stg{sl}", f"st{sl}")

                for sw in range(nsw):
                    stiles = stiles_of = tiles[sw * SWEEP:(sw + 1) * SWEEP]
                    if sw == 0:
                        emit_pre(stiles)
                    emit_gateup(stiles)
                    if sw + 1 < nsw:
                        emit_pre(tiles[(sw + 1) * SWEEP:(sw + 2) * SWEEP])
                    emit_down(stiles)
                wait_all_stores(SY)

            if "ffn1" in phases: emit_ffn(
                tiles=list(range(NT)),
                src_x=lambda t: d_x[t * 128:(t + 1) * 128, :],
                src_res=lambda t: d_x[t * 128:(t + 1) * 128, :],
                dst_of=lambda t: (d_x1ctx[t * 128:(t + 1) * 128, :] if t < OWN0 else
                                  d_x1own[(t - OWN0) * 128:(t - OWN0 + 1) * 128, :]),
            )

            def x1_src(t):
                return (d_x1ctx[t * 128:(t + 1) * 128, :] if t < OWN0 else
                        d_x1own[(t - OWN0) * 128:(t - OWN0 + 1) * 128, :])

            for nm_ in ("t", "dfree0", "dfree1", "dfree2"):
                wait(V, nm_)
            V.memset(a_M[0], 0.0)
            V.memset(a_MT[0], 0.0)
            V.drain()
            inc(V.engine_nop(), "v")
            mcur = {"i": 0}

            cs_state = {"n": 0}

            def cslot():
                bk = 2 + (cs_state["n"] % 2)
                cs_state["n"] += 1
                key = f"bank{bk}"
                pst = ps[bk][:, 0:128]
                guard(T, key)
                return pst, key

            def cmm(lhsT, rhs, dst, extra=None, n=128):
                pst, key = cslot()
                pm = inc(T.matmul(pst[:, 0:n], lhsT, rhs, start=True, stop=True), "csm")
                wait(V, "csm", pm)
                if extra is None:
                    V.tensor_copy(out=dst, in_=pst[:, 0:n])
                else:
                    extra(dst, pst[:, 0:n])
                V.drain()
                inc(V.engine_nop(), "v")
                used(key, "v")

            def emit_chunk(si, own, t_own):
                nx = 1 - mcur["i"]
                Mc, MTc = a_M[mcur["i"]], a_MT[mcur["i"]]
                Mn, MTn = a_M[nx], a_MT[nx]
                acol = a_acol[:, si:si + 1]
                bcol = a_bcol[:, si:si + 1]
                Kt = a_K[:, si, :]
                Vt = a_V[:, si, :]
                QTt = a_QT[:, si, :]
                GTt = a_GT[:, si, :]

                wait(A, "v", cnt["v"])
                ai = ai = ai = ai = ai = ai = ai = A.activation(out=a_la, in_=acol, func=AF.Ln)
                pa = inc(ai, "a")
                wait(T, "a", pa)
                cmm(c_ut, a_la, a_cum, n=1)
                wait(T, "v")
                pst, key = cslot()
                ptr = inc(T.transpose(pst[0:1, 0:128], a_cum, c_id), "csm")
                wait(V, "csm", ptr)
                V.tensor_copy(out=a_cumrow, in_=pst[0:1, 0:128])
                V.drain()
                pv = inc(V.engine_nop(), "v")
                used(key, "v")
                wait(T, "v", pv)
                cmm(c_ones, a_cumrow, a_cumrep)
                pv = cnt["v"]
                wait(A, "v", pv)
                A.activation(out=a_gam, in_=a_cum, func=AF.Exp)
                ai = A.activation(out=a_gamL, in_=a_cumrep[:, 127:128], func=AF.Exp)
                pa = inc(ai, "a")
                wait(V, "a", pa)
                V.tensor_sub(out=a_wgt, in0=a_cumrep[:, 127:128], in1=a_cum)
                V.drain()
                pv = inc(V.engine_nop(), "v")
                wait(A, "v", pv)
                ai = A.activation(out=a_wgt, in_=a_wgt, func=AF.Exp)
                pa = inc(ai, "a")
                wait(V, "a", pa)
                V.tensor_mul(out=a_wgt, in0=a_wgt, in1=bcol)
                V.tensor_scalar_mul(out=a_negb, in0=bcol, scalar1=-1.0)
                V.drain()
                V.tensor_scalar(out=a_t1, in0=a_cumrep, scalar1=a_cum, scalar2=None,
                                op0=AL.subtract)
                V.drain()
                V.tensor_add(out=a_t1, in0=a_t1, in1=c_aiu)
                V.drain()
                pv = inc(V.engine_nop(), "v")
                wait(A, "v", pv)
                ai = A.activation(out=a_gtincl, in_=a_t1, func=AF.Exp)
                if own:
                    ai = A.activation(out=a_gamrep, in_=a_cumrep, func=AF.Exp)
                pa = inc(ai, "a")
                wait(V, "a", pa)
                V.tensor_sub(out=a_t1, in0=a_gtincl, in1=c_id)
                V.drain()
                pv = inc(V.engine_nop(), "v")

                wait(T, "v", pv)
                transpose128(Kt, a_KT)
                if own:
                    transpose128(Vt, a_VT)
                wait(T, "tpe")
                cmm(a_KT, a_KT, a_kk)
                V.tensor_mul(out=s_nb[0], in0=a_t1, in1=a_kk)
                V.drain()
                V.tensor_scalar_mul(out=s_nb[0], in0=s_nb[0], scalar1=a_negb)
                V.tensor_scalar_mul(out=a_gk, in0=Kt, scalar1=a_gam)
                V.tensor_scalar_mul(out=a_kw, in0=Kt, scalar1=a_wgt)
                V.drain()
                pv = inc(V.engine_nop(), "v")
                wait(T, "v", pv)
                transpose128(s_nb[0], s_na[0])
                wait(V, "tpe")

                V.tensor_copy(out=a_Z, in_=a_kw)
                if own:
                    V.tensor_copy(out=a_Wm, in_=a_gk)
                V.drain()
                pv = inc(V.engine_nop(), "v")
                if own:
                    wait(T, "v", pv)
                    wait(T, "tpe")
                    cmm(a_VT, QTt, a_PF)
                cur = 0
                for m in range(7):
                    wait(T, "v")
                    cmm(s_na[cur], a_Z, a_Z,
                        extra=lambda d, p: V.tensor_add(out=d, in0=d, in1=p))
                    if own:
                        cmm(s_nb[cur], a_Wm, a_Wm,
                            extra=lambda d, p: V.tensor_add(out=d, in0=d, in1=p))
                        cmm(s_nb[cur], a_PF, a_PF,
                            extra=lambda d, p: V.tensor_add(out=d, in0=d, in1=p))
                    if m < 6:
                        wait(T, "v")
                        cmm(s_nb[cur], s_na[cur], s_na[1 - cur])
                        cmm(s_na[cur], s_nb[cur], s_nb[1 - cur])
                        cur = 1 - cur

                wait(T, "v")
                cmm(a_gk, a_Z, a_ET)
                cmm(a_Z, Vt, a_F)
                cmm(Vt, a_Z, a_FT)
                if own:
                    transpose128(a_Wm, a_WT)

                if own:
                    wait(T, "v")
                    cmm(MTc, QTt, a_MQ)
                    wait(T, "v")
                    wait(T, "tpe")
                    cmm(a_WT, a_MQ, a_BT,
                        extra=lambda d, p: V.scalar_tensor_tensor(
                            out=d, in0=p, scalar=-1.0, in1=a_PF,
                            op0=AL.mult, op1=AL.add))
                    V.tensor_mul(out=a_BT, in0=a_BT, in1=a_gtincl)
                    V.drain()
                    V.tensor_scalar_mul(out=a_BT, in0=a_BT, scalar1=bcol)
                    V.drain()
                    pv = inc(V.engine_nop(), "v")
                    wait(T, "v", pv)
                    cmm(Kt, a_BT, a_t2)
                    V.tensor_mul(out=a_t1, in0=a_MQ, in1=a_gamrep)
                    V.drain()
                    V.tensor_add(out=a_t1, in0=a_t1, in1=a_t2)
                    V.drain()
                    V.tensor_mul(out=a_OT, in0=a_t1, in1=GTt)
                    V.drain()
                    pv = inc(V.engine_nop(), "v")
                    wait(T, "v", pv)
                    for half in range(D // 512):
                        guard(T, "bank5")
                        mmO = T.matmul(ps[5][:, 0:512], a_OT,
                                       w_awo[:, half * 512:(half + 1) * 512],
                                       start=True, stop=True)
                        pt = inc(mmO, "t")
                        xr = xres[half % 2]
                        lvs = f"ld_xres{half % 2}"
                        lv = load(xr, d_x1own[t_own * 128:(t_own + 1) * 128,
                                              half * 512:(half + 1) * 512],
                                  key=f"xres{half % 2}")
                        wait(V, lvs, lv)
                        wait(V, "t", pt)
                        sgb = stg1[half]
                        guard(V, f"stg{half}")
                        V.tensor_add(out=sgb, in0=ps[5][:, 0:512], in1=xr)
                        V.drain()
                        V.tensor_add(out=sgb, in0=sgb,
                                     in1=b_o[:, half * 512:(half + 1) * 512])
                        V.drain()
                        pv = inc(V.engine_nop(), "v")
                        used("bank5", "v")
                        used(f"xres{half % 2}", "v")
                        wait(A, "v", pv)
                        store(d_x2own[t_own * 128:(t_own + 1) * 128,
                                      half * 512:(half + 1) * 512], sgb, half)
                        used(f"stg{half}", f"st{half}")

                wait(T, "v")
                cmm(a_ET, Mc, a_t1,
                    extra=lambda d, p: V.scalar_tensor_tensor(
                        out=d, in0=p, scalar=-1.0, in1=a_F, op0=AL.mult, op1=AL.add))
                cmm(Mc, a_ET, a_t2,
                    extra=lambda d, p: V.scalar_tensor_tensor(
                        out=d, in0=p, scalar=-1.0, in1=a_FT, op0=AL.mult, op1=AL.add))
                V.scalar_tensor_tensor(out=Mn, in0=Mc, scalar=a_gamL, in1=a_t1,
                                       op0=AL.mult, op1=AL.add)
                V.scalar_tensor_tensor(out=MTn, in0=MTc, scalar=a_gamL, in1=a_t2,
                                       op0=AL.mult, op1=AL.add)
                V.drain()
                inc(V.engine_nop(), "v")
                mcur["i"] = nx

            ngroups = NT // GROUP if "attn" in phases else 0
            for g in range(ngroups):
                gtiles = list(range(g * GROUP, (g + 1) * GROUP))
                own_g = gtiles[0] >= OWN0
                wait(A, "t")
                wait(A, "tpm")
                wait(V, "t")
                for si, t in enumerate(gtiles):
                    b = xin[si % 2]
                    lvs = f"ld_xin{si % 2}"
                    lv = load(b[:, 0:D], x1_src(t), key=f"xin{si % 2}")
                    wait(V, lvs, lv)
                    ln_tile(b, scr4, srckey=f"xin{si % 2}")
                    wait(T, "v")
                    for dc in range(DCH):
                        transpose128(scr4[:, dc * 128:(dc + 1) * 128],
                                     a_xh2T[:, dc, si * 128:(si + 1) * 128])
                    wait(V, "tpe")
                wait(T, "tpe")

                for si, t in enumerate(gtiles):
                    pp = ps[si % 2]
                    guard(T, f"bank{si % 2}")
                    for dc in range(DCH):
                        mm = T.matmul(pp[:, 0:258],
                                      a_xh2T[:, dc, si * 128:(si + 1) * 128],
                                      w_awkvab[:, dc, :], start=(dc == 0),
                                      stop=(dc == DCH - 1))
                    pt = inc(mm, "t")
                    wait(V, "t", pt)
                    V.tensor_add(out=scr1[:, 0:256], in0=pp[:, 0:256],
                                 in1=b_kvab[:, 0:256])
                    V.drain()
                    pv = inc(V.engine_nop(), "v")
                    wait(A, "v", pv)
                    A.activation(out=scr2[:, 0:128], in_=scr1[:, 0:128], func=AF.Square,
                                 accum_out=sm_s[:, 0:1])
                    A.drain()
                    A.activation(out=sm_s[:, 1:2], in_=sm_s[:, 0:1], func=AF.Sqrt)
                    A.activation(out=a_acol[:, si:si + 1], in_=pp[:, 256:257],
                                 func=AF.Sigmoid, bias=b_a)
                    ai = A.activation(out=a_bcol[:, si:si + 1], in_=pp[:, 257:258],
                                      func=AF.Sigmoid, bias=b_b)
                    pa = inc(ai, "a")
                    used(f"bank{si % 2}", "a")
                    wait(V, "a", pa)
                    V.tensor_scalar_max(out=sm_s[:, 1:2], in0=sm_s[:, 1:2],
                                        scalar1=1e-12)
                    V.drain()
                    pv = inc(V.engine_nop(), "v")
                    wait(A, "v", pv)
                    V.reciprocal(out=sm_r[:, 0:1], in_=sm_s[:, 1:2])
                    V.drain()
                    V.tensor_scalar_mul(out=a_K[:, si, :], in0=scr1[:, 0:128],
                                        scalar1=sm_r[:, 0:1])
                    V.tensor_copy(out=a_V[:, si, :], in_=scr1[:, 128:256])
                    V.drain()
                    inc(V.engine_nop(), "v")
                if own_g:
                    for si, t in enumerate(gtiles):
                        pp = ps[si % 2]
                        guard(T, f"bank{si % 2}")
                        for dc in range(DCH):
                            T.matmul(pp[:, 0:128], w_awq[:, dc, :],
                                     a_xh2T[:, dc, si * 128:(si + 1) * 128],
                                     start=(dc == 0), stop=(dc == DCH - 1))
                        for dc in range(DCH):
                            mm = T.matmul(pp[:, 128:256], w_awg[:, dc, :],
                                          a_xh2T[:, dc, si * 128:(si + 1) * 128],
                                          start=(dc == 0), stop=(dc == DCH - 1))
                        pt = inc(mm, "t")
                        wait(V, "t", pt)
                        V.tensor_scalar(out=a_QT[:, si, :], in0=pp[:, 0:128],
                                        scalar1=b_q, scalar2=None, op0=AL.add)
                        V.drain()
                        pv = inc(V.engine_nop(), "v")
                        wait(A, "v", pv)
                        ai = A.activation(out=a_GT[:, si, :], in_=pp[:, 128:256],
                                          func=AF.Sigmoid, bias=b_g)
                        pa = inc(ai, "a")
                        used(f"bank{si % 2}", "a")
                        wait(V, "a", pa)

                for si, t in enumerate(gtiles):
                    emit_chunk(si, own_g, (t - OWN0) if own_g else 0)

            wait_all_stores(SY)
            if "ffn2" in phases: emit_ffn(
                tiles=list(range(NOWN)),
                src_x=lambda t: d_x2own[t * 128:(t + 1) * 128, :],
                src_res=lambda t: d_x2own[t * 128:(t + 1) * 128, :],
                dst_of=lambda t: d_out[t * 128:(t + 1) * 128, :],
            )

            if "ffn2" not in phases:
                src_dump = d_x2own if "attn" in phases else d_x1own
                for t in range(0, NOWN * 128, 128):
                    load(xin[0][:, 0:D], src_dump[t:t + 128, :],
                         s="dbg_ld", key="xin0")
                    wait(A, "dbg_ld")
                    store(d_out[t:t + 128, :], xin[0][:, 0:D], 0)
                    used("xin0", "st0")
                    wait(SY, "st0")
            wait_all_stores(SY)

    return nc


def make_host_inputs(cfg, ins):
    NT, OWN0, D, I = cfg["NTILES"], cfg["OWN0"], cfg["D"], cfg["I"]
    NOWN = NT - OWN0
    x = np.ascontiguousarray(np.asarray(ins["L0_attn_x"], dtype=np.float32))
    B = x.shape[0]
    Wk = np.asarray(ins["L0_attn_Wk"], np.float32)
    Wv = np.asarray(ins["L0_attn_Wv"], np.float32)
    Wa = np.asarray(ins["L0_attn_Wa"], np.float32)
    Wb = np.asarray(ins["L0_attn_Wb"], np.float32)
    wkvab = np.ascontiguousarray(np.concatenate([Wk, Wv, Wa, Wb], axis=1))
    bk = np.asarray(ins["L0_attn_bk"], np.float32)
    bv = np.asarray(ins["L0_attn_bv"], np.float32)
    bkvab = np.ascontiguousarray(
        np.broadcast_to(np.concatenate([bk, bv, [0.0, 0.0]]).astype(np.float32)[None, :],
                        (128, 258)))
    common = {
        "wg": np.ascontiguousarray(np.asarray(ins["L0_ffn_w_gate"], np.float32)),
        "wu": np.ascontiguousarray(np.asarray(ins["L0_ffn_w_up"], np.float32)),
        "wd": np.ascontiguousarray(np.asarray(ins["L0_ffn_w_down"], np.float32)),
        "awq": np.ascontiguousarray(np.asarray(ins["L0_attn_Wq"], np.float32)),
        "awg": np.ascontiguousarray(np.asarray(ins["L0_attn_Wg"], np.float32)),
        "awkvab": wkvab,
        "awo": np.ascontiguousarray(np.asarray(ins["L0_attn_Wo"], np.float32)),
        "bkvab": bkvab,
        "bqcol": np.ascontiguousarray(np.asarray(ins["L0_attn_bq"], np.float32)[:, None]),
        "bgcol": np.ascontiguousarray(np.asarray(ins["L0_attn_bg"], np.float32)[:, None]),
        "bacol": np.ascontiguousarray(
            np.broadcast_to(np.asarray(ins["L0_attn_ba"], np.float32), (128,))[:, None]),
        "bbcol": np.ascontiguousarray(
            np.broadcast_to(np.asarray(ins["L0_attn_bb"], np.float32), (128,))[:, None]),
        "borep": np.ascontiguousarray(
            np.broadcast_to(np.asarray(ins["L0_attn_bo"], np.float32)[None, :], (128, D))),
        "c_ut": np.triu(np.ones((128, 128), np.float32)),
        "c_id": np.eye(128, dtype=np.float32),
        "c_idb": np.eye(128, dtype=np.float32).astype(ml_dtypes.bfloat16),
        "c_aiu": ((np.triu(np.ones((128, 128), np.float32)) - 1.0) * 1e9).astype(np.float32),
        "c_ones1": np.ones((1, 128), np.float32),
    }
    half = OWN0 * 128
    in_maps = []
    for c in range(2 * B):
        bi, hi = c // 2, c % 2
        xa = np.zeros((NT * 128, D), np.float32)
        if hi == 0:
            xa[half:half + NOWN * 128] = x[bi, 0:NOWN * 128]
        else:
            xa[0:half] = x[bi, 0:half]
            xa[half:half + NOWN * 128] = x[bi, half:half + NOWN * 128]
        in_maps.append(dict(common, xall=np.ascontiguousarray(xa)))
    return in_maps


def assemble(cfg, results, B, T):
    NOWN = cfg["NTILES"] - cfg["OWN0"]
    D = cfg["D"]
    out = np.zeros((B, T, D), np.float32)
    half = cfg["OWN0"] * 128
    for c, r in enumerate(results):
        bi, hi = c // 2, c % 2
        if hi == 0:
            out[bi, 0:NOWN * 128] = r["yout"]
        else:
            out[bi, half:half + NOWN * 128] = r["yout"]
    return out


from concourse.bass_utils import run_bass_kernel_spmd

_NC_CACHE = {}


def kernel(**inputs):
    cfg = FULL
    if "nc" not in _NC_CACHE:
        _NC_CACHE["nc"] = build_nc(cfg)
    nc = _NC_CACHE["nc"]
    in_maps = make_host_inputs(cfg, inputs)
    res = run_bass_kernel_spmd(nc, in_maps, core_ids=list(range(8)))
    x = np.asarray(inputs["L0_attn_x"])
    return assemble(cfg, res.results, x.shape[0], x.shape[1])


# version 16
# speedup vs baseline: 1.3136x; 1.0224x over previous
import contextlib
import numpy as np
import ml_dtypes
import concourse.bass as bass
import concourse.mybir as mybir

F32 = mybir.dt.float32
BF16 = mybir.dt.bfloat16
I32 = mybir.dt.int32
AF = mybir.ActivationFunctionType
AL = mybir.AluOpType
AX = mybir.AxisListType

FULL = dict(NTILES=32, OWN0=16, D=1024, I=2816, ITILE=256, SWEEP=4, GROUP=2)
TINY = dict(NTILES=2, OWN0=1, D=1024, I=2816, ITILE=256, SWEEP=2, GROUP=1)
MID = dict(NTILES=8, OWN0=4, D=1024, I=2816, ITILE=256, SWEEP=4, GROUP=2)


def build_nc(cfg, phases=("prep", "ffn1", "attn", "ffn2")):
    NT, OWN0, D, I = cfg["NTILES"], cfg["OWN0"], cfg["D"], cfg["I"]
    ITILE, SWEEP, GROUP = cfg["ITILE"], cfg["SWEEP"], cfg["GROUP"]
    DCH, ICH = D // 128, I // 128
    NIT = I // ITILE
    NOWN = NT - OWN0
    IBLK = ITILE // 128
    SCRW = D
    SLOTW = SWEEP * ITILE
    QW = max(2 * D, 2 * SLOTW)

    nc = bass.Bass("TRN2", target_bir_lowering=False, debug=False)

    d_x = nc.dram_tensor("xall", [NT * 128, D], F32, kind="ExternalInput").ap()
    d_out = nc.dram_tensor("yout", [NOWN * 128, D], F32, kind="ExternalOutput").ap()
    d_wg = nc.dram_tensor("wg", [I, D], F32, kind="ExternalInput").ap()
    d_wu = nc.dram_tensor("wu", [I, D], F32, kind="ExternalInput").ap()
    d_wd = nc.dram_tensor("wd", [D, I], F32, kind="ExternalInput").ap()
    d_awq = nc.dram_tensor("awq", [D, 128], F32, kind="ExternalInput").ap()
    d_awg = nc.dram_tensor("awg", [D, 128], F32, kind="ExternalInput").ap()
    d_awkvab = nc.dram_tensor("awkvab", [D, 258], F32, kind="ExternalInput").ap()
    d_awo = nc.dram_tensor("awo", [128, D], F32, kind="ExternalInput").ap()
    d_bkvab = nc.dram_tensor("bkvab", [128, 258], F32, kind="ExternalInput").ap()
    d_bq = nc.dram_tensor("bqcol", [128, 1], F32, kind="ExternalInput").ap()
    d_bg = nc.dram_tensor("bgcol", [128, 1], F32, kind="ExternalInput").ap()
    d_ba = nc.dram_tensor("bacol", [128, 1], F32, kind="ExternalInput").ap()
    d_bb = nc.dram_tensor("bbcol", [128, 1], F32, kind="ExternalInput").ap()
    d_bo = nc.dram_tensor("borep", [128, D], F32, kind="ExternalInput").ap()
    d_cut = nc.dram_tensor("c_ut", [128, 128], F32, kind="ExternalInput").ap()
    d_cid = nc.dram_tensor("c_id", [128, 128], F32, kind="ExternalInput").ap()
    d_cidb = nc.dram_tensor("c_idb", [128, 128], BF16, kind="ExternalInput").ap()
    d_caiu = nc.dram_tensor("c_aiu", [128, 128], F32, kind="ExternalInput").ap()
    d_cones = nc.dram_tensor("c_ones1", [1, 128], F32, kind="ExternalInput").ap()

    d_x1ctx = nc.dram_tensor("x1ctx", [max(OWN0, 1) * 128, D], F32, kind="Internal").ap()
    d_x1own = nc.dram_tensor("x1own", [NOWN * 128, D], F32, kind="Internal").ap()
    d_x2own = nc.dram_tensor("x2own", [NOWN * 128, D], F32, kind="Internal").ap()
    d_stg = [nc.dram_tensor(f"stash_g{h}", [DCH, ICH, 128, 128], BF16,
                            kind="Internal").ap() for h in range(2)]
    d_stu = [nc.dram_tensor(f"stash_u{h}", [DCH, ICH, 128, 128], BF16,
                            kind="Internal").ap() for h in range(2)]
    d_std = [nc.dram_tensor(f"stash_d{h}", [ICH, DCH, 128, 128], BF16,
                            kind="Internal").ap() for h in range(2)]

    es = contextlib.ExitStack()
    with es:
        def sb(name, shape, dt=F32):
            return es.enter_context(nc.sbuf_tensor(name, shape, dt)).ap()

        c_ut = sb("c_ut_s", [128, 128])
        c_id = sb("c_id_s", [128, 128])
        c_aiu = sb("c_aiu_s", [128, 128])
        c_ones = sb("c_ones_s", [1, 128])
        w_awq = sb("w_awq", [128, DCH, 128])
        w_awg = sb("w_awg", [128, DCH, 128])
        w_awkvab = sb("w_awkvab", [128, DCH, 258])
        w_awo = sb("w_awo", [128, D])
        b_kvab = sb("b_kvab_s", [128, 258])
        b_q = sb("b_q_s", [128, 1])
        b_g = sb("b_g_s", [128, 1])
        b_a = sb("b_a_s", [128, 1])
        b_b = sb("b_b_s", [128, 1])
        b_o = sb("b_o_s", [128, D])
        c_eps = sb("c_eps", [128, 1])

        xin = [sb("xin0", [128, QW]), sb("xin1", [128, SCRW])]
        scr1 = sb("scr1", [128, QW])
        scr2 = sb("scr2", [128, QW])
        scr3 = sb("scr3", [128, QW])
        scr4 = sb("scr4", [128, SCRW])
        sm_m = sb("sm_m", [128, QW // 16])
        sm_s = sb("sm_s", [128, QW // 16])
        sm_r = sb("sm_r", [128, QW // 16])
        xqT = [[sb(f"xqT{t}_{h}", [128, DCH, 128], BF16) for h in range(2)]
               for t in range(SWEEP)]
        wcol = [[sb(f"wcol{i}_{h}", [128, 2, DCH, ITILE], BF16) for h in range(2)]
                for i in range(2)]
        c_idb = sb("c_idb_s", [128, 128], BF16)
        sp_h = sb("sp_h", [128, QW], BF16)
        sp_l = sb("sp_l", [128, QW], BF16)
        hb_all = sb("hb_all", [128, 4 * SLOTW])
        hb = [hb_all[:, i * SLOTW:(i + 1) * SLOTW] for i in range(4)]
        hsilu = [sb(f"hsilu{i}", [128, ITILE]) for i in range(2)]
        hq_m, hq_s, hq_r = sm_m, sm_s, sm_r
        hqT = [[sb(f"hqT{t}_{h}", [128, ICH, 128], BF16) for h in range(2)]
               for t in range(SWEEP)]
        wdn = [[sb(f"wdn{i}_{h}", [128, 512], BF16) for h in range(2)]
               for i in range(3)]
        xres = [sb(f"xres{i}", [128, 512]) for i in range(2)]
        stg1 = [sb(f"stg1_{i}", [128, 512]) for i in range(4)]

        a_xh2T = hb_all[:, 0:DCH * GROUP * 128].rearrange(
            "p (d n) -> p d n", n=GROUP * 128)
        hpool = [hqT[si][h].bitcast(F32).rearrange("p a n -> p (a n)")
                 for si in (0, 1) for h in (0, 1)]
        hp_off = [0, 0, 0, 0]

        def carve(pi, w):
            c0 = hp_off[pi]
            hp_off[pi] = c0 + w
            v = hpool[pi][:, c0:c0 + w]
            assert hp_off[pi] <= ICH * 64, "hqT pool overflow"
            return v

        a_K = carve(0, GROUP * 128).rearrange("p (g n) -> p g n", n=128)
        a_V = carve(0, GROUP * 128).rearrange("p (g n) -> p g n", n=128)
        a_QT = carve(0, GROUP * 128).rearrange("p (g n) -> p g n", n=128)
        a_GT = carve(0, GROUP * 128).rearrange("p (g n) -> p g n", n=128)
        a_KT = carve(0, 128)
        a_VT = carve(0, 128)
        a_acol = sb("a_acol", [128, GROUP])
        a_bcol = sb("a_bcol", [128, GROUP])
        a_la = sb("a_la", [128, 1])
        a_cum = sb("a_cum", [128, 1])
        a_cumrow = sb("a_cumrow", [1, 128])
        a_gam = sb("a_gam", [128, 1])
        a_gamL = sb("a_gamL", [128, 1])
        a_wgt = sb("a_wgt", [128, 1])
        a_negb = sb("a_negb", [128, 1])
        a_cumrep = carve(1, 128)
        a_gamrep = carve(1, 128)
        a_gtincl = carve(1, 128)
        a_kk = carve(1, 128)
        s_na = [carve(1, 128) for i in range(2)]
        s_nb = [carve(1, 128) for i in range(2)]
        a_gk = carve(1, 128)
        a_kw = carve(1, 128)
        a_Z = carve(1, 128)
        a_Wm = carve(2, 128)
        a_PF = carve(2, 128)
        a_WT = carve(2, 128)
        a_ET = carve(2, 128)
        a_F = carve(2, 128)
        a_FT = carve(2, 128)
        a_M = [carve(2, 128) for i in range(2)]
        a_MT = [carve(2, 128) for i in range(2)]
        a_MQ = carve(3, 128)
        a_BT = carve(3, 128)
        a_OT = carve(3, 128)
        a_t1 = carve(3, 128)
        a_t2 = carve(3, 128)

        ps = [es.enter_context(nc.psum_tensor(f"ps{i}", [128, 512], F32)).ap()
              for i in range(8)]

        sems, cnt = {}, {}

        def sem(name):
            if name not in sems:
                sems[name] = es.enter_context(nc.semaphore(name))
                cnt[name] = 0
            return sems[name]

        def inc(instr, name, amt=1):
            s = sem(name)
            cnt[name] += amt
            instr.then_inc(s, amt)
            return cnt[name]

        def wait(eng, name, val=None):
            if name not in cnt and val is None:
                return
            s = sem(name)
            v = cnt[name] if val is None else val
            if v > 0:
                eng.wait_ge(s, v)

        V, A, T, G, SY = nc.vector, nc.scalar, nc.tensor, nc.gpsimd, nc.sync

        lastuse = {}

        def guard(eng, key):
            if key in lastuse:
                sname, val = lastuse[key]
                wait(eng, sname, val)

        def used(key, sname):
            sem(sname)
            lastuse[key] = (sname, cnt[sname])

        def load(out, in_, s=None, key=None):
            if key is not None:
                guard(SY, key)
            if s is None:
                s = "ld_" + (key if key is not None else "c")
            return inc(SY.dma_start(out=out, in_=in_), s, 16)

        def store(out, in_, slot):
            return inc(A.dma_start(out=out, in_=in_), f"st{slot}", 16)

        def wait_all_stores(eng):
            for i in range(4):
                wait(eng, f"st{i}")

        with nc.Block():
            load(c_ut, d_cut); load(c_id, d_cid); load(c_aiu, d_caiu)
            load(c_idb, d_cidb)
            load(c_ones, d_cones)
            load(w_awq, d_awq.rearrange("(c p) n -> p c n", p=128))
            load(w_awg, d_awg.rearrange("(c p) n -> p c n", p=128))
            load(w_awkvab, d_awkvab.rearrange("(c p) n -> p c n", p=128))
            load(w_awo, d_awo)
            load(b_kvab, d_bkvab); load(b_q, d_bq); load(b_g, d_bg)
            load(b_a, d_ba); load(b_b, d_bb); load(b_o, d_bo)
            CONST_LD = cnt["ld_c"]
            for eng in (V, A, T):
                wait(eng, "ld_c", CONST_LD)
            V.memset(c_eps, 1e-5)
            V.drain()

            SH_MUL = float(np.float32(1.0) / np.float32(6.0)) * 0.5
            SH_MIN = float(np.float32(1e-12)) * 0.5

            def quant(src, out, F, mb, ssb, rb, t1, t2, t3, srckey=None):
                nb = F // 16
                v3 = src.rearrange("p (b k) -> p b k", k=16) if src.shape[1] == F \
                    else src
                V.tensor_reduce(out=mb[:, :nb], in_=v3, axis=AX.X, op=AL.max,
                                apply_absolute_value=True)
                V.tensor_tensor(out=t1[:, :F], in0=src, in1=src, op=AL.mult)
                V.drain()
                V.tensor_scalar(out=ssb[:, :nb], in0=mb[:, :nb], scalar1=SH_MUL,
                                scalar2=SH_MIN, op0=AL.mult, op1=AL.max)
                V.drain()
                V.reciprocal(out=rb[:, :nb], in_=ssb[:, :nb])
                V.drain()
                V.tensor_tensor(out=mb[:, :nb], in0=rb[:, :nb], in1=rb[:, :nb],
                                op=AL.mult)
                V.drain()
                mb3 = mb[:, :nb].broadcast_to([128, nb, 16])
                t1v = t1[:, :F].rearrange("p (b k) -> p b k", k=16)
                V.tensor_tensor(out=t1v, in0=t1v, in1=mb3, op=AL.mult)
                V.drain()
                c1 = t3.bitcast(BF16)[:, 0:F]
                c2 = t3.bitcast(BF16)[:, F:2 * F]
                c3 = t2.bitcast(BF16)[:, 0:F]
                c4 = t2.bitcast(BF16)[:, F:2 * F]

                def cmp(dst, thr2, w):
                    V.tensor_scalar(out=dst, in0=t1[:, :F], scalar1=thr2,
                                    scalar2=w, op0=AL.is_gt, op1=AL.mult)

                guard(V, "scr3g")
                cmp(c1, 0.25, 1.0)
                guard(V, "scr2g")
                cmp(c3, 2.25, 1.0)
                cmp(c4, 6.25, 1.0)
                cmp(c2, 12.25, 1.0)
                V.drain()
                V.tensor_add(out=c1, in0=c1, in1=c3)
                V.tensor_add(out=c4, in0=c4, in1=c2)
                V.drain()
                cmp(c3, 25.0, 2.0)
                cmp(c2, 49.0, 2.0)
                V.drain()
                V.tensor_add(out=c1, in0=c1, in1=c4)
                V.tensor_add(out=c3, in0=c3, in1=c2)
                V.drain()
                cmp(c2, 100.0, 4.0)
                V.tensor_add(out=c1, in0=c1, in1=c3)
                V.drain()
                V.tensor_add(out=c1, in0=c1, in1=c2)
                V.drain()
                sh3 = ssb[:, :nb].broadcast_to([128, nb, 16])
                t2v = t2[:, :F].rearrange("p (b k) -> p b k", k=16)
                mm = V.tensor_tensor(out=t2v,
                                     in0=c1.rearrange("p (b k) -> p b k", k=16),
                                     in1=sh3, op=AL.mult)
                pv = inc(mm, "v")
                sgb = c2
                wait(A, "v", pv)
                wait(A, "tpm")
                A.copy(out=sp_h[:, :F], in_=t2[:, :F])
                ai = A.activation(out=sgb, in_=src, func=AF.Sign)
                pa = inc(ai, "a")
                if srckey is not None:
                    used(srckey, "a")
                wait(G, "a", pa)
                G.tensor_tensor(out=sp_l[:, :F], in0=t2[:, :F], in1=sp_h[:, :F],
                                op=AL.subtract)
                G.drain()
                G.tensor_mul(out=sp_h[:, :F], in0=sp_h[:, :F], in1=sgb)
                gi = G.tensor_mul(out=sp_l[:, :F], in0=sp_l[:, :F], in1=sgb)
                inc(gi, "gq")
                used("scr2g", "gq")
                used("scr3g", "gq")
                return pv

            tp_state = {"n": 0}

            def transpose128(src_ap, dst_ap, ident=None):
                bk = 6 + (tp_state["n"] % 2)
                tp_state["n"] += 1
                key = f"bank{bk}"
                if ident is None:
                    pst = ps[bk][:, 0:128]
                else:
                    pst = ps[bk].bitcast(BF16)[:, 0:128]
                guard(T, key)
                mm = T.transpose(pst, src_ap, c_id if ident is None else ident)
                pt = inc(mm, "tpm")
                wait(A, "tpm", pt)
                inc(A.copy(out=dst_ap, in_=pst), "tpe")
                used(key, "tpe")
                return mm

            def transpose_hl(colslice, dsth, dstl):
                transpose128(sp_h[:, colslice], dsth, ident=c_idb)
                return transpose128(sp_l[:, colslice], dstl, ident=c_idb)

            def transpose_batch(pairs):
                assert len(pairs) <= 8
                bk = 6 + (tp_state["n"] % 2)
                tp_state["n"] += 1
                key = f"bank{bk}"
                pst = ps[bk].bitcast(BF16)
                guard(T, key)
                for i, (src, _) in enumerate(pairs):
                    mm = T.transpose(pst[:, i * 128:(i + 1) * 128], src, c_idb)
                pt = inc(mm, "tpm")
                wait(A, "tpm", pt)
                for i, (_, dst) in enumerate(pairs):
                    ai = A.copy(out=dst, in_=pst[:, i * 128:(i + 1) * 128])
                inc(ai, "tpe")
                used(key, "tpe")

            def transpose_batch_wide(srcs, dst):
                assert len(srcs) <= 8
                bk = 6 + (tp_state["n"] % 2)
                tp_state["n"] += 1
                key = f"bank{bk}"
                pst = ps[bk].bitcast(BF16)
                guard(T, key)
                for i, src in enumerate(srcs):
                    mm = T.transpose(pst[:, i * 128:(i + 1) * 128], src, c_idb)
                pt = inc(mm, "tpm")
                wait(A, "tpm", pt)
                inc(A.copy(out=dst, in_=pst[:, 0:len(srcs) * 128]), "tpe")
                used(key, "tpe")

            def ln_tile(src, out, srckey=None):
                V.tensor_reduce(out=sm_m[:, 0:1], in_=src[:, 0:D], axis=AX.X, op=AL.add)
                V.drain()
                V.tensor_scalar_mul(out=sm_m[:, 0:1], in0=sm_m[:, 0:1], scalar1=1.0 / D)
                V.drain()
                guard(V, "scr1sign")
                vi = V.tensor_scalar(out=scr1[:, 0:D], in0=src[:, 0:D],
                                     scalar1=sm_m[:, 0:1],
                                     scalar2=None, op0=AL.subtract)
                pv = inc(vi, "v")
                V.drain()
                if srckey is not None:
                    used(srckey, "v")
                wait(A, "v", pv)
                guard(A, "scr2g")
                A.activation(out=scr2[:, 0:D], in_=scr1[:, 0:D], func=AF.Square,
                             accum_out=sm_s[:, 0:1])
                A.drain()
                ai = A.activation(out=sm_r[:, 0:1], in_=sm_s[:, 0:1], func=AF.Sqrt,
                                  scale=1.0 / D, bias=c_eps)
                pa = inc(ai, "a")
                wait(V, "a", pa)
                V.reciprocal(out=sm_r[:, 1:2], in_=sm_r[:, 0:1])
                V.drain()
                vi = V.tensor_scalar_mul(out=out[:, 0:D], in0=scr1[:, 0:D],
                                         scalar1=sm_r[:, 1:2])
                pv = inc(vi, "v")
                V.drain()
                return pv

            stgb = [hqT[-1][h].rearrange("p a n -> p (a n)")[:, 0:1024]
                    for h in range(2)]

            def prep_group(segs, rt2):
                b = xin[0]
                lvs = "ld_xin0"
                off = 0
                for dram_ap, Fq, _ in segs:
                    lv = load(b[:, off:off + Fq], dram_ap, key="xin0")
                    off += Fq
                Ftot = off
                wait(V, lvs, lv)
                quant(b[:, :Ftot], b[:, :Ftot], Ftot, sm_m, sm_s, sm_r,
                      scr1, scr2, scr3, srckey="xin0")
                wait(T, "gq")
                off = 0
                for _, Fq, wdst in segs:
                    nbk = Fq // 128
                    for h in range(2):
                        sp = sp_h if h == 0 else sp_l
                        for p0 in range(0, nbk, 8):
                            pw = min(8, nbk - p0)
                            sgb = stgb[h]
                            guard(A, f"stg{h}")
                            transpose_batch_wide(
                                [sp[:, off + (p0 + j) * 128:off + (p0 + j + 1) * 128]
                                 for j in range(pw)],
                                sgb[:, 0:pw * 128])
                            wait(A, "tpe")
                            store(wdst(h, p0, pw),
                                  sgb[:, 0:pw * 128]
                                  .rearrange("p (b n) -> p b n", n=128), h)
                            used(f"stg{h}", f"st{h}")
                    off += Fq

            rt2 = 0
            if "prep" in phases:
                groups = []
                for stash, wsrc in ((d_stg, d_wg), (d_stu, d_wu)):
                    for r0 in range(0, ICH, 2):
                        groups.append([
                            (wsrc[rt * 128:(rt + 1) * 128, :], D,
                             lambda h, p0, pw, rt=rt, stash=stash:
                             stash[h][p0:p0 + pw, rt].rearrange("b p n -> p b n"))
                            for rt in (r0, r0 + 1) if rt < ICH])
                for rt in range(DCH):
                    c0 = 0
                    while c0 < I:
                        cw = min(2 * D, I - c0)
                        groups.append([
                            (d_wd[rt * 128:(rt + 1) * 128, c0:c0 + cw], cw,
                             lambda h, p0, pw, rt=rt, c0=c0:
                             d_std[h][c0 // 128 + p0:c0 // 128 + p0 + pw, rt]
                             .rearrange("b p n -> p b n"))])
                        c0 += cw
                for g in groups:
                    prep_group(g, rt2)
                    rt2 += 1
            wait_all_stores(SY)
            wait_all_stores(A)

            def emit_ffn(tiles, src_x, src_res, dst_of):
                nsw = (len(tiles) + SWEEP - 1) // SWEEP

                def emit_pre(stiles):
                    wait(A, "t")
                    for si, t in enumerate(stiles):
                        b = xin[si % 2]
                        lvs = f"ld_xin{si % 2}"
                        lv = load(b[:, 0:D], src_x(t), key=f"xin{si % 2}")
                        wait(V, lvs, lv)
                        ln_tile(b, scr4, srckey=f"xin{si % 2}")
                        quant(scr4[:, 0:D], scr4[:, 0:D], D, sm_m, sm_s, sm_r,
                              scr1, scr2, scr3)
                        wait(T, "gq")
                        transpose_batch_wide(
                            [sp_h[:, dc * 128:(dc + 1) * 128] for dc in range(DCH)],
                            xqT[si][0].rearrange("p a n -> p (a n)"))
                        transpose_batch_wide(
                            [sp_l[:, dc * 128:(dc + 1) * 128] for dc in range(DCH)],
                            xqT[si][1].rearrange("p a n -> p (a n)"))
                        wait(V, "tpe")
                    wait(T, "tpe")

                def emit_gateup(stiles):
                    ns = len(stiles)
                    pending_it = []

                    def flush_items(items):
                        hs0 = items[0][1]
                        F0 = sum(nsX * ITILE for _, _, nsX in items)
                        src = hb_all[:, hs0 * SLOTW:hs0 * SLOTW + F0]
                        wait(V, "g")
                        wait(A, "g")
                        quant(src, src, F0, hq_m, hq_s, hq_r, scr1, scr2, scr3)
                        wait(T, "gq")
                        pairs_h, pairs_l = [], []
                        for k, (itX, hsX, nsX) in enumerate(items):
                            for si0 in range(nsX):
                                for bi in range(IBLK):
                                    col = k * SLOTW + si0 * ITILE + bi * 128
                                    pairs_h.append((sp_h[:, col:col + 128],
                                                    hqT[si0][0][:, itX * IBLK + bi, :]))
                                    pairs_l.append((sp_l[:, col:col + 128],
                                                    hqT[si0][1][:, itX * IBLK + bi, :]))
                        for pp_ in (pairs_h, pairs_l):
                            for k0 in range(0, len(pp_), 8):
                                transpose_batch(pp_[k0:k0 + 8])
                        for _, hsX, _ in items:
                            used(f"hb{hsX}", "tpm")

                    def load_wcol(slot, it0):
                        for mi, stash in enumerate((d_stg, d_stu)):
                            for h in range(2):
                                v5 = wcol[slot][h].rearrange(
                                    "p m d (b n) -> p m d b n", b=IBLK)
                                for bb in range(IBLK):
                                    load(v5[:, mi, :, bb, :],
                                         stash[h][:, it0 * IBLK + bb]
                                         .rearrange("d p n -> p d n"),
                                         s=f"wld{slot}", key=f"wcol{slot}")

                    for it in range(NIT):
                        if it == 0:
                            load_wcol(0, 0)
                        if it + 1 < NIT:
                            load_wcol((it + 1) % 2, it + 1)
                        wb = wcol[it % 2]
                        wait(T, f"wld{it % 2}")
                        hs = it % 4
                        for si in range(ns):
                            pp = ps[si % 2]
                            guard(T, f"bank{si % 2}")
                            passes = ((0, 0), (0, 1), (1, 0))
                            np_ = len(passes)
                            for mi2 in range(2):
                                reg = pp[:, mi2 * ITILE:(mi2 + 1) * ITILE]
                                k = 0
                                for xh_, wh_ in passes:
                                    for dc in range(DCH):
                                        mm = T.matmul(
                                            reg, xqT[si][xh_][:, dc, :],
                                            wb[wh_][:, mi2, dc, :],
                                            start=(k == 0),
                                            stop=(k == np_ * DCH - 1))
                                        k += 1
                            pt = inc(mm, "t")
                            if si == ns - 1:
                                used(f"wcol{it % 2}", "t")
                            wait(A, "t", pt)
                            hp = si % 2
                            guard(A, f"hsilu{hp}")
                            A.activation(out=hsilu[hp], in_=pp[:, 0:ITILE],
                                         func=AF.Silu)
                            hu = xres[hp]
                            guard(A, f"xres{hp}")
                            ai = A.copy(out=hu[:, 0:ITILE], in_=pp[:, ITILE:2 * ITILE])
                            pa = inc(ai, "a")
                            used(f"bank{si % 2}", "a")
                            wait(G, "a", pa)
                            if si == 0:
                                guard(G, f"hb{hs}")
                            gi = G.tensor_mul(out=hb[hs][:, si * ITILE:(si + 1) * ITILE],
                                              in0=hsilu[hp], in1=hu[:, 0:ITILE])
                            pg = inc(gi, "g")
                            used(f"xres{hp}", "g")
                            used(f"hsilu{hp}", "g")
                        pending_it.append((it, hs, ns))
                        if len(pending_it) >= 4:
                            flush_items([pending_it.pop(0), pending_it.pop(0)])
                    while pending_it:
                        if (len(pending_it) >= 2 and pending_it[0][1] % 2 == 0
                                and pending_it[1][1] == pending_it[0][1] + 1):
                            flush_items([pending_it.pop(0), pending_it.pop(0)])
                        else:
                            flush_items([pending_it.pop(0)])
                    wait(T, "tpe")

                def emit_down(stiles):
                    ns = len(stiles)
                    if True:
                        htiles = stiles
                        for dt in range(D // 512):
                            pts = []

                            def load_wdn(ic0):
                                sl = ic0 % 3
                                for h in range(2):
                                    load(wdn[sl][h].rearrange("p (b n) -> p b n", b=4),
                                         d_std[h][ic0, dt * 4:(dt + 1) * 4]
                                         .rearrange("b p n -> p b n"),
                                         s=f"dld{sl}", key=f"wdn{sl}")

                            load_wdn(0)
                            load_wdn(1)
                            for ic in range(ICH):
                                wbD = wdn[ic % 3]
                                lv = cnt[f"dld{ic % 3}"]
                                if ic + 2 < ICH:
                                    load_wdn(ic + 2)
                                wait(T, f"dld{ic % 3}", lv)
                                mms = []
                                for si2 in range(len(htiles)):
                                    si = si2
                                    pd = ps[2 + si2]
                                    if ic == 0:
                                        guard(T, f"bank{2 + si2}")
                                    for pi, (xh_, wh_) in enumerate(
                                            ((0, 0), (0, 1), (1, 0))):
                                        mms.append(T.matmul(
                                            pd[:, :512], hqT[si][xh_][:, ic, :],
                                            wbD[wh_], start=(ic == 0 and pi == 0),
                                            stop=(ic == ICH - 1 and pi == 2)))
                                inc(mms[-1], f"dfree{ic % 3}")
                                used(f"wdn{ic % 3}", f"dfree{ic % 3}")
                                if ic == ICH - 1:
                                    for si2 in range(len(htiles)):
                                        m_ = mms[si2 * 3 + 2]
                                        if si2 == len(htiles) - 1:
                                            pts.append((f"dfree{ic % 3}",
                                                        cnt[f"dfree{ic % 3}"]))
                                        else:
                                            pts.append(("t", inc(m_, "t")))
                            for si2, t in enumerate(htiles):
                                pd = ps[2 + si2]
                                xr = xres[si2 % 2]
                                lvs = f"ld_xres{si2 % 2}"
                                lv = load(xr, src_res(t)[:, dt * 512:(dt + 1) * 512],
                                          key=f"xres{si2 % 2}")
                                wait(A, pts[si2][0], pts[si2][1])
                                sl = si2
                                sgb = stg1[sl]
                                guard(A, f"stg{sl}")
                                ai = A.activation(out=sgb, in_=pd[:, :512],
                                                  func=AF.Copy, scale=0.5)
                                pa = inc(ai, "a")
                                used(f"bank{2 + si2}", "a")
                                wait(G, "a", pa)
                                wait(G, lvs, lv)
                                pg = inc(G.tensor_add(out=sgb, in0=sgb, in1=xr), "g")
                                used(f"xres{si2 % 2}", "g")
                                wait(A, "g", pg)
                                store(dst_of(t)[:, dt * 512:(dt + 1) * 512], sgb, sl)
                                used(f"stg{sl}", f"st{sl}")

                for sw in range(nsw):
                    stiles = stiles_of = tiles[sw * SWEEP:(sw + 1) * SWEEP]
                    if sw == 0:
                        emit_pre(stiles)
                    emit_gateup(stiles)
                    if sw + 1 < nsw:
                        emit_pre(tiles[(sw + 1) * SWEEP:(sw + 2) * SWEEP])
                    emit_down(stiles)
                wait_all_stores(SY)

            if "ffn1" in phases: emit_ffn(
                tiles=list(range(NT)),
                src_x=lambda t: d_x[t * 128:(t + 1) * 128, :],
                src_res=lambda t: d_x[t * 128:(t + 1) * 128, :],
                dst_of=lambda t: (d_x1ctx[t * 128:(t + 1) * 128, :] if t < OWN0 else
                                  d_x1own[(t - OWN0) * 128:(t - OWN0 + 1) * 128, :]),
            )

            def x1_src(t):
                return (d_x1ctx[t * 128:(t + 1) * 128, :] if t < OWN0 else
                        d_x1own[(t - OWN0) * 128:(t - OWN0 + 1) * 128, :])

            for nm_ in ("t", "dfree0", "dfree1", "dfree2"):
                wait(V, nm_)
            V.memset(a_M[0], 0.0)
            V.memset(a_MT[0], 0.0)
            V.drain()
            inc(V.engine_nop(), "v")
            mcur = {"i": 0}

            cs_state = {"n": 0}

            def cslot():
                bk = 2 + (cs_state["n"] % 2)
                cs_state["n"] += 1
                key = f"bank{bk}"
                pst = ps[bk][:, 0:128]
                guard(T, key)
                return pst, key

            def cmm(lhsT, rhs, dst, extra=None, n=128):
                pst, key = cslot()
                pm = inc(T.matmul(pst[:, 0:n], lhsT, rhs, start=True, stop=True), "csm")
                wait(V, "csm", pm)
                if extra is None:
                    V.tensor_copy(out=dst, in_=pst[:, 0:n])
                else:
                    extra(dst, pst[:, 0:n])
                V.drain()
                inc(V.engine_nop(), "v")
                used(key, "v")

            def emit_chunk(si, own, t_own):
                nx = 1 - mcur["i"]
                Mc, MTc = a_M[mcur["i"]], a_MT[mcur["i"]]
                Mn, MTn = a_M[nx], a_MT[nx]
                acol = a_acol[:, si:si + 1]
                bcol = a_bcol[:, si:si + 1]
                Kt = a_K[:, si, :]
                Vt = a_V[:, si, :]
                QTt = a_QT[:, si, :]
                GTt = a_GT[:, si, :]

                wait(A, "v", cnt["v"])
                ai = ai = ai = ai = ai = ai = ai = A.activation(out=a_la, in_=acol, func=AF.Ln)
                pa = inc(ai, "a")
                wait(T, "a", pa)
                cmm(c_ut, a_la, a_cum, n=1)
                wait(T, "v")
                pst, key = cslot()
                ptr = inc(T.transpose(pst[0:1, 0:128], a_cum, c_id), "csm")
                wait(V, "csm", ptr)
                V.tensor_copy(out=a_cumrow, in_=pst[0:1, 0:128])
                V.drain()
                pv = inc(V.engine_nop(), "v")
                used(key, "v")
                wait(T, "v", pv)
                cmm(c_ones, a_cumrow, a_cumrep)
                pv = cnt["v"]
                wait(A, "v", pv)
                A.activation(out=a_gam, in_=a_cum, func=AF.Exp)
                ai = A.activation(out=a_gamL, in_=a_cumrep[:, 127:128], func=AF.Exp)
                pa = inc(ai, "a")
                wait(V, "a", pa)
                V.tensor_sub(out=a_wgt, in0=a_cumrep[:, 127:128], in1=a_cum)
                V.drain()
                pv = inc(V.engine_nop(), "v")
                wait(A, "v", pv)
                ai = A.activation(out=a_wgt, in_=a_wgt, func=AF.Exp)
                pa = inc(ai, "a")
                wait(V, "a", pa)
                V.tensor_mul(out=a_wgt, in0=a_wgt, in1=bcol)
                V.tensor_scalar_mul(out=a_negb, in0=bcol, scalar1=-1.0)
                V.drain()
                V.tensor_scalar(out=a_t1, in0=a_cumrep, scalar1=a_cum, scalar2=None,
                                op0=AL.subtract)
                V.drain()
                V.tensor_add(out=a_t1, in0=a_t1, in1=c_aiu)
                V.drain()
                pv = inc(V.engine_nop(), "v")
                wait(A, "v", pv)
                ai = A.activation(out=a_gtincl, in_=a_t1, func=AF.Exp)
                if own:
                    ai = A.activation(out=a_gamrep, in_=a_cumrep, func=AF.Exp)
                pa = inc(ai, "a")
                wait(V, "a", pa)
                V.tensor_sub(out=a_t1, in0=a_gtincl, in1=c_id)
                V.drain()
                pv = inc(V.engine_nop(), "v")

                wait(T, "v", pv)
                transpose128(Kt, a_KT)
                if own:
                    transpose128(Vt, a_VT)
                wait(T, "tpe")
                cmm(a_KT, a_KT, a_kk)
                V.tensor_mul(out=s_nb[0], in0=a_t1, in1=a_kk)
                V.drain()
                V.tensor_scalar_mul(out=s_nb[0], in0=s_nb[0], scalar1=a_negb)
                V.tensor_scalar_mul(out=a_gk, in0=Kt, scalar1=a_gam)
                V.tensor_scalar_mul(out=a_kw, in0=Kt, scalar1=a_wgt)
                V.drain()
                pv = inc(V.engine_nop(), "v")
                wait(T, "v", pv)
                transpose128(s_nb[0], s_na[0])
                wait(V, "tpe")

                V.tensor_copy(out=a_Z, in_=a_kw)
                if own:
                    V.tensor_copy(out=a_Wm, in_=a_gk)
                V.drain()
                pv = inc(V.engine_nop(), "v")
                if own:
                    wait(T, "v", pv)
                    wait(T, "tpe")
                    cmm(a_VT, QTt, a_PF)
                cur = 0
                for m in range(7):
                    wait(T, "v")
                    cmm(s_na[cur], a_Z, a_Z,
                        extra=lambda d, p: V.tensor_add(out=d, in0=d, in1=p))
                    if own:
                        cmm(s_nb[cur], a_Wm, a_Wm,
                            extra=lambda d, p: V.tensor_add(out=d, in0=d, in1=p))
                        cmm(s_nb[cur], a_PF, a_PF,
                            extra=lambda d, p: V.tensor_add(out=d, in0=d, in1=p))
                    if m < 6:
                        wait(T, "v")
                        cmm(s_nb[cur], s_na[cur], s_na[1 - cur])
                        cmm(s_na[cur], s_nb[cur], s_nb[1 - cur])
                        cur = 1 - cur

                wait(T, "v")
                cmm(a_gk, a_Z, a_ET)
                cmm(a_Z, Vt, a_F)
                cmm(Vt, a_Z, a_FT)
                if own:
                    transpose128(a_Wm, a_WT)

                if own:
                    wait(T, "v")
                    cmm(MTc, QTt, a_MQ)
                    wait(T, "v")
                    wait(T, "tpe")
                    cmm(a_WT, a_MQ, a_BT,
                        extra=lambda d, p: V.scalar_tensor_tensor(
                            out=d, in0=p, scalar=-1.0, in1=a_PF,
                            op0=AL.mult, op1=AL.add))
                    V.tensor_mul(out=a_BT, in0=a_BT, in1=a_gtincl)
                    V.drain()
                    V.tensor_scalar_mul(out=a_BT, in0=a_BT, scalar1=bcol)
                    V.drain()
                    pv = inc(V.engine_nop(), "v")
                    wait(T, "v", pv)
                    cmm(Kt, a_BT, a_t2)
                    V.tensor_mul(out=a_t1, in0=a_MQ, in1=a_gamrep)
                    V.drain()
                    V.tensor_add(out=a_t1, in0=a_t1, in1=a_t2)
                    V.drain()
                    V.tensor_mul(out=a_OT, in0=a_t1, in1=GTt)
                    V.drain()
                    pv = inc(V.engine_nop(), "v")
                    wait(T, "v", pv)
                    for half in range(D // 512):
                        guard(T, "bank5")
                        mmO = T.matmul(ps[5][:, 0:512], a_OT,
                                       w_awo[:, half * 512:(half + 1) * 512],
                                       start=True, stop=True)
                        pt = inc(mmO, "t")
                        xr = xres[half % 2]
                        lvs = f"ld_xres{half % 2}"
                        lv = load(xr, d_x1own[t_own * 128:(t_own + 1) * 128,
                                              half * 512:(half + 1) * 512],
                                  key=f"xres{half % 2}")
                        wait(V, lvs, lv)
                        wait(V, "t", pt)
                        sgb = stg1[half]
                        guard(V, f"stg{half}")
                        V.tensor_add(out=sgb, in0=ps[5][:, 0:512], in1=xr)
                        V.drain()
                        V.tensor_add(out=sgb, in0=sgb,
                                     in1=b_o[:, half * 512:(half + 1) * 512])
                        V.drain()
                        pv = inc(V.engine_nop(), "v")
                        used("bank5", "v")
                        used(f"xres{half % 2}", "v")
                        wait(A, "v", pv)
                        store(d_x2own[t_own * 128:(t_own + 1) * 128,
                                      half * 512:(half + 1) * 512], sgb, half)
                        used(f"stg{half}", f"st{half}")

                wait(T, "v")
                cmm(a_ET, Mc, a_t1,
                    extra=lambda d, p: V.scalar_tensor_tensor(
                        out=d, in0=p, scalar=-1.0, in1=a_F, op0=AL.mult, op1=AL.add))
                cmm(Mc, a_ET, a_t2,
                    extra=lambda d, p: V.scalar_tensor_tensor(
                        out=d, in0=p, scalar=-1.0, in1=a_FT, op0=AL.mult, op1=AL.add))
                V.scalar_tensor_tensor(out=Mn, in0=Mc, scalar=a_gamL, in1=a_t1,
                                       op0=AL.mult, op1=AL.add)
                V.scalar_tensor_tensor(out=MTn, in0=MTc, scalar=a_gamL, in1=a_t2,
                                       op0=AL.mult, op1=AL.add)
                V.drain()
                inc(V.engine_nop(), "v")
                mcur["i"] = nx

            ngroups = NT // GROUP if "attn" in phases else 0
            for g in range(ngroups):
                gtiles = list(range(g * GROUP, (g + 1) * GROUP))
                own_g = gtiles[0] >= OWN0
                wait(A, "t")
                wait(A, "tpm")
                wait(V, "t")
                for si, t in enumerate(gtiles):
                    b = xin[si % 2]
                    lvs = f"ld_xin{si % 2}"
                    lv = load(b[:, 0:D], x1_src(t), key=f"xin{si % 2}")
                    wait(V, lvs, lv)
                    ln_tile(b, scr4, srckey=f"xin{si % 2}")
                    wait(T, "v")
                    for dc in range(DCH):
                        transpose128(scr4[:, dc * 128:(dc + 1) * 128],
                                     a_xh2T[:, dc, si * 128:(si + 1) * 128])
                    wait(V, "tpe")
                wait(T, "tpe")

                for si, t in enumerate(gtiles):
                    pp = ps[si % 2]
                    guard(T, f"bank{si % 2}")
                    for dc in range(DCH):
                        mm = T.matmul(pp[:, 0:258],
                                      a_xh2T[:, dc, si * 128:(si + 1) * 128],
                                      w_awkvab[:, dc, :], start=(dc == 0),
                                      stop=(dc == DCH - 1))
                    pt = inc(mm, "t")
                    wait(V, "t", pt)
                    V.tensor_add(out=scr1[:, 0:256], in0=pp[:, 0:256],
                                 in1=b_kvab[:, 0:256])
                    V.drain()
                    pv = inc(V.engine_nop(), "v")
                    wait(A, "v", pv)
                    A.activation(out=scr2[:, 0:128], in_=scr1[:, 0:128], func=AF.Square,
                                 accum_out=sm_s[:, 0:1])
                    A.drain()
                    A.activation(out=sm_s[:, 1:2], in_=sm_s[:, 0:1], func=AF.Sqrt)
                    A.activation(out=a_acol[:, si:si + 1], in_=pp[:, 256:257],
                                 func=AF.Sigmoid, bias=b_a)
                    ai = A.activation(out=a_bcol[:, si:si + 1], in_=pp[:, 257:258],
                                      func=AF.Sigmoid, bias=b_b)
                    pa = inc(ai, "a")
                    used(f"bank{si % 2}", "a")
                    wait(V, "a", pa)
                    V.tensor_scalar_max(out=sm_s[:, 1:2], in0=sm_s[:, 1:2],
                                        scalar1=1e-12)
                    V.drain()
                    pv = inc(V.engine_nop(), "v")
                    wait(A, "v", pv)
                    V.reciprocal(out=sm_r[:, 0:1], in_=sm_s[:, 1:2])
                    V.drain()
                    V.tensor_scalar_mul(out=a_K[:, si, :], in0=scr1[:, 0:128],
                                        scalar1=sm_r[:, 0:1])
                    V.tensor_copy(out=a_V[:, si, :], in_=scr1[:, 128:256])
                    V.drain()
                    inc(V.engine_nop(), "v")
                if own_g:
                    for si, t in enumerate(gtiles):
                        pp = ps[si % 2]
                        guard(T, f"bank{si % 2}")
                        for dc in range(DCH):
                            T.matmul(pp[:, 0:128], w_awq[:, dc, :],
                                     a_xh2T[:, dc, si * 128:(si + 1) * 128],
                                     start=(dc == 0), stop=(dc == DCH - 1))
                        for dc in range(DCH):
                            mm = T.matmul(pp[:, 128:256], w_awg[:, dc, :],
                                          a_xh2T[:, dc, si * 128:(si + 1) * 128],
                                          start=(dc == 0), stop=(dc == DCH - 1))
                        pt = inc(mm, "t")
                        wait(V, "t", pt)
                        V.tensor_scalar(out=a_QT[:, si, :], in0=pp[:, 0:128],
                                        scalar1=b_q, scalar2=None, op0=AL.add)
                        V.drain()
                        pv = inc(V.engine_nop(), "v")
                        wait(A, "v", pv)
                        ai = A.activation(out=a_GT[:, si, :], in_=pp[:, 128:256],
                                          func=AF.Sigmoid, bias=b_g)
                        pa = inc(ai, "a")
                        used(f"bank{si % 2}", "a")
                        wait(V, "a", pa)

                for si, t in enumerate(gtiles):
                    emit_chunk(si, own_g, (t - OWN0) if own_g else 0)

            wait_all_stores(SY)
            if "ffn2" in phases: emit_ffn(
                tiles=list(range(NOWN)),
                src_x=lambda t: d_x2own[t * 128:(t + 1) * 128, :],
                src_res=lambda t: d_x2own[t * 128:(t + 1) * 128, :],
                dst_of=lambda t: d_out[t * 128:(t + 1) * 128, :],
            )

            if "ffn2" not in phases:
                src_dump = d_x2own if "attn" in phases else d_x1own
                for t in range(0, NOWN * 128, 128):
                    load(xin[0][:, 0:D], src_dump[t:t + 128, :],
                         s="dbg_ld", key="xin0")
                    wait(A, "dbg_ld")
                    store(d_out[t:t + 128, :], xin[0][:, 0:D], 0)
                    used("xin0", "st0")
                    wait(SY, "st0")
            wait_all_stores(SY)

    return nc


def make_host_inputs(cfg, ins):
    NT, OWN0, D, I = cfg["NTILES"], cfg["OWN0"], cfg["D"], cfg["I"]
    NOWN = NT - OWN0
    x = np.ascontiguousarray(np.asarray(ins["L0_attn_x"], dtype=np.float32))
    B = x.shape[0]
    Wk = np.asarray(ins["L0_attn_Wk"], np.float32)
    Wv = np.asarray(ins["L0_attn_Wv"], np.float32)
    Wa = np.asarray(ins["L0_attn_Wa"], np.float32)
    Wb = np.asarray(ins["L0_attn_Wb"], np.float32)
    wkvab = np.ascontiguousarray(np.concatenate([Wk, Wv, Wa, Wb], axis=1))
    bk = np.asarray(ins["L0_attn_bk"], np.float32)
    bv = np.asarray(ins["L0_attn_bv"], np.float32)
    bkvab = np.ascontiguousarray(
        np.broadcast_to(np.concatenate([bk, bv, [0.0, 0.0]]).astype(np.float32)[None, :],
                        (128, 258)))
    common = {
        "wg": np.ascontiguousarray(np.asarray(ins["L0_ffn_w_gate"], np.float32)),
        "wu": np.ascontiguousarray(np.asarray(ins["L0_ffn_w_up"], np.float32)),
        "wd": np.ascontiguousarray(np.asarray(ins["L0_ffn_w_down"], np.float32)),
        "awq": np.ascontiguousarray(np.asarray(ins["L0_attn_Wq"], np.float32)),
        "awg": np.ascontiguousarray(np.asarray(ins["L0_attn_Wg"], np.float32)),
        "awkvab": wkvab,
        "awo": np.ascontiguousarray(np.asarray(ins["L0_attn_Wo"], np.float32)),
        "bkvab": bkvab,
        "bqcol": np.ascontiguousarray(np.asarray(ins["L0_attn_bq"], np.float32)[:, None]),
        "bgcol": np.ascontiguousarray(np.asarray(ins["L0_attn_bg"], np.float32)[:, None]),
        "bacol": np.ascontiguousarray(
            np.broadcast_to(np.asarray(ins["L0_attn_ba"], np.float32), (128,))[:, None]),
        "bbcol": np.ascontiguousarray(
            np.broadcast_to(np.asarray(ins["L0_attn_bb"], np.float32), (128,))[:, None]),
        "borep": np.ascontiguousarray(
            np.broadcast_to(np.asarray(ins["L0_attn_bo"], np.float32)[None, :], (128, D))),
        "c_ut": np.triu(np.ones((128, 128), np.float32)),
        "c_id": np.eye(128, dtype=np.float32),
        "c_idb": np.eye(128, dtype=np.float32).astype(ml_dtypes.bfloat16),
        "c_aiu": ((np.triu(np.ones((128, 128), np.float32)) - 1.0) * 1e9).astype(np.float32),
        "c_ones1": np.ones((1, 128), np.float32),
    }
    half = OWN0 * 128
    in_maps = []
    for c in range(2 * B):
        bi, hi = c // 2, c % 2
        xa = np.zeros((NT * 128, D), np.float32)
        if hi == 0:
            xa[half:half + NOWN * 128] = x[bi, 0:NOWN * 128]
        else:
            xa[0:half] = x[bi, 0:half]
            xa[half:half + NOWN * 128] = x[bi, half:half + NOWN * 128]
        in_maps.append(dict(common, xall=np.ascontiguousarray(xa)))
    return in_maps


def assemble(cfg, results, B, T):
    NOWN = cfg["NTILES"] - cfg["OWN0"]
    D = cfg["D"]
    out = np.zeros((B, T, D), np.float32)
    half = cfg["OWN0"] * 128
    for c, r in enumerate(results):
        bi, hi = c // 2, c % 2
        if hi == 0:
            out[bi, 0:NOWN * 128] = r["yout"]
        else:
            out[bi, half:half + NOWN * 128] = r["yout"]
    return out


from concourse.bass_utils import run_bass_kernel_spmd

_NC_CACHE = {}


def kernel(**inputs):
    cfg = FULL
    if "nc" not in _NC_CACHE:
        _NC_CACHE["nc"] = build_nc(cfg)
    nc = _NC_CACHE["nc"]
    in_maps = make_host_inputs(cfg, inputs)
    res = run_bass_kernel_spmd(nc, in_maps, core_ids=list(range(8)))
    x = np.asarray(inputs["L0_attn_x"])
    return assemble(cfg, res.results, x.shape[0], x.shape[1])


# version 17
# speedup vs baseline: 1.3313x; 1.0135x over previous
import contextlib
import numpy as np
import ml_dtypes
import concourse.bass as bass
import concourse.mybir as mybir

F32 = mybir.dt.float32
BF16 = mybir.dt.bfloat16
I32 = mybir.dt.int32
AF = mybir.ActivationFunctionType
AL = mybir.AluOpType
AX = mybir.AxisListType

FULL = dict(NTILES=32, OWN0=16, D=1024, I=2816, ITILE=256, SWEEP=4, GROUP=2)
TINY = dict(NTILES=2, OWN0=1, D=1024, I=2816, ITILE=256, SWEEP=2, GROUP=1)
MID = dict(NTILES=8, OWN0=4, D=1024, I=2816, ITILE=256, SWEEP=4, GROUP=2)


def build_nc(cfg, phases=("prep", "ffn1", "attn", "ffn2")):
    NT, OWN0, D, I = cfg["NTILES"], cfg["OWN0"], cfg["D"], cfg["I"]
    ITILE, SWEEP, GROUP = cfg["ITILE"], cfg["SWEEP"], cfg["GROUP"]
    DCH, ICH = D // 128, I // 128
    NIT = I // ITILE
    NOWN = NT - OWN0
    IBLK = ITILE // 128
    SCRW = D
    SLOTW = SWEEP * ITILE
    QW = max(2 * D, 2 * SLOTW)

    nc = bass.Bass("TRN2", target_bir_lowering=False, debug=False)

    d_x = nc.dram_tensor("xall", [NT * 128, D], F32, kind="ExternalInput").ap()
    d_out = nc.dram_tensor("yout", [NOWN * 128, D], F32, kind="ExternalOutput").ap()
    d_wg = nc.dram_tensor("wg", [I, D], F32, kind="ExternalInput").ap()
    d_wu = nc.dram_tensor("wu", [I, D], F32, kind="ExternalInput").ap()
    d_wd = nc.dram_tensor("wd", [D, I], F32, kind="ExternalInput").ap()
    d_awq = nc.dram_tensor("awq", [D, 128], F32, kind="ExternalInput").ap()
    d_awg = nc.dram_tensor("awg", [D, 128], F32, kind="ExternalInput").ap()
    d_awkvab = nc.dram_tensor("awkvab", [D, 258], F32, kind="ExternalInput").ap()
    d_awo = nc.dram_tensor("awo", [128, D], F32, kind="ExternalInput").ap()
    d_bkvab = nc.dram_tensor("bkvab", [128, 258], F32, kind="ExternalInput").ap()
    d_bq = nc.dram_tensor("bqcol", [128, 1], F32, kind="ExternalInput").ap()
    d_bg = nc.dram_tensor("bgcol", [128, 1], F32, kind="ExternalInput").ap()
    d_ba = nc.dram_tensor("bacol", [128, 1], F32, kind="ExternalInput").ap()
    d_bb = nc.dram_tensor("bbcol", [128, 1], F32, kind="ExternalInput").ap()
    d_bo = nc.dram_tensor("borep", [128, D], F32, kind="ExternalInput").ap()
    d_cut = nc.dram_tensor("c_ut", [128, 128], F32, kind="ExternalInput").ap()
    d_cid = nc.dram_tensor("c_id", [128, 128], F32, kind="ExternalInput").ap()
    d_cidb = nc.dram_tensor("c_idb", [128, 128], BF16, kind="ExternalInput").ap()
    d_caiu = nc.dram_tensor("c_aiu", [128, 128], F32, kind="ExternalInput").ap()
    d_cones = nc.dram_tensor("c_ones1", [1, 128], F32, kind="ExternalInput").ap()

    d_x1ctx = nc.dram_tensor("x1ctx", [max(OWN0, 1) * 128, D], F32, kind="Internal").ap()
    d_x1own = nc.dram_tensor("x1own", [NOWN * 128, D], F32, kind="Internal").ap()
    d_x2own = nc.dram_tensor("x2own", [NOWN * 128, D], F32, kind="Internal").ap()
    d_stg = [nc.dram_tensor(f"stash_g{h}", [DCH, ICH, 128, 128], BF16,
                            kind="Internal").ap() for h in range(2)]
    d_stu = [nc.dram_tensor(f"stash_u{h}", [DCH, ICH, 128, 128], BF16,
                            kind="Internal").ap() for h in range(2)]
    d_std = [nc.dram_tensor(f"stash_d{h}", [ICH, DCH, 128, 128], BF16,
                            kind="Internal").ap() for h in range(2)]

    es = contextlib.ExitStack()
    with es:
        def sb(name, shape, dt=F32):
            return es.enter_context(nc.sbuf_tensor(name, shape, dt)).ap()

        c_ut = sb("c_ut_s", [128, 128])
        c_id = sb("c_id_s", [128, 128])
        c_aiu = sb("c_aiu_s", [128, 128])
        c_ones = sb("c_ones_s", [1, 128])
        w_awq = sb("w_awq", [128, DCH, 128])
        w_awg = sb("w_awg", [128, DCH, 128])
        w_awkvab = sb("w_awkvab", [128, DCH, 258])
        w_awo = sb("w_awo", [128, D])
        b_kvab = sb("b_kvab_s", [128, 258])
        b_q = sb("b_q_s", [128, 1])
        b_g = sb("b_g_s", [128, 1])
        b_a = sb("b_a_s", [128, 1])
        b_b = sb("b_b_s", [128, 1])
        b_o = sb("b_o_s", [128, D])
        c_eps = sb("c_eps", [128, 1])

        xin = [sb("xin0", [128, QW]), sb("xin1", [128, SCRW])]
        scr1 = sb("scr1", [128, QW])
        scr2 = sb("scr2", [128, QW])
        scr3 = sb("scr3", [128, QW])
        scr4 = sb("scr4", [128, SCRW])
        sm_m = sb("sm_m", [128, QW // 16])
        sm_s = sb("sm_s", [128, QW // 16])
        sm_r = sb("sm_r", [128, QW // 16])
        xqT = [[sb(f"xqT{t}_{h}", [128, DCH, 128], BF16) for h in range(2)]
               for t in range(SWEEP)]
        wcol = [[sb(f"wcol{i}_{h}", [128, 2, DCH, ITILE], BF16) for h in range(2)]
                for i in range(2)]
        c_idb = sb("c_idb_s", [128, 128], BF16)
        sp_h = sb("sp_h", [128, QW], BF16)
        sp_l = sb("sp_l", [128, QW], BF16)
        hb_all = sb("hb_all", [128, 4 * SLOTW])
        hb = [hb_all[:, i * SLOTW:(i + 1) * SLOTW] for i in range(4)]
        hsilu = [sb(f"hsilu{i}", [128, ITILE]) for i in range(2)]
        hq_m, hq_s, hq_r = sm_m, sm_s, sm_r
        hqT = [[sb(f"hqT{t}_{h}", [128, ICH, 128], BF16) for h in range(2)]
               for t in range(SWEEP)]
        wdn = [[sb(f"wdn{i}_{h}", [128, 512], BF16) for h in range(2)]
               for i in range(3)]
        xres = [sb(f"xres{i}", [128, 512]) for i in range(2)]
        stg1 = [sb(f"stg1_{i}", [128, 512]) for i in range(4)]

        a_xh2T = hb_all[:, 0:DCH * GROUP * 128].rearrange(
            "p (d n) -> p d n", n=GROUP * 128)
        hpool = [hqT[si][h].bitcast(F32).rearrange("p a n -> p (a n)")
                 for si in (0, 1) for h in (0, 1)]
        hp_off = [0, 0, 0, 0]

        def carve(pi, w):
            c0 = hp_off[pi]
            hp_off[pi] = c0 + w
            v = hpool[pi][:, c0:c0 + w]
            assert hp_off[pi] <= ICH * 64, "hqT pool overflow"
            return v

        a_K = carve(0, GROUP * 128).rearrange("p (g n) -> p g n", n=128)
        a_V = carve(0, GROUP * 128).rearrange("p (g n) -> p g n", n=128)
        a_QT = carve(0, GROUP * 128).rearrange("p (g n) -> p g n", n=128)
        a_GT = carve(0, GROUP * 128).rearrange("p (g n) -> p g n", n=128)
        a_KT = carve(0, 128)
        a_VT = carve(0, 128)
        a_acol = sb("a_acol", [128, GROUP])
        a_bcol = sb("a_bcol", [128, GROUP])
        a_la = sb("a_la", [128, 1])
        a_cum = sb("a_cum", [128, 1])
        a_cumrow = sb("a_cumrow", [1, 128])
        a_gam = sb("a_gam", [128, 1])
        a_gamL = sb("a_gamL", [128, 1])
        a_wgt = sb("a_wgt", [128, 1])
        a_negb = sb("a_negb", [128, 1])
        a_cumrep = carve(1, 128)
        a_gamrep = carve(1, 128)
        a_gtincl = carve(1, 128)
        a_kk = carve(1, 128)
        s_na = [carve(1, 128) for i in range(2)]
        s_nb = [carve(1, 128) for i in range(2)]
        a_gk = carve(1, 128)
        a_kw = carve(1, 128)
        a_Z = carve(1, 128)
        a_Wm = carve(2, 128)
        a_PF = carve(2, 128)
        a_WT = carve(2, 128)
        a_ET = carve(2, 128)
        a_F = carve(2, 128)
        a_FT = carve(2, 128)
        a_M = [carve(2, 128) for i in range(2)]
        a_MT = [carve(2, 128) for i in range(2)]
        a_MQ = carve(3, 128)
        a_BT = carve(3, 128)
        a_OT = carve(3, 128)
        a_t1 = carve(3, 128)
        a_t2 = carve(3, 128)

        ps = [es.enter_context(nc.psum_tensor(f"ps{i}", [128, 512], F32)).ap()
              for i in range(8)]

        sems, cnt = {}, {}

        def sem(name):
            if name not in sems:
                sems[name] = es.enter_context(nc.semaphore(name))
                cnt[name] = 0
            return sems[name]

        def inc(instr, name, amt=1):
            s = sem(name)
            cnt[name] += amt
            instr.then_inc(s, amt)
            return cnt[name]

        def wait(eng, name, val=None):
            if name not in cnt and val is None:
                return
            s = sem(name)
            v = cnt[name] if val is None else val
            if v > 0:
                eng.wait_ge(s, v)

        V, A, T, G, SY = nc.vector, nc.scalar, nc.tensor, nc.gpsimd, nc.sync

        lastuse = {}

        def guard(eng, key):
            if key in lastuse:
                sname, val = lastuse[key]
                wait(eng, sname, val)

        def used(key, sname):
            sem(sname)
            lastuse[key] = (sname, cnt[sname])

        def load(out, in_, s=None, key=None):
            if key is not None:
                guard(SY, key)
            if s is None:
                s = "ld_" + (key if key is not None else "c")
            return inc(SY.dma_start(out=out, in_=in_), s, 16)

        def store(out, in_, slot):
            return inc(A.dma_start(out=out, in_=in_), f"st{slot}", 16)

        def wait_all_stores(eng):
            for i in range(4):
                wait(eng, f"st{i}")

        with nc.Block():
            load(c_ut, d_cut); load(c_id, d_cid); load(c_aiu, d_caiu)
            load(c_idb, d_cidb)
            load(c_ones, d_cones)
            load(w_awq, d_awq.rearrange("(c p) n -> p c n", p=128))
            load(w_awg, d_awg.rearrange("(c p) n -> p c n", p=128))
            load(w_awkvab, d_awkvab.rearrange("(c p) n -> p c n", p=128))
            load(w_awo, d_awo)
            load(b_kvab, d_bkvab); load(b_q, d_bq); load(b_g, d_bg)
            load(b_a, d_ba); load(b_b, d_bb); load(b_o, d_bo)
            CONST_LD = cnt["ld_c"]
            for eng in (V, A, T):
                wait(eng, "ld_c", CONST_LD)
            V.memset(c_eps, 1e-5)
            V.drain()

            SH_MUL = float(np.float32(1.0) / np.float32(6.0)) * 0.5
            SH_MIN = float(np.float32(1e-12)) * 0.5

            def quant(src, out, F, mb, ssb, rb, t1, t2, t3, srckey=None):
                nb = F // 16
                v3 = src.rearrange("p (b k) -> p b k", k=16) if src.shape[1] == F \
                    else src
                V.tensor_reduce(out=mb[:, :nb], in_=v3, axis=AX.X, op=AL.max,
                                apply_absolute_value=True)
                V.tensor_tensor(out=t1[:, :F], in0=src, in1=src, op=AL.mult)
                V.drain()
                V.tensor_scalar(out=ssb[:, :nb], in0=mb[:, :nb], scalar1=SH_MUL,
                                scalar2=SH_MIN, op0=AL.mult, op1=AL.max)
                V.drain()
                V.reciprocal(out=rb[:, :nb], in_=ssb[:, :nb])
                V.drain()
                V.tensor_tensor(out=mb[:, :nb], in0=rb[:, :nb], in1=rb[:, :nb],
                                op=AL.mult)
                V.drain()
                mb3 = mb[:, :nb].broadcast_to([128, nb, 16])
                t1v = t1[:, :F].rearrange("p (b k) -> p b k", k=16)
                V.tensor_tensor(out=t1v, in0=t1v, in1=mb3, op=AL.mult)
                V.drain()
                c1 = t3.bitcast(BF16)[:, 0:F]
                c2 = t3.bitcast(BF16)[:, F:2 * F]
                c3 = t2.bitcast(BF16)[:, 0:F]
                c4 = t2.bitcast(BF16)[:, F:2 * F]

                def cmp(dst, thr2, w):
                    V.tensor_scalar(out=dst, in0=t1[:, :F], scalar1=thr2,
                                    scalar2=w, op0=AL.is_gt, op1=AL.mult)

                guard(V, "scr3g")
                cmp(c1, 0.25, 1.0)
                guard(V, "scr2g")
                cmp(c3, 2.25, 1.0)
                cmp(c4, 6.25, 1.0)
                cmp(c2, 12.25, 1.0)
                V.drain()
                V.tensor_add(out=c1, in0=c1, in1=c3)
                V.tensor_add(out=c4, in0=c4, in1=c2)
                V.drain()
                cmp(c3, 25.0, 2.0)
                cmp(c2, 49.0, 2.0)
                V.drain()
                V.tensor_add(out=c1, in0=c1, in1=c4)
                V.tensor_add(out=c3, in0=c3, in1=c2)
                V.drain()
                cmp(c2, 100.0, 4.0)
                V.tensor_add(out=c1, in0=c1, in1=c3)
                V.drain()
                V.tensor_add(out=c1, in0=c1, in1=c2)
                V.drain()
                sh3 = ssb[:, :nb].broadcast_to([128, nb, 16])
                t2v = t2[:, :F].rearrange("p (b k) -> p b k", k=16)
                mm = V.tensor_tensor(out=t2v,
                                     in0=c1.rearrange("p (b k) -> p b k", k=16),
                                     in1=sh3, op=AL.mult)
                pv = inc(mm, "v")
                sgb = c2
                wait(A, "v", pv)
                wait(A, "tpm")
                A.copy(out=sp_h[:, :F], in_=t2[:, :F])
                ai = A.activation(out=sgb, in_=src, func=AF.Sign)
                pa = inc(ai, "a")
                if srckey is not None:
                    used(srckey, "a")
                wait(G, "a", pa)
                G.tensor_tensor(out=sp_l[:, :F], in0=t2[:, :F], in1=sp_h[:, :F],
                                op=AL.subtract)
                G.drain()
                G.tensor_mul(out=sp_h[:, :F], in0=sp_h[:, :F], in1=sgb)
                gi = G.tensor_mul(out=sp_l[:, :F], in0=sp_l[:, :F], in1=sgb)
                inc(gi, "gq")
                used("scr2g", "gq")
                used("scr3g", "gq")
                return pv

            tp_state = {"n": 0}

            def transpose128(src_ap, dst_ap, ident=None):
                bk = 6 + (tp_state["n"] % 2)
                tp_state["n"] += 1
                key = f"bank{bk}"
                if ident is None:
                    pst = ps[bk][:, 0:128]
                else:
                    pst = ps[bk].bitcast(BF16)[:, 0:128]
                guard(T, key)
                mm = T.transpose(pst, src_ap, c_id if ident is None else ident)
                pt = inc(mm, "tpm")
                wait(A, "tpm", pt)
                inc(A.copy(out=dst_ap, in_=pst), "tpe")
                used(key, "tpe")
                return mm

            def transpose_hl(colslice, dsth, dstl):
                transpose128(sp_h[:, colslice], dsth, ident=c_idb)
                return transpose128(sp_l[:, colslice], dstl, ident=c_idb)

            def transpose_batch(pairs):
                assert len(pairs) <= 8
                bk = 6 + (tp_state["n"] % 2)
                tp_state["n"] += 1
                key = f"bank{bk}"
                pst = ps[bk].bitcast(BF16)
                guard(T, key)
                for i, (src, _) in enumerate(pairs):
                    mm = T.transpose(pst[:, i * 128:(i + 1) * 128], src, c_idb)
                pt = inc(mm, "tpm")
                wait(A, "tpm", pt)
                for i, (_, dst) in enumerate(pairs):
                    ai = A.copy(out=dst, in_=pst[:, i * 128:(i + 1) * 128])
                inc(ai, "tpe")
                used(key, "tpe")

            def transpose_batch_wide(srcs, dst):
                assert len(srcs) <= 8
                bk = 6 + (tp_state["n"] % 2)
                tp_state["n"] += 1
                key = f"bank{bk}"
                pst = ps[bk].bitcast(BF16)
                guard(T, key)
                for i, src in enumerate(srcs):
                    mm = T.transpose(pst[:, i * 128:(i + 1) * 128], src, c_idb)
                pt = inc(mm, "tpm")
                wait(A, "tpm", pt)
                inc(A.copy(out=dst, in_=pst[:, 0:len(srcs) * 128]), "tpe")
                used(key, "tpe")

            def ln_tile(src, out, srckey=None):
                V.tensor_reduce(out=sm_m[:, 0:1], in_=src[:, 0:D], axis=AX.X, op=AL.add)
                V.drain()
                V.tensor_scalar_mul(out=sm_m[:, 0:1], in0=sm_m[:, 0:1], scalar1=1.0 / D)
                V.drain()
                guard(V, "scr1sign")
                vi = V.tensor_scalar(out=scr1[:, 0:D], in0=src[:, 0:D],
                                     scalar1=sm_m[:, 0:1],
                                     scalar2=None, op0=AL.subtract)
                pv = inc(vi, "v")
                V.drain()
                if srckey is not None:
                    used(srckey, "v")
                wait(A, "v", pv)
                guard(A, "scr2g")
                A.activation(out=scr2[:, 0:D], in_=scr1[:, 0:D], func=AF.Square,
                             accum_out=sm_s[:, 0:1])
                A.drain()
                ai = A.activation(out=sm_r[:, 0:1], in_=sm_s[:, 0:1], func=AF.Sqrt,
                                  scale=1.0 / D, bias=c_eps)
                pa = inc(ai, "a")
                wait(V, "a", pa)
                V.reciprocal(out=sm_r[:, 1:2], in_=sm_r[:, 0:1])
                V.drain()
                vi = V.tensor_scalar_mul(out=out[:, 0:D], in0=scr1[:, 0:D],
                                         scalar1=sm_r[:, 1:2])
                pv = inc(vi, "v")
                V.drain()
                return pv

            stgb = [hqT[-1][h].rearrange("p a n -> p (a n)")[:, 0:1024]
                    for h in range(2)]

            def prep_group(segs, rt2):
                b = xin[0]
                lvs = "ld_xin0"
                off = 0
                for dram_ap, Fq, _ in segs:
                    lv = load(b[:, off:off + Fq], dram_ap, key="xin0")
                    off += Fq
                Ftot = off
                wait(V, lvs, lv)
                quant(b[:, :Ftot], b[:, :Ftot], Ftot, sm_m, sm_s, sm_r,
                      scr1, scr2, scr3, srckey="xin0")
                wait(T, "gq")
                off = 0
                for _, Fq, wdst in segs:
                    nbk = Fq // 128
                    for h in range(2):
                        sp = sp_h if h == 0 else sp_l
                        for p0 in range(0, nbk, 8):
                            pw = min(8, nbk - p0)
                            sgb = stgb[h]
                            guard(A, f"stg{h}")
                            transpose_batch_wide(
                                [sp[:, off + (p0 + j) * 128:off + (p0 + j + 1) * 128]
                                 for j in range(pw)],
                                sgb[:, 0:pw * 128])
                            wait(A, "tpe")
                            store(wdst(h, p0, pw),
                                  sgb[:, 0:pw * 128]
                                  .rearrange("p (b n) -> p b n", n=128), h)
                            used(f"stg{h}", f"st{h}")
                    off += Fq

            rt2 = 0
            if "prep" in phases:
                groups = []
                for stash, wsrc in ((d_stg, d_wg), (d_stu, d_wu)):
                    for r0 in range(0, ICH, 2):
                        groups.append([
                            (wsrc[rt * 128:(rt + 1) * 128, :], D,
                             lambda h, p0, pw, rt=rt, stash=stash:
                             stash[h][p0:p0 + pw, rt].rearrange("b p n -> p b n"))
                            for rt in (r0, r0 + 1) if rt < ICH])
                for rt in range(DCH):
                    c0 = 0
                    while c0 < I:
                        cw = min(2 * D, I - c0)
                        groups.append([
                            (d_wd[rt * 128:(rt + 1) * 128, c0:c0 + cw], cw,
                             lambda h, p0, pw, rt=rt, c0=c0:
                             d_std[h][c0 // 128 + p0:c0 // 128 + p0 + pw, rt]
                             .rearrange("b p n -> p b n"))])
                        c0 += cw
                for g in groups:
                    prep_group(g, rt2)
                    rt2 += 1
            wait_all_stores(SY)
            wait_all_stores(A)

            def emit_ffn(tiles, src_x, src_res, dst_of):
                nsw = (len(tiles) + SWEEP - 1) // SWEEP

                def emit_pre(stiles):
                    wait(A, "t")
                    for si, t in enumerate(stiles):
                        b = xin[si % 2]
                        lvs = f"ld_xin{si % 2}"
                        lv = load(b[:, 0:D], src_x(t), key=f"xin{si % 2}")
                        wait(V, lvs, lv)
                        ln_tile(b, scr4, srckey=f"xin{si % 2}")
                        quant(scr4[:, 0:D], scr4[:, 0:D], D, sm_m, sm_s, sm_r,
                              scr1, scr2, scr3)
                        wait(T, "gq")
                        transpose_batch_wide(
                            [sp_h[:, dc * 128:(dc + 1) * 128] for dc in range(DCH)],
                            xqT[si][0].rearrange("p a n -> p (a n)"))
                        transpose_batch_wide(
                            [sp_l[:, dc * 128:(dc + 1) * 128] for dc in range(DCH)],
                            xqT[si][1].rearrange("p a n -> p (a n)"))
                        wait(V, "tpe")
                    wait(T, "tpe")

                def emit_gateup(stiles):
                    ns = len(stiles)
                    pending_it = []

                    def flush_items(items):
                        hs0 = items[0][1]
                        F0 = sum(nsX * ITILE for _, _, nsX in items)
                        src = hb_all[:, hs0 * SLOTW:hs0 * SLOTW + F0]
                        wait(V, "g")
                        wait(A, "g")
                        quant(src, src, F0, hq_m, hq_s, hq_r, scr1, scr2, scr3)
                        wait(T, "gq")
                        pairs_h, pairs_l = [], []
                        for k, (itX, hsX, nsX) in enumerate(items):
                            for si0 in range(nsX):
                                for bi in range(IBLK):
                                    col = k * SLOTW + si0 * ITILE + bi * 128
                                    pairs_h.append((sp_h[:, col:col + 128],
                                                    hqT[si0][0][:, itX * IBLK + bi, :]))
                                    pairs_l.append((sp_l[:, col:col + 128],
                                                    hqT[si0][1][:, itX * IBLK + bi, :]))
                        for pp_ in (pairs_h, pairs_l):
                            for k0 in range(0, len(pp_), 8):
                                transpose_batch(pp_[k0:k0 + 8])
                        for _, hsX, _ in items:
                            used(f"hb{hsX}", "tpm")

                    def load_wcol(slot, it0):
                        for mi, stash in enumerate((d_stg, d_stu)):
                            for h in range(2):
                                v5 = wcol[slot][h].rearrange(
                                    "p m d (b n) -> p m d b n", b=IBLK)
                                for bb in range(IBLK):
                                    load(v5[:, mi, :, bb, :],
                                         stash[h][:, it0 * IBLK + bb]
                                         .rearrange("d p n -> p d n"),
                                         s=f"wld{slot}", key=f"wcol{slot}")

                    for it in range(NIT):
                        if it == 0:
                            load_wcol(0, 0)
                        if it + 1 < NIT:
                            load_wcol((it + 1) % 2, it + 1)
                        wb = wcol[it % 2]
                        wait(T, f"wld{it % 2}")
                        hs = it % 4
                        for si in range(ns):
                            pp = ps[si % 2]
                            guard(T, f"bank{si % 2}")
                            passes = ((0, 0), (0, 1), (1, 0))
                            np_ = len(passes)
                            for mi2 in range(2):
                                reg = pp[:, mi2 * ITILE:(mi2 + 1) * ITILE]
                                k = 0
                                for xh_, wh_ in passes:
                                    for dc in range(DCH):
                                        mm = T.matmul(
                                            reg, xqT[si][xh_][:, dc, :],
                                            wb[wh_][:, mi2, dc, :],
                                            start=(k == 0),
                                            stop=(k == np_ * DCH - 1))
                                        k += 1
                            pt = inc(mm, "t")
                            if si == ns - 1:
                                used(f"wcol{it % 2}", "t")
                            wait(A, "t", pt)
                            hp = si % 2
                            guard(A, f"hsilu{hp}")
                            A.activation(out=hsilu[hp], in_=pp[:, 0:ITILE],
                                         func=AF.Silu)
                            hu = xres[hp]
                            guard(A, f"xres{hp}")
                            ai = A.copy(out=hu[:, 0:ITILE], in_=pp[:, ITILE:2 * ITILE])
                            pa = inc(ai, "a")
                            used(f"bank{si % 2}", "a")
                            wait(G, "a", pa)
                            if si == 0:
                                guard(G, f"hb{hs}")
                            gi = G.tensor_mul(out=hb[hs][:, si * ITILE:(si + 1) * ITILE],
                                              in0=hsilu[hp], in1=hu[:, 0:ITILE])
                            pg = inc(gi, "g")
                            used(f"xres{hp}", "g")
                            used(f"hsilu{hp}", "g")
                        pending_it.append((it, hs, ns))
                        if len(pending_it) >= 4:
                            flush_items([pending_it.pop(0), pending_it.pop(0)])
                    while pending_it:
                        if (len(pending_it) >= 2 and pending_it[0][1] % 2 == 0
                                and pending_it[1][1] == pending_it[0][1] + 1):
                            flush_items([pending_it.pop(0), pending_it.pop(0)])
                        else:
                            flush_items([pending_it.pop(0)])
                    wait(T, "tpe")

                def emit_down(stiles):
                    ns = len(stiles)
                    if True:
                        htiles = stiles
                        for dt in range(D // 512):
                            pts = []

                            def load_wdn(ic0):
                                sl = ic0 % 3
                                for h in range(2):
                                    load(wdn[sl][h].rearrange("p (b n) -> p b n", b=4),
                                         d_std[h][ic0, dt * 4:(dt + 1) * 4]
                                         .rearrange("b p n -> p b n"),
                                         s=f"dld{sl}", key=f"wdn{sl}")

                            load_wdn(0)
                            load_wdn(1)
                            for ic in range(ICH):
                                wbD = wdn[ic % 3]
                                lv = cnt[f"dld{ic % 3}"]
                                if ic + 2 < ICH:
                                    load_wdn(ic + 2)
                                wait(T, f"dld{ic % 3}", lv)
                                mms = []
                                for si2 in range(len(htiles)):
                                    si = si2
                                    pd = ps[2 + si2]
                                    if ic == 0:
                                        guard(T, f"bank{2 + si2}")
                                    for pi, (xh_, wh_) in enumerate(
                                            ((0, 0), (0, 1), (1, 0))):
                                        mms.append(T.matmul(
                                            pd[:, :512], hqT[si][xh_][:, ic, :],
                                            wbD[wh_], start=(ic == 0 and pi == 0),
                                            stop=(ic == ICH - 1 and pi == 2)))
                                inc(mms[-1], f"dfree{ic % 3}")
                                used(f"wdn{ic % 3}", f"dfree{ic % 3}")
                                if ic == ICH - 1:
                                    for si2 in range(len(htiles)):
                                        m_ = mms[si2 * 3 + 2]
                                        if si2 == len(htiles) - 1:
                                            pts.append((f"dfree{ic % 3}",
                                                        cnt[f"dfree{ic % 3}"]))
                                        else:
                                            pts.append(("t", inc(m_, "t")))
                            for si2, t in enumerate(htiles):
                                pd = ps[2 + si2]
                                xr = xres[si2 % 2]
                                lvs = f"ld_xres{si2 % 2}"
                                lv = load(xr, src_res(t)[:, dt * 512:(dt + 1) * 512],
                                          key=f"xres{si2 % 2}")
                                wait(A, pts[si2][0], pts[si2][1])
                                sl = si2
                                sgb = stg1[sl]
                                guard(A, f"stg{sl}")
                                ai = A.activation(out=sgb, in_=pd[:, :512],
                                                  func=AF.Copy, scale=0.5)
                                pa = inc(ai, "a")
                                used(f"bank{2 + si2}", "a")
                                wait(G, "a", pa)
                                wait(G, lvs, lv)
                                pg = inc(G.tensor_add(out=sgb, in0=sgb, in1=xr), "g")
                                used(f"xres{si2 % 2}", "g")
                                wait(A, "g", pg)
                                store(dst_of(t)[:, dt * 512:(dt + 1) * 512], sgb, sl)
                                used(f"stg{sl}", f"st{sl}")

                for sw in range(nsw):
                    stiles = stiles_of = tiles[sw * SWEEP:(sw + 1) * SWEEP]
                    if sw == 0:
                        emit_pre(stiles)
                    emit_gateup(stiles)
                    if sw + 1 < nsw:
                        emit_pre(tiles[(sw + 1) * SWEEP:(sw + 2) * SWEEP])
                    emit_down(stiles)
                wait_all_stores(SY)

            if "ffn1" in phases: emit_ffn(
                tiles=list(range(NT)),
                src_x=lambda t: d_x[t * 128:(t + 1) * 128, :],
                src_res=lambda t: d_x[t * 128:(t + 1) * 128, :],
                dst_of=lambda t: (d_x1ctx[t * 128:(t + 1) * 128, :] if t < OWN0 else
                                  d_x1own[(t - OWN0) * 128:(t - OWN0 + 1) * 128, :]),
            )

            def x1_src(t):
                return (d_x1ctx[t * 128:(t + 1) * 128, :] if t < OWN0 else
                        d_x1own[(t - OWN0) * 128:(t - OWN0 + 1) * 128, :])

            for nm_ in ("t", "dfree0", "dfree1", "dfree2"):
                wait(V, nm_)
            V.memset(a_M[0], 0.0)
            V.memset(a_MT[0], 0.0)
            V.drain()
            inc(V.engine_nop(), "v")
            mcur = {"i": 0}

            cs_state = {"n": 0}

            def cslot():
                bk = 2 + (cs_state["n"] % 2)
                cs_state["n"] += 1
                key = f"bank{bk}"
                pst = ps[bk][:, 0:128]
                guard(T, key)
                return pst, key

            def cmm(lhsT, rhs, dst, extra=None, n=128):
                pst, key = cslot()
                pm = inc(T.matmul(pst[:, 0:n], lhsT, rhs, start=True, stop=True), "csm")
                wait(V, "csm", pm)
                if extra is None:
                    vi = V.tensor_copy(out=dst, in_=pst[:, 0:n])
                else:
                    vi = extra(dst, pst[:, 0:n])
                if vi is None:
                    V.drain()
                    vi = V.engine_nop()
                inc(vi, "v")
                used(key, "v")

            def emit_chunk(si, own, t_own):
                nx = 1 - mcur["i"]
                Mc, MTc = a_M[mcur["i"]], a_MT[mcur["i"]]
                Mn, MTn = a_M[nx], a_MT[nx]
                acol = a_acol[:, si:si + 1]
                bcol = a_bcol[:, si:si + 1]
                Kt = a_K[:, si, :]
                Vt = a_V[:, si, :]
                QTt = a_QT[:, si, :]
                GTt = a_GT[:, si, :]

                wait(A, "v", cnt["v"])
                ai = ai = ai = ai = ai = ai = ai = A.activation(out=a_la, in_=acol, func=AF.Ln)
                pa = inc(ai, "a")
                wait(T, "a", pa)
                cmm(c_ut, a_la, a_cum, n=1)
                wait(T, "v")
                pst, key = cslot()
                ptr = inc(T.transpose(pst[0:1, 0:128], a_cum, c_id), "csm")
                wait(V, "csm", ptr)
                V.tensor_copy(out=a_cumrow, in_=pst[0:1, 0:128])
                V.drain()
                pv = inc(V.engine_nop(), "v")
                used(key, "v")
                wait(T, "v", pv)
                cmm(c_ones, a_cumrow, a_cumrep)
                pv = cnt["v"]
                wait(A, "v", pv)
                A.activation(out=a_gam, in_=a_cum, func=AF.Exp)
                ai = A.activation(out=a_gamL, in_=a_cumrep[:, 127:128], func=AF.Exp)
                pa = inc(ai, "a")
                wait(V, "a", pa)
                V.tensor_sub(out=a_wgt, in0=a_cumrep[:, 127:128], in1=a_cum)
                V.drain()
                pv = inc(V.engine_nop(), "v")
                wait(A, "v", pv)
                ai = A.activation(out=a_wgt, in_=a_wgt, func=AF.Exp)
                pa = inc(ai, "a")
                wait(V, "a", pa)
                V.tensor_mul(out=a_wgt, in0=a_wgt, in1=bcol)
                V.tensor_scalar_mul(out=a_negb, in0=bcol, scalar1=-1.0)
                V.drain()
                V.tensor_scalar(out=a_t1, in0=a_cumrep, scalar1=a_cum, scalar2=None,
                                op0=AL.subtract)
                V.drain()
                V.tensor_add(out=a_t1, in0=a_t1, in1=c_aiu)
                V.drain()
                pv = inc(V.engine_nop(), "v")
                wait(A, "v", pv)
                ai = A.activation(out=a_gtincl, in_=a_t1, func=AF.Exp)
                if own:
                    ai = A.activation(out=a_gamrep, in_=a_cumrep, func=AF.Exp)
                pa = inc(ai, "a")
                wait(V, "a", pa)
                V.tensor_sub(out=a_t1, in0=a_gtincl, in1=c_id)
                V.drain()
                pv = inc(V.engine_nop(), "v")

                wait(T, "v", pv)
                transpose128(Kt, a_KT)
                if own:
                    transpose128(Vt, a_VT)
                wait(T, "tpe")
                cmm(a_KT, a_KT, a_kk)
                V.drain()
                V.tensor_mul(out=s_nb[0], in0=a_t1, in1=a_kk)
                V.drain()
                V.tensor_scalar_mul(out=s_nb[0], in0=s_nb[0], scalar1=a_negb)
                V.tensor_scalar_mul(out=a_gk, in0=Kt, scalar1=a_gam)
                V.tensor_scalar_mul(out=a_kw, in0=Kt, scalar1=a_wgt)
                V.drain()
                pv = inc(V.engine_nop(), "v")
                wait(T, "v", pv)
                transpose128(s_nb[0], s_na[0])
                wait(V, "tpe")

                V.tensor_copy(out=a_Z, in_=a_kw)
                if own:
                    V.tensor_copy(out=a_Wm, in_=a_gk)
                V.drain()
                pv = inc(V.engine_nop(), "v")
                if own:
                    wait(T, "v", pv)
                    wait(T, "tpe")
                    cmm(a_VT, QTt, a_PF)
                cur = 0
                for m in range(7):
                    wait(T, "v")
                    cmm(s_na[cur], a_Z, a_Z,
                        extra=lambda d, p: V.tensor_add(out=d, in0=d, in1=p))
                    if own:
                        cmm(s_nb[cur], a_Wm, a_Wm,
                            extra=lambda d, p: V.tensor_add(out=d, in0=d, in1=p))
                        cmm(s_nb[cur], a_PF, a_PF,
                            extra=lambda d, p: V.tensor_add(out=d, in0=d, in1=p))
                    if m < 6:
                        wait(T, "v")
                        cmm(s_nb[cur], s_na[cur], s_na[1 - cur])
                        cmm(s_na[cur], s_nb[cur], s_nb[1 - cur])
                        cur = 1 - cur

                wait(T, "v")
                cmm(a_gk, a_Z, a_ET)
                cmm(a_Z, Vt, a_F)
                cmm(Vt, a_Z, a_FT)
                if own:
                    transpose128(a_Wm, a_WT)

                if own:
                    wait(T, "v")
                    cmm(MTc, QTt, a_MQ)
                    wait(T, "v")
                    wait(T, "tpe")
                    cmm(a_WT, a_MQ, a_BT,
                        extra=lambda d, p: V.scalar_tensor_tensor(
                            out=d, in0=p, scalar=-1.0, in1=a_PF,
                            op0=AL.mult, op1=AL.add))
                    V.drain()
                    V.tensor_mul(out=a_BT, in0=a_BT, in1=a_gtincl)
                    V.drain()
                    V.tensor_scalar_mul(out=a_BT, in0=a_BT, scalar1=bcol)
                    V.drain()
                    pv = inc(V.engine_nop(), "v")
                    wait(T, "v", pv)
                    cmm(Kt, a_BT, a_t2)
                    V.tensor_mul(out=a_t1, in0=a_MQ, in1=a_gamrep)
                    V.drain()
                    V.tensor_add(out=a_t1, in0=a_t1, in1=a_t2)
                    V.drain()
                    V.tensor_mul(out=a_OT, in0=a_t1, in1=GTt)
                    V.drain()
                    pv = inc(V.engine_nop(), "v")
                    wait(T, "v", pv)
                    for half in range(D // 512):
                        guard(T, "bank5")
                        mmO = T.matmul(ps[5][:, 0:512], a_OT,
                                       w_awo[:, half * 512:(half + 1) * 512],
                                       start=True, stop=True)
                        pt = inc(mmO, "t")
                        xr = xres[half % 2]
                        lvs = f"ld_xres{half % 2}"
                        lv = load(xr, d_x1own[t_own * 128:(t_own + 1) * 128,
                                              half * 512:(half + 1) * 512],
                                  key=f"xres{half % 2}")
                        wait(V, lvs, lv)
                        wait(V, "t", pt)
                        sgb = stg1[half]
                        guard(V, f"stg{half}")
                        V.tensor_add(out=sgb, in0=ps[5][:, 0:512], in1=xr)
                        V.drain()
                        V.tensor_add(out=sgb, in0=sgb,
                                     in1=b_o[:, half * 512:(half + 1) * 512])
                        V.drain()
                        pv = inc(V.engine_nop(), "v")
                        used("bank5", "v")
                        used(f"xres{half % 2}", "v")
                        wait(A, "v", pv)
                        store(d_x2own[t_own * 128:(t_own + 1) * 128,
                                      half * 512:(half + 1) * 512], sgb, half)
                        used(f"stg{half}", f"st{half}")

                wait(T, "v")
                cmm(a_ET, Mc, a_t1,
                    extra=lambda d, p: V.scalar_tensor_tensor(
                        out=d, in0=p, scalar=-1.0, in1=a_F, op0=AL.mult, op1=AL.add))
                cmm(Mc, a_ET, a_t2,
                    extra=lambda d, p: V.scalar_tensor_tensor(
                        out=d, in0=p, scalar=-1.0, in1=a_FT, op0=AL.mult, op1=AL.add))
                V.drain()
                V.scalar_tensor_tensor(out=Mn, in0=Mc, scalar=a_gamL, in1=a_t1,
                                       op0=AL.mult, op1=AL.add)
                V.scalar_tensor_tensor(out=MTn, in0=MTc, scalar=a_gamL, in1=a_t2,
                                       op0=AL.mult, op1=AL.add)
                V.drain()
                inc(V.engine_nop(), "v")
                mcur["i"] = nx

            ngroups = NT // GROUP if "attn" in phases else 0
            for g in range(ngroups):
                gtiles = list(range(g * GROUP, (g + 1) * GROUP))
                own_g = gtiles[0] >= OWN0
                wait(A, "t")
                wait(A, "tpm")
                wait(V, "t")
                for si, t in enumerate(gtiles):
                    b = xin[si % 2]
                    lvs = f"ld_xin{si % 2}"
                    lv = load(b[:, 0:D], x1_src(t), key=f"xin{si % 2}")
                    wait(V, lvs, lv)
                    ln_tile(b, scr4, srckey=f"xin{si % 2}")
                    wait(T, "v")
                    for dc in range(DCH):
                        transpose128(scr4[:, dc * 128:(dc + 1) * 128],
                                     a_xh2T[:, dc, si * 128:(si + 1) * 128])
                    wait(V, "tpe")
                wait(T, "tpe")

                for si, t in enumerate(gtiles):
                    pp = ps[si % 2]
                    guard(T, f"bank{si % 2}")
                    for dc in range(DCH):
                        mm = T.matmul(pp[:, 0:258],
                                      a_xh2T[:, dc, si * 128:(si + 1) * 128],
                                      w_awkvab[:, dc, :], start=(dc == 0),
                                      stop=(dc == DCH - 1))
                    pt = inc(mm, "t")
                    wait(V, "t", pt)
                    V.tensor_add(out=scr1[:, 0:256], in0=pp[:, 0:256],
                                 in1=b_kvab[:, 0:256])
                    V.drain()
                    pv = inc(V.engine_nop(), "v")
                    wait(A, "v", pv)
                    A.activation(out=scr2[:, 0:128], in_=scr1[:, 0:128], func=AF.Square,
                                 accum_out=sm_s[:, 0:1])
                    A.drain()
                    A.activation(out=sm_s[:, 1:2], in_=sm_s[:, 0:1], func=AF.Sqrt)
                    A.activation(out=a_acol[:, si:si + 1], in_=pp[:, 256:257],
                                 func=AF.Sigmoid, bias=b_a)
                    ai = A.activation(out=a_bcol[:, si:si + 1], in_=pp[:, 257:258],
                                      func=AF.Sigmoid, bias=b_b)
                    pa = inc(ai, "a")
                    used(f"bank{si % 2}", "a")
                    wait(V, "a", pa)
                    V.tensor_scalar_max(out=sm_s[:, 1:2], in0=sm_s[:, 1:2],
                                        scalar1=1e-12)
                    V.drain()
                    pv = inc(V.engine_nop(), "v")
                    wait(A, "v", pv)
                    V.reciprocal(out=sm_r[:, 0:1], in_=sm_s[:, 1:2])
                    V.drain()
                    V.tensor_scalar_mul(out=a_K[:, si, :], in0=scr1[:, 0:128],
                                        scalar1=sm_r[:, 0:1])
                    V.tensor_copy(out=a_V[:, si, :], in_=scr1[:, 128:256])
                    V.drain()
                    inc(V.engine_nop(), "v")
                if own_g:
                    for si, t in enumerate(gtiles):
                        pp = ps[si % 2]
                        guard(T, f"bank{si % 2}")
                        for dc in range(DCH):
                            T.matmul(pp[:, 0:128], w_awq[:, dc, :],
                                     a_xh2T[:, dc, si * 128:(si + 1) * 128],
                                     start=(dc == 0), stop=(dc == DCH - 1))
                        for dc in range(DCH):
                            mm = T.matmul(pp[:, 128:256], w_awg[:, dc, :],
                                          a_xh2T[:, dc, si * 128:(si + 1) * 128],
                                          start=(dc == 0), stop=(dc == DCH - 1))
                        pt = inc(mm, "t")
                        wait(V, "t", pt)
                        V.tensor_scalar(out=a_QT[:, si, :], in0=pp[:, 0:128],
                                        scalar1=b_q, scalar2=None, op0=AL.add)
                        V.drain()
                        pv = inc(V.engine_nop(), "v")
                        wait(A, "v", pv)
                        ai = A.activation(out=a_GT[:, si, :], in_=pp[:, 128:256],
                                          func=AF.Sigmoid, bias=b_g)
                        pa = inc(ai, "a")
                        used(f"bank{si % 2}", "a")
                        wait(V, "a", pa)

                for si, t in enumerate(gtiles):
                    emit_chunk(si, own_g, (t - OWN0) if own_g else 0)

            wait_all_stores(SY)
            if "ffn2" in phases: emit_ffn(
                tiles=list(range(NOWN)),
                src_x=lambda t: d_x2own[t * 128:(t + 1) * 128, :],
                src_res=lambda t: d_x2own[t * 128:(t + 1) * 128, :],
                dst_of=lambda t: d_out[t * 128:(t + 1) * 128, :],
            )

            if "ffn2" not in phases:
                src_dump = d_x2own if "attn" in phases else d_x1own
                for t in range(0, NOWN * 128, 128):
                    load(xin[0][:, 0:D], src_dump[t:t + 128, :],
                         s="dbg_ld", key="xin0")
                    wait(A, "dbg_ld")
                    store(d_out[t:t + 128, :], xin[0][:, 0:D], 0)
                    used("xin0", "st0")
                    wait(SY, "st0")
            wait_all_stores(SY)

    return nc


def make_host_inputs(cfg, ins):
    NT, OWN0, D, I = cfg["NTILES"], cfg["OWN0"], cfg["D"], cfg["I"]
    NOWN = NT - OWN0
    x = np.ascontiguousarray(np.asarray(ins["L0_attn_x"], dtype=np.float32))
    B = x.shape[0]
    Wk = np.asarray(ins["L0_attn_Wk"], np.float32)
    Wv = np.asarray(ins["L0_attn_Wv"], np.float32)
    Wa = np.asarray(ins["L0_attn_Wa"], np.float32)
    Wb = np.asarray(ins["L0_attn_Wb"], np.float32)
    wkvab = np.ascontiguousarray(np.concatenate([Wk, Wv, Wa, Wb], axis=1))
    bk = np.asarray(ins["L0_attn_bk"], np.float32)
    bv = np.asarray(ins["L0_attn_bv"], np.float32)
    bkvab = np.ascontiguousarray(
        np.broadcast_to(np.concatenate([bk, bv, [0.0, 0.0]]).astype(np.float32)[None, :],
                        (128, 258)))
    common = {
        "wg": np.ascontiguousarray(np.asarray(ins["L0_ffn_w_gate"], np.float32)),
        "wu": np.ascontiguousarray(np.asarray(ins["L0_ffn_w_up"], np.float32)),
        "wd": np.ascontiguousarray(np.asarray(ins["L0_ffn_w_down"], np.float32)),
        "awq": np.ascontiguousarray(np.asarray(ins["L0_attn_Wq"], np.float32)),
        "awg": np.ascontiguousarray(np.asarray(ins["L0_attn_Wg"], np.float32)),
        "awkvab": wkvab,
        "awo": np.ascontiguousarray(np.asarray(ins["L0_attn_Wo"], np.float32)),
        "bkvab": bkvab,
        "bqcol": np.ascontiguousarray(np.asarray(ins["L0_attn_bq"], np.float32)[:, None]),
        "bgcol": np.ascontiguousarray(np.asarray(ins["L0_attn_bg"], np.float32)[:, None]),
        "bacol": np.ascontiguousarray(
            np.broadcast_to(np.asarray(ins["L0_attn_ba"], np.float32), (128,))[:, None]),
        "bbcol": np.ascontiguousarray(
            np.broadcast_to(np.asarray(ins["L0_attn_bb"], np.float32), (128,))[:, None]),
        "borep": np.ascontiguousarray(
            np.broadcast_to(np.asarray(ins["L0_attn_bo"], np.float32)[None, :], (128, D))),
        "c_ut": np.triu(np.ones((128, 128), np.float32)),
        "c_id": np.eye(128, dtype=np.float32),
        "c_idb": np.eye(128, dtype=np.float32).astype(ml_dtypes.bfloat16),
        "c_aiu": ((np.triu(np.ones((128, 128), np.float32)) - 1.0) * 1e9).astype(np.float32),
        "c_ones1": np.ones((1, 128), np.float32),
    }
    half = OWN0 * 128
    in_maps = []
    for c in range(2 * B):
        bi, hi = c // 2, c % 2
        xa = np.zeros((NT * 128, D), np.float32)
        if hi == 0:
            xa[half:half + NOWN * 128] = x[bi, 0:NOWN * 128]
        else:
            xa[0:half] = x[bi, 0:half]
            xa[half:half + NOWN * 128] = x[bi, half:half + NOWN * 128]
        in_maps.append(dict(common, xall=np.ascontiguousarray(xa)))
    return in_maps


def assemble(cfg, results, B, T):
    NOWN = cfg["NTILES"] - cfg["OWN0"]
    D = cfg["D"]
    out = np.zeros((B, T, D), np.float32)
    half = cfg["OWN0"] * 128
    for c, r in enumerate(results):
        bi, hi = c // 2, c % 2
        if hi == 0:
            out[bi, 0:NOWN * 128] = r["yout"]
        else:
            out[bi, half:half + NOWN * 128] = r["yout"]
    return out


from concourse.bass_utils import run_bass_kernel_spmd

_NC_CACHE = {}


def kernel(**inputs):
    cfg = FULL
    if "nc" not in _NC_CACHE:
        _NC_CACHE["nc"] = build_nc(cfg)
    nc = _NC_CACHE["nc"]
    in_maps = make_host_inputs(cfg, inputs)
    res = run_bass_kernel_spmd(nc, in_maps, core_ids=list(range(8)))
    x = np.asarray(inputs["L0_attn_x"])
    return assemble(cfg, res.results, x.shape[0], x.shape[1])


# version 18
# speedup vs baseline: 1.3320x; 1.0005x over previous
import contextlib
import numpy as np
import ml_dtypes
import concourse.bass as bass
import concourse.mybir as mybir

F32 = mybir.dt.float32
BF16 = mybir.dt.bfloat16
I32 = mybir.dt.int32
AF = mybir.ActivationFunctionType
AL = mybir.AluOpType
AX = mybir.AxisListType

FULL = dict(NTILES=32, OWN0=16, D=1024, I=2816, ITILE=256, SWEEP=4, GROUP=2)
TINY = dict(NTILES=2, OWN0=1, D=1024, I=2816, ITILE=256, SWEEP=2, GROUP=1)
MID = dict(NTILES=8, OWN0=4, D=1024, I=2816, ITILE=256, SWEEP=4, GROUP=2)


def build_nc(cfg, phases=("prep", "ffn1", "attn", "ffn2")):
    NT, OWN0, D, I = cfg["NTILES"], cfg["OWN0"], cfg["D"], cfg["I"]
    ITILE, SWEEP, GROUP = cfg["ITILE"], cfg["SWEEP"], cfg["GROUP"]
    DCH, ICH = D // 128, I // 128
    NIT = I // ITILE
    NOWN = NT - OWN0
    IBLK = ITILE // 128
    SCRW = D
    SLOTW = SWEEP * ITILE
    QW = max(2 * D, 2 * SLOTW)

    nc = bass.Bass("TRN2", target_bir_lowering=False, debug=False)

    d_x = nc.dram_tensor("xall", [NT * 128, D], F32, kind="ExternalInput").ap()
    d_out = nc.dram_tensor("yout", [NOWN * 128, D], F32, kind="ExternalOutput").ap()
    d_wg = nc.dram_tensor("wg", [I, D], F32, kind="ExternalInput").ap()
    d_wu = nc.dram_tensor("wu", [I, D], F32, kind="ExternalInput").ap()
    d_wd = nc.dram_tensor("wd", [D, I], F32, kind="ExternalInput").ap()
    d_awq = nc.dram_tensor("awq", [D, 128], F32, kind="ExternalInput").ap()
    d_awg = nc.dram_tensor("awg", [D, 128], F32, kind="ExternalInput").ap()
    d_awkvab = nc.dram_tensor("awkvab", [D, 258], F32, kind="ExternalInput").ap()
    d_awo = nc.dram_tensor("awo", [128, D], F32, kind="ExternalInput").ap()
    d_bkvab = nc.dram_tensor("bkvab", [128, 258], F32, kind="ExternalInput").ap()
    d_bq = nc.dram_tensor("bqcol", [128, 1], F32, kind="ExternalInput").ap()
    d_bg = nc.dram_tensor("bgcol", [128, 1], F32, kind="ExternalInput").ap()
    d_ba = nc.dram_tensor("bacol", [128, 1], F32, kind="ExternalInput").ap()
    d_bb = nc.dram_tensor("bbcol", [128, 1], F32, kind="ExternalInput").ap()
    d_bo = nc.dram_tensor("borep", [128, D], F32, kind="ExternalInput").ap()
    d_cut = nc.dram_tensor("c_ut", [128, 128], F32, kind="ExternalInput").ap()
    d_cid = nc.dram_tensor("c_id", [128, 128], F32, kind="ExternalInput").ap()
    d_cidb = nc.dram_tensor("c_idb", [128, 128], BF16, kind="ExternalInput").ap()
    d_caiu = nc.dram_tensor("c_aiu", [128, 128], F32, kind="ExternalInput").ap()
    d_cones = nc.dram_tensor("c_ones1", [1, 128], F32, kind="ExternalInput").ap()

    d_x1ctx = nc.dram_tensor("x1ctx", [max(OWN0, 1) * 128, D], F32, kind="Internal").ap()
    d_x1own = nc.dram_tensor("x1own", [NOWN * 128, D], F32, kind="Internal").ap()
    d_x2own = nc.dram_tensor("x2own", [NOWN * 128, D], F32, kind="Internal").ap()
    d_stg = [nc.dram_tensor(f"stash_g{h}", [DCH, ICH, 128, 128], BF16,
                            kind="Internal").ap() for h in range(2)]
    d_stu = [nc.dram_tensor(f"stash_u{h}", [DCH, ICH, 128, 128], BF16,
                            kind="Internal").ap() for h in range(2)]
    d_std = [nc.dram_tensor(f"stash_d{h}", [ICH, DCH, 128, 128], BF16,
                            kind="Internal").ap() for h in range(2)]

    es = contextlib.ExitStack()
    with es:
        def sb(name, shape, dt=F32):
            return es.enter_context(nc.sbuf_tensor(name, shape, dt)).ap()

        c_ut = sb("c_ut_s", [128, 128])
        c_id = sb("c_id_s", [128, 128])
        c_aiu = sb("c_aiu_s", [128, 128])
        c_ones = sb("c_ones_s", [1, 128])
        w_awq = sb("w_awq", [128, DCH, 128])
        w_awg = sb("w_awg", [128, DCH, 128])
        w_awkvab = sb("w_awkvab", [128, DCH, 258])
        w_awo = sb("w_awo", [128, D])
        b_kvab = sb("b_kvab_s", [128, 258])
        b_q = sb("b_q_s", [128, 1])
        b_g = sb("b_g_s", [128, 1])
        b_a = sb("b_a_s", [128, 1])
        b_b = sb("b_b_s", [128, 1])
        b_o = sb("b_o_s", [128, D])
        c_eps = sb("c_eps", [128, 1])

        xin = [sb("xin0", [128, QW]), sb("xin1", [128, SCRW])]
        scr1 = sb("scr1", [128, QW])
        scr2 = sb("scr2", [128, QW])
        scr3 = sb("scr3", [128, QW])
        scr4 = sb("scr4", [128, SCRW])
        sm_m = sb("sm_m", [128, QW // 16])
        sm_s = sb("sm_s", [128, QW // 16])
        sm_r = sb("sm_r", [128, QW // 16])
        xqT = [[sb(f"xqT{t}_{h}", [128, DCH, 128], BF16) for h in range(2)]
               for t in range(SWEEP)]
        wcol = [[sb(f"wcol{i}_{h}", [128, 2, DCH, ITILE], BF16) for h in range(2)]
                for i in range(2)]
        c_idb = sb("c_idb_s", [128, 128], BF16)
        sp_h = sb("sp_h", [128, QW], BF16)
        sp_l = sb("sp_l", [128, QW], BF16)
        hb_all = sb("hb_all", [128, 4 * SLOTW])
        hb = [hb_all[:, i * SLOTW:(i + 1) * SLOTW] for i in range(4)]
        hsilu = [sb(f"hsilu{i}", [128, ITILE]) for i in range(2)]
        hq_m, hq_s, hq_r = sm_m, sm_s, sm_r
        hqT = [[sb(f"hqT{t}_{h}", [128, ICH, 128], BF16) for h in range(2)]
               for t in range(SWEEP)]
        wdn = [[sb(f"wdn{i}_{h}", [128, 512], BF16) for h in range(2)]
               for i in range(3)]
        xres = [sb(f"xres{i}", [128, 512]) for i in range(2)]
        stg1 = [sb(f"stg1_{i}", [128, 512]) for i in range(4)]

        a_xh2T = hb_all[:, 0:DCH * GROUP * 128].rearrange(
            "p (d n) -> p d n", n=GROUP * 128)
        hpool = [hqT[si][h].bitcast(F32).rearrange("p a n -> p (a n)")
                 for si in (0, 1) for h in (0, 1)]
        hp_off = [0, 0, 0, 0]

        def carve(pi, w):
            c0 = hp_off[pi]
            hp_off[pi] = c0 + w
            v = hpool[pi][:, c0:c0 + w]
            assert hp_off[pi] <= ICH * 64, "hqT pool overflow"
            return v

        a_K = carve(0, GROUP * 128).rearrange("p (g n) -> p g n", n=128)
        a_V = carve(0, GROUP * 128).rearrange("p (g n) -> p g n", n=128)
        a_QT = carve(0, GROUP * 128).rearrange("p (g n) -> p g n", n=128)
        a_GT = carve(0, GROUP * 128).rearrange("p (g n) -> p g n", n=128)
        a_KT = carve(0, 128)
        a_VT = carve(0, 128)
        a_acol = sb("a_acol", [128, GROUP])
        a_bcol = sb("a_bcol", [128, GROUP])
        a_la = sb("a_la", [128, 1])
        a_cum = sb("a_cum", [128, 1])
        a_cumrow = sb("a_cumrow", [1, 128])
        a_gam = sb("a_gam", [128, 1])
        a_gamL = sb("a_gamL", [128, 1])
        a_wgt = sb("a_wgt", [128, 1])
        a_negb = sb("a_negb", [128, 1])
        a_cumrep = carve(1, 128)
        a_gamrep = carve(1, 128)
        a_gtincl = carve(1, 128)
        a_kk = carve(1, 128)
        s_na = [carve(1, 128) for i in range(2)]
        s_nb = [carve(1, 128) for i in range(2)]
        a_gk = carve(1, 128)
        a_kw = carve(1, 128)
        a_Z = carve(1, 128)
        a_Wm = carve(2, 128)
        a_PF = carve(2, 128)
        a_WT = carve(2, 128)
        a_ET = carve(2, 128)
        a_F = carve(2, 128)
        a_FT = carve(2, 128)
        a_M = [carve(2, 128) for i in range(2)]
        a_MT = [carve(2, 128) for i in range(2)]
        a_MQ = carve(3, 128)
        a_BT = carve(3, 128)
        a_OT = carve(3, 128)
        a_t1 = carve(3, 128)
        a_t2 = carve(3, 128)

        ps = [es.enter_context(nc.psum_tensor(f"ps{i}", [128, 512], F32)).ap()
              for i in range(8)]

        sems, cnt = {}, {}

        def sem(name):
            if name not in sems:
                sems[name] = es.enter_context(nc.semaphore(name))
                cnt[name] = 0
            return sems[name]

        def inc(instr, name, amt=1):
            s = sem(name)
            cnt[name] += amt
            instr.then_inc(s, amt)
            return cnt[name]

        def wait(eng, name, val=None):
            if name not in cnt and val is None:
                return
            s = sem(name)
            v = cnt[name] if val is None else val
            if v > 0:
                eng.wait_ge(s, v)

        V, A, T, G, SY = nc.vector, nc.scalar, nc.tensor, nc.gpsimd, nc.sync

        lastuse = {}

        def guard(eng, key):
            if key in lastuse:
                sname, val = lastuse[key]
                wait(eng, sname, val)

        def used(key, sname):
            sem(sname)
            lastuse[key] = (sname, cnt[sname])

        def load(out, in_, s=None, key=None):
            if key is not None:
                guard(SY, key)
            if s is None:
                s = "ld_" + (key if key is not None else "c")
            return inc(SY.dma_start(out=out, in_=in_), s, 16)

        def store(out, in_, slot):
            return inc(A.dma_start(out=out, in_=in_), f"st{slot}", 16)

        def wait_all_stores(eng):
            for i in range(4):
                wait(eng, f"st{i}")

        with nc.Block():
            load(c_ut, d_cut); load(c_id, d_cid); load(c_aiu, d_caiu)
            load(c_idb, d_cidb)
            load(c_ones, d_cones)
            load(w_awq, d_awq.rearrange("(c p) n -> p c n", p=128))
            load(w_awg, d_awg.rearrange("(c p) n -> p c n", p=128))
            load(w_awkvab, d_awkvab.rearrange("(c p) n -> p c n", p=128))
            load(w_awo, d_awo)
            load(b_kvab, d_bkvab); load(b_q, d_bq); load(b_g, d_bg)
            load(b_a, d_ba); load(b_b, d_bb); load(b_o, d_bo)
            CONST_LD = cnt["ld_c"]
            for eng in (V, A, T):
                wait(eng, "ld_c", CONST_LD)
            V.memset(c_eps, 1e-5)
            V.drain()

            SH_MUL = float(np.float32(1.0) / np.float32(6.0)) * 0.5
            SH_MIN = float(np.float32(1e-12)) * 0.5

            def quant(src, out, F, mb, ssb, rb, t1, t2, t3, srckey=None):
                nb = F // 16
                v3 = src.rearrange("p (b k) -> p b k", k=16) if src.shape[1] == F \
                    else src
                V.tensor_reduce(out=mb[:, :nb], in_=v3, axis=AX.X, op=AL.max,
                                apply_absolute_value=True)
                V.tensor_tensor(out=t1[:, :F], in0=src, in1=src, op=AL.mult)
                V.drain()
                V.tensor_scalar(out=ssb[:, :nb], in0=mb[:, :nb], scalar1=SH_MUL,
                                scalar2=SH_MIN, op0=AL.mult, op1=AL.max)
                V.drain()
                V.reciprocal(out=rb[:, :nb], in_=ssb[:, :nb])
                V.drain()
                V.tensor_tensor(out=mb[:, :nb], in0=rb[:, :nb], in1=rb[:, :nb],
                                op=AL.mult)
                V.drain()
                mb3 = mb[:, :nb].broadcast_to([128, nb, 16])
                t1v = t1[:, :F].rearrange("p (b k) -> p b k", k=16)
                V.tensor_tensor(out=t1v, in0=t1v, in1=mb3, op=AL.mult)
                V.drain()
                c1 = t3.bitcast(BF16)[:, 0:F]
                c2 = t3.bitcast(BF16)[:, F:2 * F]
                c3 = t2.bitcast(BF16)[:, 0:F]
                c4 = t2.bitcast(BF16)[:, F:2 * F]

                def cmp(dst, thr2, w):
                    V.tensor_scalar(out=dst, in0=t1[:, :F], scalar1=thr2,
                                    scalar2=w, op0=AL.is_gt, op1=AL.mult)

                guard(V, "scr3g")
                cmp(c1, 0.25, 1.0)
                guard(V, "scr2g")
                cmp(c3, 2.25, 1.0)
                cmp(c4, 6.25, 1.0)
                cmp(c2, 12.25, 1.0)
                V.drain()
                V.tensor_add(out=c1, in0=c1, in1=c3)
                V.tensor_add(out=c4, in0=c4, in1=c2)
                V.drain()
                cmp(c3, 25.0, 2.0)
                cmp(c2, 49.0, 2.0)
                V.drain()
                V.tensor_add(out=c1, in0=c1, in1=c4)
                V.tensor_add(out=c3, in0=c3, in1=c2)
                V.drain()
                cmp(c2, 100.0, 4.0)
                V.tensor_add(out=c1, in0=c1, in1=c3)
                V.drain()
                V.tensor_add(out=c1, in0=c1, in1=c2)
                V.drain()
                sh3 = ssb[:, :nb].broadcast_to([128, nb, 16])
                t2v = t2[:, :F].rearrange("p (b k) -> p b k", k=16)
                mm = V.tensor_tensor(out=t2v,
                                     in0=c1.rearrange("p (b k) -> p b k", k=16),
                                     in1=sh3, op=AL.mult)
                pv = inc(mm, "v")
                sgb = c2
                wait(A, "v", pv)
                wait(A, "tpm")
                A.copy(out=sp_h[:, :F], in_=t2[:, :F])
                ai = A.activation(out=sgb, in_=src, func=AF.Sign)
                pa = inc(ai, "a")
                if srckey is not None:
                    used(srckey, "a")
                wait(G, "a", pa)
                G.tensor_tensor(out=sp_l[:, :F], in0=t2[:, :F], in1=sp_h[:, :F],
                                op=AL.subtract)
                G.drain()
                G.tensor_mul(out=sp_h[:, :F], in0=sp_h[:, :F], in1=sgb)
                gi = G.tensor_mul(out=sp_l[:, :F], in0=sp_l[:, :F], in1=sgb)
                inc(gi, "gq")
                used("scr2g", "gq")
                used("scr3g", "gq")
                return pv

            tp_state = {"n": 0}

            def transpose128(src_ap, dst_ap, ident=None):
                bk = 6 + (tp_state["n"] % 2)
                tp_state["n"] += 1
                key = f"bank{bk}"
                if ident is None:
                    pst = ps[bk][:, 0:128]
                else:
                    pst = ps[bk].bitcast(BF16)[:, 0:128]
                guard(T, key)
                mm = T.transpose(pst, src_ap, c_id if ident is None else ident)
                pt = inc(mm, "tpm")
                wait(A, "tpm", pt)
                inc(A.copy(out=dst_ap, in_=pst), "tpe")
                used(key, "tpe")
                return mm

            def transpose_hl(colslice, dsth, dstl):
                transpose128(sp_h[:, colslice], dsth, ident=c_idb)
                return transpose128(sp_l[:, colslice], dstl, ident=c_idb)

            def transpose_batch(pairs):
                assert len(pairs) <= 8
                bk = 6 + (tp_state["n"] % 2)
                tp_state["n"] += 1
                key = f"bank{bk}"
                pst = ps[bk].bitcast(BF16)
                guard(T, key)
                for i, (src, _) in enumerate(pairs):
                    mm = T.transpose(pst[:, i * 128:(i + 1) * 128], src, c_idb)
                pt = inc(mm, "tpm")
                wait(A, "tpm", pt)
                for i, (_, dst) in enumerate(pairs):
                    ai = A.copy(out=dst, in_=pst[:, i * 128:(i + 1) * 128])
                inc(ai, "tpe")
                used(key, "tpe")

            def transpose_batch_wide(srcs, dst):
                assert len(srcs) <= 8
                bk = 6 + (tp_state["n"] % 2)
                tp_state["n"] += 1
                key = f"bank{bk}"
                pst = ps[bk].bitcast(BF16)
                guard(T, key)
                for i, src in enumerate(srcs):
                    mm = T.transpose(pst[:, i * 128:(i + 1) * 128], src, c_idb)
                pt = inc(mm, "tpm")
                wait(A, "tpm", pt)
                inc(A.copy(out=dst, in_=pst[:, 0:len(srcs) * 128]), "tpe")
                used(key, "tpe")

            def ln_tile(src, out, srckey=None):
                V.tensor_reduce(out=sm_m[:, 0:1], in_=src[:, 0:D], axis=AX.X, op=AL.add)
                V.drain()
                V.tensor_scalar_mul(out=sm_m[:, 0:1], in0=sm_m[:, 0:1], scalar1=1.0 / D)
                V.drain()
                guard(V, "scr1sign")
                vi = V.tensor_scalar(out=scr1[:, 0:D], in0=src[:, 0:D],
                                     scalar1=sm_m[:, 0:1],
                                     scalar2=None, op0=AL.subtract)
                pv = inc(vi, "v")
                V.drain()
                if srckey is not None:
                    used(srckey, "v")
                wait(A, "v", pv)
                guard(A, "scr2g")
                A.activation(out=scr2[:, 0:D], in_=scr1[:, 0:D], func=AF.Square,
                             accum_out=sm_s[:, 0:1])
                A.drain()
                ai = A.activation(out=sm_r[:, 0:1], in_=sm_s[:, 0:1], func=AF.Sqrt,
                                  scale=1.0 / D, bias=c_eps)
                pa = inc(ai, "a")
                wait(V, "a", pa)
                V.reciprocal(out=sm_r[:, 1:2], in_=sm_r[:, 0:1])
                V.drain()
                vi = V.tensor_scalar_mul(out=out[:, 0:D], in0=scr1[:, 0:D],
                                         scalar1=sm_r[:, 1:2])
                pv = inc(vi, "v")
                V.drain()
                return pv

            stgb = [stg1[h].bitcast(BF16) for h in range(2)]

            def prep_group(segs, rt2):
                b = xin[0]
                lvs = "ld_xin0"
                off = 0
                for dram_ap, Fq, _ in segs:
                    lv = load(b[:, off:off + Fq], dram_ap, key="xin0")
                    off += Fq
                Ftot = off
                wait(V, lvs, lv)
                quant(b[:, :Ftot], b[:, :Ftot], Ftot, sm_m, sm_s, sm_r,
                      scr1, scr2, scr3, srckey="xin0")
                wait(T, "gq")
                off = 0
                for _, Fq, wdst in segs:
                    nbk = Fq // 128
                    for h in range(2):
                        sp = sp_h if h == 0 else sp_l
                        for p0 in range(0, nbk, 8):
                            pw = min(8, nbk - p0)
                            sgb = stgb[h]
                            guard(A, f"stg{h}")
                            transpose_batch_wide(
                                [sp[:, off + (p0 + j) * 128:off + (p0 + j + 1) * 128]
                                 for j in range(pw)],
                                sgb[:, 0:pw * 128])
                            wait(A, "tpe")
                            store(wdst(h, p0, pw),
                                  sgb[:, 0:pw * 128]
                                  .rearrange("p (b n) -> p b n", n=128), h)
                            used(f"stg{h}", f"st{h}")
                    off += Fq

            rt2 = [0]
            wg_st = {}
            wd_st = None
            if "prep" in phases:
                for rt in range(DCH):
                    c0 = 0
                    while c0 < I:
                        cw = min(2 * D, I - c0)
                        prep_group([
                            (d_wd[rt * 128:(rt + 1) * 128, c0:c0 + cw], cw,
                             lambda h, p0, pw, rt=rt, c0=c0:
                             d_std[h][c0 // 128 + p0:c0 // 128 + p0 + pw, rt]
                             .rearrange("b p n -> p b n"))], rt2[0])
                        rt2[0] += 1
                        c0 += cw
                wd_st = (cnt.get("st0", 0), cnt.get("st1", 0))

                def emit_wgwu(it0):
                    for stash, wsrc in ((d_stg, d_wg), (d_stu, d_wu)):
                        prep_group([
                            (wsrc[rt * 128:(rt + 1) * 128, :], D,
                             lambda h, p0, pw, rt=rt, stash=stash:
                             stash[h][p0:p0 + pw, rt].rearrange("b p n -> p b n"))
                            for rt in (2 * it0, 2 * it0 + 1) if rt < ICH], rt2[0])
                        rt2[0] += 1
                    wg_st[it0] = (cnt.get("st0", 0), cnt.get("st1", 0))

                def sweep0_side(it):
                    todo = (0, 1) if it == 0 else (it + 1,)
                    for it0 in todo:
                        if it0 < NIT and it0 not in wg_st:
                            emit_wgwu(it0)
            else:
                def sweep0_side(it):
                    pass

            def emit_ffn(tiles, src_x, src_res, dst_of, side=None):
                nsw = (len(tiles) + SWEEP - 1) // SWEEP

                def emit_pre(stiles):
                    wait(A, "t")
                    for si, t in enumerate(stiles):
                        b = xin[si % 2]
                        lvs = f"ld_xin{si % 2}"
                        lv = load(b[:, 0:D], src_x(t), key=f"xin{si % 2}")
                        wait(V, lvs, lv)
                        ln_tile(b, scr4, srckey=f"xin{si % 2}")
                        quant(scr4[:, 0:D], scr4[:, 0:D], D, sm_m, sm_s, sm_r,
                              scr1, scr2, scr3)
                        wait(T, "gq")
                        transpose_batch_wide(
                            [sp_h[:, dc * 128:(dc + 1) * 128] for dc in range(DCH)],
                            xqT[si][0].rearrange("p a n -> p (a n)"))
                        transpose_batch_wide(
                            [sp_l[:, dc * 128:(dc + 1) * 128] for dc in range(DCH)],
                            xqT[si][1].rearrange("p a n -> p (a n)"))
                        wait(V, "tpe")
                    wait(T, "tpe")

                def emit_gateup(stiles, side_work=None):
                    ns = len(stiles)
                    pending_it = []

                    def flush_items(items):
                        hs0 = items[0][1]
                        F0 = sum(nsX * ITILE for _, _, nsX in items)
                        src = hb_all[:, hs0 * SLOTW:hs0 * SLOTW + F0]
                        wait(V, "g")
                        wait(A, "g")
                        quant(src, src, F0, hq_m, hq_s, hq_r, scr1, scr2, scr3)
                        wait(T, "gq")
                        pairs_h, pairs_l = [], []
                        for k, (itX, hsX, nsX) in enumerate(items):
                            for si0 in range(nsX):
                                for bi in range(IBLK):
                                    col = k * SLOTW + si0 * ITILE + bi * 128
                                    pairs_h.append((sp_h[:, col:col + 128],
                                                    hqT[si0][0][:, itX * IBLK + bi, :]))
                                    pairs_l.append((sp_l[:, col:col + 128],
                                                    hqT[si0][1][:, itX * IBLK + bi, :]))
                        for pp_ in (pairs_h, pairs_l):
                            for k0 in range(0, len(pp_), 8):
                                transpose_batch(pp_[k0:k0 + 8])
                        for _, hsX, _ in items:
                            used(f"hb{hsX}", "tpm")

                    def load_wcol(slot, it0):
                        if it0 in wg_st:
                            wait(SY, "st0", wg_st[it0][0])
                            wait(SY, "st1", wg_st[it0][1])
                        for mi, stash in enumerate((d_stg, d_stu)):
                            for h in range(2):
                                v5 = wcol[slot][h].rearrange(
                                    "p m d (b n) -> p m d b n", b=IBLK)
                                for bb in range(IBLK):
                                    load(v5[:, mi, :, bb, :],
                                         stash[h][:, it0 * IBLK + bb]
                                         .rearrange("d p n -> p d n"),
                                         s=f"wld{slot}", key=f"wcol{slot}")

                    for it in range(NIT):
                        if side_work is not None:
                            side_work(it)
                        if it == 0:
                            load_wcol(0, 0)
                        if it + 1 < NIT:
                            load_wcol((it + 1) % 2, it + 1)
                        wb = wcol[it % 2]
                        wait(T, f"wld{it % 2}")
                        hs = it % 4
                        for si in range(ns):
                            pp = ps[si % 2]
                            guard(T, f"bank{si % 2}")
                            passes = ((0, 0), (0, 1), (1, 0))
                            np_ = len(passes)
                            for mi2 in range(2):
                                reg = pp[:, mi2 * ITILE:(mi2 + 1) * ITILE]
                                k = 0
                                for xh_, wh_ in passes:
                                    for dc in range(DCH):
                                        mm = T.matmul(
                                            reg, xqT[si][xh_][:, dc, :],
                                            wb[wh_][:, mi2, dc, :],
                                            start=(k == 0),
                                            stop=(k == np_ * DCH - 1))
                                        k += 1
                            pt = inc(mm, "t")
                            if si == ns - 1:
                                used(f"wcol{it % 2}", "t")
                            wait(A, "t", pt)
                            hp = si % 2
                            guard(A, f"hsilu{hp}")
                            A.activation(out=hsilu[hp], in_=pp[:, 0:ITILE],
                                         func=AF.Silu)
                            hu = xres[hp]
                            guard(A, f"xres{hp}")
                            ai = A.copy(out=hu[:, 0:ITILE], in_=pp[:, ITILE:2 * ITILE])
                            pa = inc(ai, "a")
                            used(f"bank{si % 2}", "a")
                            wait(G, "a", pa)
                            if si == 0:
                                guard(G, f"hb{hs}")
                            gi = G.tensor_mul(out=hb[hs][:, si * ITILE:(si + 1) * ITILE],
                                              in0=hsilu[hp], in1=hu[:, 0:ITILE])
                            pg = inc(gi, "g")
                            used(f"xres{hp}", "g")
                            used(f"hsilu{hp}", "g")
                        pending_it.append((it, hs, ns))
                        if len(pending_it) >= 4:
                            flush_items([pending_it.pop(0), pending_it.pop(0)])
                    while pending_it:
                        if (len(pending_it) >= 2 and pending_it[0][1] % 2 == 0
                                and pending_it[1][1] == pending_it[0][1] + 1):
                            flush_items([pending_it.pop(0), pending_it.pop(0)])
                        else:
                            flush_items([pending_it.pop(0)])
                    wait(T, "tpe")

                def emit_down(stiles):
                    ns = len(stiles)
                    if wd_st is not None:
                        wait(SY, "st0", wd_st[0])
                        wait(SY, "st1", wd_st[1])
                    if True:
                        htiles = stiles
                        for dt in range(D // 512):
                            pts = []

                            def load_wdn(ic0):
                                sl = ic0 % 3
                                for h in range(2):
                                    load(wdn[sl][h].rearrange("p (b n) -> p b n", b=4),
                                         d_std[h][ic0, dt * 4:(dt + 1) * 4]
                                         .rearrange("b p n -> p b n"),
                                         s=f"dld{sl}", key=f"wdn{sl}")

                            load_wdn(0)
                            load_wdn(1)
                            for ic in range(ICH):
                                wbD = wdn[ic % 3]
                                lv = cnt[f"dld{ic % 3}"]
                                if ic + 2 < ICH:
                                    load_wdn(ic + 2)
                                wait(T, f"dld{ic % 3}", lv)
                                mms = []
                                for si2 in range(len(htiles)):
                                    si = si2
                                    pd = ps[2 + si2]
                                    if ic == 0:
                                        guard(T, f"bank{2 + si2}")
                                    for pi, (xh_, wh_) in enumerate(
                                            ((0, 0), (0, 1), (1, 0))):
                                        mms.append(T.matmul(
                                            pd[:, :512], hqT[si][xh_][:, ic, :],
                                            wbD[wh_], start=(ic == 0 and pi == 0),
                                            stop=(ic == ICH - 1 and pi == 2)))
                                inc(mms[-1], f"dfree{ic % 3}")
                                used(f"wdn{ic % 3}", f"dfree{ic % 3}")
                                if ic == ICH - 1:
                                    for si2 in range(len(htiles)):
                                        m_ = mms[si2 * 3 + 2]
                                        if si2 == len(htiles) - 1:
                                            pts.append((f"dfree{ic % 3}",
                                                        cnt[f"dfree{ic % 3}"]))
                                        else:
                                            pts.append(("t", inc(m_, "t")))
                            for si2, t in enumerate(htiles):
                                pd = ps[2 + si2]
                                xr = xres[si2 % 2]
                                lvs = f"ld_xres{si2 % 2}"
                                lv = load(xr, src_res(t)[:, dt * 512:(dt + 1) * 512],
                                          key=f"xres{si2 % 2}")
                                wait(A, pts[si2][0], pts[si2][1])
                                sl = si2
                                sgb = stg1[sl]
                                guard(A, f"stg{sl}")
                                ai = A.activation(out=sgb, in_=pd[:, :512],
                                                  func=AF.Copy, scale=0.5)
                                pa = inc(ai, "a")
                                used(f"bank{2 + si2}", "a")
                                wait(G, "a", pa)
                                wait(G, lvs, lv)
                                pg = inc(G.tensor_add(out=sgb, in0=sgb, in1=xr), "g")
                                used(f"xres{si2 % 2}", "g")
                                wait(A, "g", pg)
                                store(dst_of(t)[:, dt * 512:(dt + 1) * 512], sgb, sl)
                                used(f"stg{sl}", f"st{sl}")

                for sw in range(nsw):
                    stiles = stiles_of = tiles[sw * SWEEP:(sw + 1) * SWEEP]
                    if sw == 0:
                        emit_pre(stiles)
                    emit_gateup(stiles, side if sw == 0 else None)
                    if sw + 1 < nsw:
                        emit_pre(tiles[(sw + 1) * SWEEP:(sw + 2) * SWEEP])
                    emit_down(stiles)
                wait_all_stores(SY)

            if "ffn1" in phases: emit_ffn(
                tiles=list(range(NT)),
                side=sweep0_side,
                src_x=lambda t: d_x[t * 128:(t + 1) * 128, :],
                src_res=lambda t: d_x[t * 128:(t + 1) * 128, :],
                dst_of=lambda t: (d_x1ctx[t * 128:(t + 1) * 128, :] if t < OWN0 else
                                  d_x1own[(t - OWN0) * 128:(t - OWN0 + 1) * 128, :]),
            )

            def x1_src(t):
                return (d_x1ctx[t * 128:(t + 1) * 128, :] if t < OWN0 else
                        d_x1own[(t - OWN0) * 128:(t - OWN0 + 1) * 128, :])

            for nm_ in ("t", "dfree0", "dfree1", "dfree2"):
                wait(V, nm_)
            V.memset(a_M[0], 0.0)
            V.memset(a_MT[0], 0.0)
            V.drain()
            inc(V.engine_nop(), "v")
            mcur = {"i": 0}

            cs_state = {"n": 0}

            def cslot():
                bk = 2 + (cs_state["n"] % 2)
                cs_state["n"] += 1
                key = f"bank{bk}"
                pst = ps[bk][:, 0:128]
                guard(T, key)
                return pst, key

            def cmm(lhsT, rhs, dst, extra=None, n=128):
                pst, key = cslot()
                pm = inc(T.matmul(pst[:, 0:n], lhsT, rhs, start=True, stop=True), "csm")
                wait(V, "csm", pm)
                if extra is None:
                    vi = V.tensor_copy(out=dst, in_=pst[:, 0:n])
                else:
                    vi = extra(dst, pst[:, 0:n])
                if vi is None:
                    V.drain()
                    vi = V.engine_nop()
                inc(vi, "v")
                used(key, "v")

            def emit_chunk(si, own, t_own):
                nx = 1 - mcur["i"]
                Mc, MTc = a_M[mcur["i"]], a_MT[mcur["i"]]
                Mn, MTn = a_M[nx], a_MT[nx]
                acol = a_acol[:, si:si + 1]
                bcol = a_bcol[:, si:si + 1]
                Kt = a_K[:, si, :]
                Vt = a_V[:, si, :]
                QTt = a_QT[:, si, :]
                GTt = a_GT[:, si, :]

                wait(A, "v", cnt["v"])
                ai = ai = ai = ai = ai = ai = ai = A.activation(out=a_la, in_=acol, func=AF.Ln)
                pa = inc(ai, "a")
                wait(T, "a", pa)
                cmm(c_ut, a_la, a_cum, n=1)
                wait(T, "v")
                pst, key = cslot()
                ptr = inc(T.transpose(pst[0:1, 0:128], a_cum, c_id), "csm")
                wait(V, "csm", ptr)
                V.tensor_copy(out=a_cumrow, in_=pst[0:1, 0:128])
                V.drain()
                pv = inc(V.engine_nop(), "v")
                used(key, "v")
                wait(T, "v", pv)
                cmm(c_ones, a_cumrow, a_cumrep)
                pv = cnt["v"]
                wait(A, "v", pv)
                A.activation(out=a_gam, in_=a_cum, func=AF.Exp)
                ai = A.activation(out=a_gamL, in_=a_cumrep[:, 127:128], func=AF.Exp)
                pa = inc(ai, "a")
                wait(V, "a", pa)
                V.tensor_sub(out=a_wgt, in0=a_cumrep[:, 127:128], in1=a_cum)
                V.drain()
                pv = inc(V.engine_nop(), "v")
                wait(A, "v", pv)
                ai = A.activation(out=a_wgt, in_=a_wgt, func=AF.Exp)
                pa = inc(ai, "a")
                wait(V, "a", pa)
                V.tensor_mul(out=a_wgt, in0=a_wgt, in1=bcol)
                V.tensor_scalar_mul(out=a_negb, in0=bcol, scalar1=-1.0)
                V.drain()
                V.tensor_scalar(out=a_t1, in0=a_cumrep, scalar1=a_cum, scalar2=None,
                                op0=AL.subtract)
                V.drain()
                V.tensor_add(out=a_t1, in0=a_t1, in1=c_aiu)
                V.drain()
                pv = inc(V.engine_nop(), "v")
                wait(A, "v", pv)
                ai = A.activation(out=a_gtincl, in_=a_t1, func=AF.Exp)
                if own:
                    ai = A.activation(out=a_gamrep, in_=a_cumrep, func=AF.Exp)
                pa = inc(ai, "a")
                wait(V, "a", pa)
                V.tensor_sub(out=a_t1, in0=a_gtincl, in1=c_id)
                V.drain()
                pv = inc(V.engine_nop(), "v")

                wait(T, "v", pv)
                transpose128(Kt, a_KT)
                if own:
                    transpose128(Vt, a_VT)
                wait(T, "tpe")
                cmm(a_KT, a_KT, a_kk)
                V.drain()
                V.tensor_mul(out=s_nb[0], in0=a_t1, in1=a_kk)
                V.drain()
                V.tensor_scalar_mul(out=s_nb[0], in0=s_nb[0], scalar1=a_negb)
                V.tensor_scalar_mul(out=a_gk, in0=Kt, scalar1=a_gam)
                V.tensor_scalar_mul(out=a_kw, in0=Kt, scalar1=a_wgt)
                V.drain()
                pv = inc(V.engine_nop(), "v")
                wait(T, "v", pv)
                transpose128(s_nb[0], s_na[0])
                wait(V, "tpe")

                V.tensor_copy(out=a_Z, in_=a_kw)
                if own:
                    V.tensor_copy(out=a_Wm, in_=a_gk)
                V.drain()
                pv = inc(V.engine_nop(), "v")
                if own:
                    wait(T, "v", pv)
                    wait(T, "tpe")
                    cmm(a_VT, QTt, a_PF)
                cur = 0
                for m in range(7):
                    wait(T, "v")
                    cmm(s_na[cur], a_Z, a_Z,
                        extra=lambda d, p: V.tensor_add(out=d, in0=d, in1=p))
                    if own:
                        cmm(s_nb[cur], a_Wm, a_Wm,
                            extra=lambda d, p: V.tensor_add(out=d, in0=d, in1=p))
                        cmm(s_nb[cur], a_PF, a_PF,
                            extra=lambda d, p: V.tensor_add(out=d, in0=d, in1=p))
                    if m < 6:
                        wait(T, "v")
                        cmm(s_nb[cur], s_na[cur], s_na[1 - cur])
                        cmm(s_na[cur], s_nb[cur], s_nb[1 - cur])
                        cur = 1 - cur

                wait(T, "v")
                cmm(a_gk, a_Z, a_ET)
                cmm(a_Z, Vt, a_F)
                cmm(Vt, a_Z, a_FT)
                if own:
                    transpose128(a_Wm, a_WT)

                if own:
                    wait(T, "v")
                    cmm(MTc, QTt, a_MQ)
                    wait(T, "v")
                    wait(T, "tpe")
                    cmm(a_WT, a_MQ, a_BT,
                        extra=lambda d, p: V.scalar_tensor_tensor(
                            out=d, in0=p, scalar=-1.0, in1=a_PF,
                            op0=AL.mult, op1=AL.add))
                    V.drain()
                    V.tensor_mul(out=a_BT, in0=a_BT, in1=a_gtincl)
                    V.drain()
                    V.tensor_scalar_mul(out=a_BT, in0=a_BT, scalar1=bcol)
                    V.drain()
                    pv = inc(V.engine_nop(), "v")
                    wait(T, "v", pv)
                    cmm(Kt, a_BT, a_t2)
                    V.tensor_mul(out=a_t1, in0=a_MQ, in1=a_gamrep)
                    V.drain()
                    V.tensor_add(out=a_t1, in0=a_t1, in1=a_t2)
                    V.drain()
                    V.tensor_mul(out=a_OT, in0=a_t1, in1=GTt)
                    V.drain()
                    pv = inc(V.engine_nop(), "v")
                    wait(T, "v", pv)
                    for half in range(D // 512):
                        guard(T, "bank5")
                        mmO = T.matmul(ps[5][:, 0:512], a_OT,
                                       w_awo[:, half * 512:(half + 1) * 512],
                                       start=True, stop=True)
                        pt = inc(mmO, "t")
                        xr = xres[half % 2]
                        lvs = f"ld_xres{half % 2}"
                        lv = load(xr, d_x1own[t_own * 128:(t_own + 1) * 128,
                                              half * 512:(half + 1) * 512],
                                  key=f"xres{half % 2}")
                        wait(V, lvs, lv)
                        wait(V, "t", pt)
                        sgb = stg1[half]
                        guard(V, f"stg{half}")
                        V.tensor_add(out=sgb, in0=ps[5][:, 0:512], in1=xr)
                        V.drain()
                        V.tensor_add(out=sgb, in0=sgb,
                                     in1=b_o[:, half * 512:(half + 1) * 512])
                        V.drain()
                        pv = inc(V.engine_nop(), "v")
                        used("bank5", "v")
                        used(f"xres{half % 2}", "v")
                        wait(A, "v", pv)
                        store(d_x2own[t_own * 128:(t_own + 1) * 128,
                                      half * 512:(half + 1) * 512], sgb, half)
                        used(f"stg{half}", f"st{half}")

                wait(T, "v")
                cmm(a_ET, Mc, a_t1,
                    extra=lambda d, p: V.scalar_tensor_tensor(
                        out=d, in0=p, scalar=-1.0, in1=a_F, op0=AL.mult, op1=AL.add))
                cmm(Mc, a_ET, a_t2,
                    extra=lambda d, p: V.scalar_tensor_tensor(
                        out=d, in0=p, scalar=-1.0, in1=a_FT, op0=AL.mult, op1=AL.add))
                V.drain()
                V.scalar_tensor_tensor(out=Mn, in0=Mc, scalar=a_gamL, in1=a_t1,
                                       op0=AL.mult, op1=AL.add)
                V.scalar_tensor_tensor(out=MTn, in0=MTc, scalar=a_gamL, in1=a_t2,
                                       op0=AL.mult, op1=AL.add)
                V.drain()
                inc(V.engine_nop(), "v")
                mcur["i"] = nx

            ngroups = NT // GROUP if "attn" in phases else 0
            for g in range(ngroups):
                gtiles = list(range(g * GROUP, (g + 1) * GROUP))
                own_g = gtiles[0] >= OWN0
                wait(A, "t")
                wait(A, "tpm")
                wait(V, "t")
                for si, t in enumerate(gtiles):
                    b = xin[si % 2]
                    lvs = f"ld_xin{si % 2}"
                    lv = load(b[:, 0:D], x1_src(t), key=f"xin{si % 2}")
                    wait(V, lvs, lv)
                    ln_tile(b, scr4, srckey=f"xin{si % 2}")
                    wait(T, "v")
                    for dc in range(DCH):
                        transpose128(scr4[:, dc * 128:(dc + 1) * 128],
                                     a_xh2T[:, dc, si * 128:(si + 1) * 128])
                    wait(V, "tpe")
                wait(T, "tpe")

                for si, t in enumerate(gtiles):
                    pp = ps[si % 2]
                    guard(T, f"bank{si % 2}")
                    for dc in range(DCH):
                        mm = T.matmul(pp[:, 0:258],
                                      a_xh2T[:, dc, si * 128:(si + 1) * 128],
                                      w_awkvab[:, dc, :], start=(dc == 0),
                                      stop=(dc == DCH - 1))
                    pt = inc(mm, "t")
                    wait(V, "t", pt)
                    V.tensor_add(out=scr1[:, 0:256], in0=pp[:, 0:256],
                                 in1=b_kvab[:, 0:256])
                    V.drain()
                    pv = inc(V.engine_nop(), "v")
                    wait(A, "v", pv)
                    A.activation(out=scr2[:, 0:128], in_=scr1[:, 0:128], func=AF.Square,
                                 accum_out=sm_s[:, 0:1])
                    A.drain()
                    A.activation(out=sm_s[:, 1:2], in_=sm_s[:, 0:1], func=AF.Sqrt)
                    A.activation(out=a_acol[:, si:si + 1], in_=pp[:, 256:257],
                                 func=AF.Sigmoid, bias=b_a)
                    ai = A.activation(out=a_bcol[:, si:si + 1], in_=pp[:, 257:258],
                                      func=AF.Sigmoid, bias=b_b)
                    pa = inc(ai, "a")
                    used(f"bank{si % 2}", "a")
                    wait(V, "a", pa)
                    V.tensor_scalar_max(out=sm_s[:, 1:2], in0=sm_s[:, 1:2],
                                        scalar1=1e-12)
                    V.drain()
                    pv = inc(V.engine_nop(), "v")
                    wait(A, "v", pv)
                    V.reciprocal(out=sm_r[:, 0:1], in_=sm_s[:, 1:2])
                    V.drain()
                    V.tensor_scalar_mul(out=a_K[:, si, :], in0=scr1[:, 0:128],
                                        scalar1=sm_r[:, 0:1])
                    V.tensor_copy(out=a_V[:, si, :], in_=scr1[:, 128:256])
                    V.drain()
                    inc(V.engine_nop(), "v")
                if own_g:
                    for si, t in enumerate(gtiles):
                        pp = ps[si % 2]
                        guard(T, f"bank{si % 2}")
                        for dc in range(DCH):
                            T.matmul(pp[:, 0:128], w_awq[:, dc, :],
                                     a_xh2T[:, dc, si * 128:(si + 1) * 128],
                                     start=(dc == 0), stop=(dc == DCH - 1))
                        for dc in range(DCH):
                            mm = T.matmul(pp[:, 128:256], w_awg[:, dc, :],
                                          a_xh2T[:, dc, si * 128:(si + 1) * 128],
                                          start=(dc == 0), stop=(dc == DCH - 1))
                        pt = inc(mm, "t")
                        wait(V, "t", pt)
                        V.tensor_scalar(out=a_QT[:, si, :], in0=pp[:, 0:128],
                                        scalar1=b_q, scalar2=None, op0=AL.add)
                        V.drain()
                        pv = inc(V.engine_nop(), "v")
                        wait(A, "v", pv)
                        ai = A.activation(out=a_GT[:, si, :], in_=pp[:, 128:256],
                                          func=AF.Sigmoid, bias=b_g)
                        pa = inc(ai, "a")
                        used(f"bank{si % 2}", "a")
                        wait(V, "a", pa)

                for si, t in enumerate(gtiles):
                    emit_chunk(si, own_g, (t - OWN0) if own_g else 0)

            wait_all_stores(SY)
            if "ffn2" in phases: emit_ffn(
                tiles=list(range(NOWN)),
                src_x=lambda t: d_x2own[t * 128:(t + 1) * 128, :],
                src_res=lambda t: d_x2own[t * 128:(t + 1) * 128, :],
                dst_of=lambda t: d_out[t * 128:(t + 1) * 128, :],
            )

            if "ffn2" not in phases:
                src_dump = d_x2own if "attn" in phases else d_x1own
                for t in range(0, NOWN * 128, 128):
                    load(xin[0][:, 0:D], src_dump[t:t + 128, :],
                         s="dbg_ld", key="xin0")
                    wait(A, "dbg_ld")
                    store(d_out[t:t + 128, :], xin[0][:, 0:D], 0)
                    used("xin0", "st0")
                    wait(SY, "st0")
            wait_all_stores(SY)

    return nc


def make_host_inputs(cfg, ins):
    NT, OWN0, D, I = cfg["NTILES"], cfg["OWN0"], cfg["D"], cfg["I"]
    NOWN = NT - OWN0
    x = np.ascontiguousarray(np.asarray(ins["L0_attn_x"], dtype=np.float32))
    B = x.shape[0]
    Wk = np.asarray(ins["L0_attn_Wk"], np.float32)
    Wv = np.asarray(ins["L0_attn_Wv"], np.float32)
    Wa = np.asarray(ins["L0_attn_Wa"], np.float32)
    Wb = np.asarray(ins["L0_attn_Wb"], np.float32)
    wkvab = np.ascontiguousarray(np.concatenate([Wk, Wv, Wa, Wb], axis=1))
    bk = np.asarray(ins["L0_attn_bk"], np.float32)
    bv = np.asarray(ins["L0_attn_bv"], np.float32)
    bkvab = np.ascontiguousarray(
        np.broadcast_to(np.concatenate([bk, bv, [0.0, 0.0]]).astype(np.float32)[None, :],
                        (128, 258)))
    common = {
        "wg": np.ascontiguousarray(np.asarray(ins["L0_ffn_w_gate"], np.float32)),
        "wu": np.ascontiguousarray(np.asarray(ins["L0_ffn_w_up"], np.float32)),
        "wd": np.ascontiguousarray(np.asarray(ins["L0_ffn_w_down"], np.float32)),
        "awq": np.ascontiguousarray(np.asarray(ins["L0_attn_Wq"], np.float32)),
        "awg": np.ascontiguousarray(np.asarray(ins["L0_attn_Wg"], np.float32)),
        "awkvab": wkvab,
        "awo": np.ascontiguousarray(np.asarray(ins["L0_attn_Wo"], np.float32)),
        "bkvab": bkvab,
        "bqcol": np.ascontiguousarray(np.asarray(ins["L0_attn_bq"], np.float32)[:, None]),
        "bgcol": np.ascontiguousarray(np.asarray(ins["L0_attn_bg"], np.float32)[:, None]),
        "bacol": np.ascontiguousarray(
            np.broadcast_to(np.asarray(ins["L0_attn_ba"], np.float32), (128,))[:, None]),
        "bbcol": np.ascontiguousarray(
            np.broadcast_to(np.asarray(ins["L0_attn_bb"], np.float32), (128,))[:, None]),
        "borep": np.ascontiguousarray(
            np.broadcast_to(np.asarray(ins["L0_attn_bo"], np.float32)[None, :], (128, D))),
        "c_ut": np.triu(np.ones((128, 128), np.float32)),
        "c_id": np.eye(128, dtype=np.float32),
        "c_idb": np.eye(128, dtype=np.float32).astype(ml_dtypes.bfloat16),
        "c_aiu": ((np.triu(np.ones((128, 128), np.float32)) - 1.0) * 1e9).astype(np.float32),
        "c_ones1": np.ones((1, 128), np.float32),
    }
    half = OWN0 * 128
    in_maps = []
    for c in range(2 * B):
        bi, hi = c // 2, c % 2
        xa = np.zeros((NT * 128, D), np.float32)
        if hi == 0:
            xa[half:half + NOWN * 128] = x[bi, 0:NOWN * 128]
        else:
            xa[0:half] = x[bi, 0:half]
            xa[half:half + NOWN * 128] = x[bi, half:half + NOWN * 128]
        in_maps.append(dict(common, xall=np.ascontiguousarray(xa)))
    return in_maps


def assemble(cfg, results, B, T):
    NOWN = cfg["NTILES"] - cfg["OWN0"]
    D = cfg["D"]
    out = np.zeros((B, T, D), np.float32)
    half = cfg["OWN0"] * 128
    for c, r in enumerate(results):
        bi, hi = c // 2, c % 2
        if hi == 0:
            out[bi, 0:NOWN * 128] = r["yout"]
        else:
            out[bi, half:half + NOWN * 128] = r["yout"]
    return out


from concourse.bass_utils import run_bass_kernel_spmd

_NC_CACHE = {}


def kernel(**inputs):
    cfg = FULL
    if "nc" not in _NC_CACHE:
        _NC_CACHE["nc"] = build_nc(cfg)
    nc = _NC_CACHE["nc"]
    in_maps = make_host_inputs(cfg, inputs)
    res = run_bass_kernel_spmd(nc, in_maps, core_ids=list(range(8)))
    x = np.asarray(inputs["L0_attn_x"])
    return assemble(cfg, res.results, x.shape[0], x.shape[1])


# version 19
# speedup vs baseline: 1.3439x; 1.0090x over previous
import contextlib
import numpy as np
import ml_dtypes
import concourse.bass as bass
import concourse.mybir as mybir

F32 = mybir.dt.float32
BF16 = mybir.dt.bfloat16
I32 = mybir.dt.int32
AF = mybir.ActivationFunctionType
AL = mybir.AluOpType
AX = mybir.AxisListType

FULL = dict(NTILES=32, OWN0=16, D=1024, I=2816, ITILE=256, SWEEP=4, GROUP=2)
TINY = dict(NTILES=2, OWN0=1, D=1024, I=2816, ITILE=256, SWEEP=2, GROUP=1)
MID = dict(NTILES=8, OWN0=4, D=1024, I=2816, ITILE=256, SWEEP=4, GROUP=2)


def build_nc(cfg, phases=("prep", "ffn1", "attn", "ffn2")):
    NT, OWN0, D, I = cfg["NTILES"], cfg["OWN0"], cfg["D"], cfg["I"]
    ITILE, SWEEP, GROUP = cfg["ITILE"], cfg["SWEEP"], cfg["GROUP"]
    DCH, ICH = D // 128, I // 128
    NIT = I // ITILE
    NOWN = NT - OWN0
    IBLK = ITILE // 128
    SCRW = D
    SLOTW = SWEEP * ITILE
    QW = max(2 * D, 2 * SLOTW)

    nc = bass.Bass("TRN2", target_bir_lowering=False, debug=False)

    d_x = nc.dram_tensor("xall", [NT * 128, D], F32, kind="ExternalInput").ap()
    d_out = nc.dram_tensor("yout", [NOWN * 128, D], F32, kind="ExternalOutput").ap()
    d_wg = nc.dram_tensor("wg", [I, D], F32, kind="ExternalInput").ap()
    d_wu = nc.dram_tensor("wu", [I, D], F32, kind="ExternalInput").ap()
    d_wd = nc.dram_tensor("wd", [D, I], F32, kind="ExternalInput").ap()
    d_awq = nc.dram_tensor("awq", [D, 128], F32, kind="ExternalInput").ap()
    d_awg = nc.dram_tensor("awg", [D, 128], F32, kind="ExternalInput").ap()
    d_awkvab = nc.dram_tensor("awkvab", [D, 258], F32, kind="ExternalInput").ap()
    d_awo = nc.dram_tensor("awo", [128, D], F32, kind="ExternalInput").ap()
    d_bkvab = nc.dram_tensor("bkvab", [128, 258], F32, kind="ExternalInput").ap()
    d_bq = nc.dram_tensor("bqcol", [128, 1], F32, kind="ExternalInput").ap()
    d_bg = nc.dram_tensor("bgcol", [128, 1], F32, kind="ExternalInput").ap()
    d_ba = nc.dram_tensor("bacol", [128, 1], F32, kind="ExternalInput").ap()
    d_bb = nc.dram_tensor("bbcol", [128, 1], F32, kind="ExternalInput").ap()
    d_bo = nc.dram_tensor("borep", [128, D], F32, kind="ExternalInput").ap()
    d_cut = nc.dram_tensor("c_ut", [128, 128], F32, kind="ExternalInput").ap()
    d_cid = nc.dram_tensor("c_id", [128, 128], F32, kind="ExternalInput").ap()
    d_cidb = nc.dram_tensor("c_idb", [128, 128], BF16, kind="ExternalInput").ap()
    d_caiu = nc.dram_tensor("c_aiu", [128, 128], F32, kind="ExternalInput").ap()
    d_cones = nc.dram_tensor("c_ones1", [1, 128], F32, kind="ExternalInput").ap()

    d_x1ctx = nc.dram_tensor("x1ctx", [max(OWN0, 1) * 128, D], F32, kind="Internal").ap()
    d_x1own = nc.dram_tensor("x1own", [NOWN * 128, D], F32, kind="Internal").ap()
    d_x2own = nc.dram_tensor("x2own", [NOWN * 128, D], F32, kind="Internal").ap()
    d_stg = [nc.dram_tensor(f"stash_g{h}", [DCH, ICH, 128, 128], BF16,
                            kind="Internal").ap() for h in range(2)]
    d_stu = [nc.dram_tensor(f"stash_u{h}", [DCH, ICH, 128, 128], BF16,
                            kind="Internal").ap() for h in range(2)]
    d_std = [nc.dram_tensor(f"stash_d{h}", [ICH, DCH, 128, 128], BF16,
                            kind="Internal").ap() for h in range(2)]

    es = contextlib.ExitStack()
    with es:
        def sb(name, shape, dt=F32):
            return es.enter_context(nc.sbuf_tensor(name, shape, dt)).ap()

        c_ut = sb("c_ut_s", [128, 128])
        c_id = sb("c_id_s", [128, 128])
        c_aiu = sb("c_aiu_s", [128, 128])
        c_ones = sb("c_ones_s", [1, 128])
        w_awq = sb("w_awq", [128, DCH, 128])
        w_awg = sb("w_awg", [128, DCH, 128])
        w_awkvab = sb("w_awkvab", [128, DCH, 258])
        w_awo = sb("w_awo", [128, D])
        b_kvab = sb("b_kvab_s", [128, 258])
        b_q = sb("b_q_s", [128, 1])
        b_g = sb("b_g_s", [128, 1])
        b_a = sb("b_a_s", [128, 1])
        b_b = sb("b_b_s", [128, 1])
        b_o = sb("b_o_s", [128, D])
        c_eps = sb("c_eps", [128, 1])

        xin = [sb("xin0", [128, QW]), sb("xin1", [128, SCRW])]
        scr1 = sb("scr1", [128, QW])
        scr2 = sb("scr2", [128, QW])
        scr3 = sb("scr3", [128, QW])
        scr4 = sb("scr4", [128, SCRW])
        sm_m = sb("sm_m", [128, QW // 16])
        sm_s = sb("sm_s", [128, QW // 16])
        sm_r = sb("sm_r", [128, QW // 16])
        xqT = [[sb(f"xqT{t}_{h}", [128, DCH, 128], BF16) for h in range(2)]
               for t in range(SWEEP)]
        wcol = [[sb(f"wcol{i}_{h}", [128, 2, DCH, ITILE], BF16) for h in range(2)]
                for i in range(2)]
        c_idb = sb("c_idb_s", [128, 128], BF16)
        sp_h = sb("sp_h", [128, QW], BF16)
        sp_l = sb("sp_l", [128, QW], BF16)
        hb_all = sb("hb_all", [128, 4 * SLOTW])
        hb = [hb_all[:, i * SLOTW:(i + 1) * SLOTW] for i in range(4)]
        hsilu = [sb(f"hsilu{i}", [128, ITILE]) for i in range(2)]
        hq_m, hq_s, hq_r = sm_m, sm_s, sm_r
        hqT = [[sb(f"hqT{t}_{h}", [128, ICH, 128], BF16) for h in range(2)]
               for t in range(SWEEP)]
        wdn = [[sb(f"wdn{i}_{h}", [128, 512], BF16) for h in range(2)]
               for i in range(3)]
        xres = [sb(f"xres{i}", [128, 512]) for i in range(2)]
        stg1 = [sb(f"stg1_{i}", [128, 512]) for i in range(4)]

        a_xh2T = hb_all[:, 0:DCH * GROUP * 128].rearrange(
            "p (d n) -> p d n", n=GROUP * 128)
        hpool = [hqT[si][h].bitcast(F32).rearrange("p a n -> p (a n)")
                 for si in (0, 1) for h in (0, 1)]
        hp_off = [0, 0, 0, 0]

        def carve(pi, w):
            c0 = hp_off[pi]
            hp_off[pi] = c0 + w
            v = hpool[pi][:, c0:c0 + w]
            assert hp_off[pi] <= ICH * 64, "hqT pool overflow"
            return v

        a_K = carve(0, GROUP * 128).rearrange("p (g n) -> p g n", n=128)
        a_V = carve(0, GROUP * 128).rearrange("p (g n) -> p g n", n=128)
        a_QT = carve(0, GROUP * 128).rearrange("p (g n) -> p g n", n=128)
        a_GT = carve(0, GROUP * 128).rearrange("p (g n) -> p g n", n=128)
        a_KT = carve(0, 128)
        a_VT = carve(0, 128)
        a_acol = sb("a_acol", [128, GROUP])
        a_bcol = sb("a_bcol", [128, GROUP])
        a_la = sb("a_la", [128, 1])
        a_cum = sb("a_cum", [128, 1])
        a_cumrow = sb("a_cumrow", [1, 128])
        a_gam = sb("a_gam", [128, 1])
        a_gamL = sb("a_gamL", [128, 1])
        a_wgt = sb("a_wgt", [128, 1])
        a_negb = sb("a_negb", [128, 1])
        a_cumrep = carve(1, 128)
        a_gamrep = carve(1, 128)
        a_gtincl = carve(1, 128)
        a_kk = carve(1, 128)
        s_na = [carve(1, 128) for i in range(2)]
        s_nb = [carve(1, 128) for i in range(2)]
        a_gk = carve(1, 128)
        a_kw = carve(1, 128)
        a_Z = carve(1, 128)
        a_Wm = carve(2, 128)
        a_PF = carve(2, 128)
        a_WT = carve(2, 128)
        a_ET = carve(2, 128)
        a_F = carve(2, 128)
        a_FT = carve(2, 128)
        a_M = [carve(2, 128) for i in range(2)]
        a_MT = [carve(2, 128) for i in range(2)]
        a_MQ = carve(3, 128)
        a_BT = carve(3, 128)
        a_OT = carve(3, 128)
        a_t1 = carve(3, 128)
        a_t2 = carve(3, 128)

        ps = [es.enter_context(nc.psum_tensor(f"ps{i}", [128, 512], F32)).ap()
              for i in range(8)]

        sems, cnt = {}, {}

        def sem(name):
            if name not in sems:
                sems[name] = es.enter_context(nc.semaphore(name))
                cnt[name] = 0
            return sems[name]

        def inc(instr, name, amt=1):
            s = sem(name)
            cnt[name] += amt
            instr.then_inc(s, amt)
            return cnt[name]

        def wait(eng, name, val=None):
            if name not in cnt and val is None:
                return
            s = sem(name)
            v = cnt[name] if val is None else val
            if v > 0:
                eng.wait_ge(s, v)

        V, A, T, G, SY = nc.vector, nc.scalar, nc.tensor, nc.gpsimd, nc.sync

        lastuse = {}

        def guard(eng, key):
            if key in lastuse:
                sname, val = lastuse[key]
                wait(eng, sname, val)

        def used(key, sname):
            sem(sname)
            lastuse[key] = (sname, cnt[sname])

        def load(out, in_, s=None, key=None):
            if key is not None:
                guard(SY, key)
            if s is None:
                s = "ld_" + (key if key is not None else "c")
            return inc(SY.dma_start(out=out, in_=in_), s, 16)

        def store(out, in_, slot):
            return inc(A.dma_start(out=out, in_=in_), f"st{slot}", 16)

        def wait_all_stores(eng):
            for i in range(4):
                wait(eng, f"st{i}")

        with nc.Block():
            load(c_ut, d_cut); load(c_id, d_cid); load(c_aiu, d_caiu)
            load(c_idb, d_cidb)
            load(c_ones, d_cones)
            load(w_awq, d_awq.rearrange("(c p) n -> p c n", p=128))
            load(w_awg, d_awg.rearrange("(c p) n -> p c n", p=128))
            load(w_awkvab, d_awkvab.rearrange("(c p) n -> p c n", p=128))
            load(w_awo, d_awo)
            load(b_kvab, d_bkvab); load(b_q, d_bq); load(b_g, d_bg)
            load(b_a, d_ba); load(b_b, d_bb); load(b_o, d_bo)
            CONST_LD = cnt["ld_c"]
            for eng in (V, A, T):
                wait(eng, "ld_c", CONST_LD)
            V.memset(c_eps, 1e-5)
            V.drain()

            SH_MUL = float(np.float32(1.0) / np.float32(6.0)) * 0.5
            SH_MIN = float(np.float32(1e-12)) * 0.5

            def quant(src, out, F, mb, ssb, rb, t1, t2, t3, srckey=None):
                nb = F // 16
                v3 = src.rearrange("p (b k) -> p b k", k=16) if src.shape[1] == F \
                    else src
                V.tensor_reduce(out=mb[:, :nb], in_=v3, axis=AX.X, op=AL.max,
                                apply_absolute_value=True)
                V.tensor_tensor(out=t1[:, :F], in0=src, in1=src, op=AL.mult)
                V.drain()
                V.tensor_scalar(out=ssb[:, :nb], in0=mb[:, :nb], scalar1=SH_MUL,
                                scalar2=SH_MIN, op0=AL.mult, op1=AL.max)
                V.drain()
                V.reciprocal(out=rb[:, :nb], in_=ssb[:, :nb])
                V.drain()
                V.tensor_tensor(out=mb[:, :nb], in0=rb[:, :nb], in1=rb[:, :nb],
                                op=AL.mult)
                V.drain()
                mb3 = mb[:, :nb].broadcast_to([128, nb, 16])
                t1v = t1[:, :F].rearrange("p (b k) -> p b k", k=16)
                V.tensor_tensor(out=t1v, in0=t1v, in1=mb3, op=AL.mult)
                V.drain()
                c1 = t3.bitcast(BF16)[:, 0:F]
                c2 = t3.bitcast(BF16)[:, F:2 * F]
                c3 = t2.bitcast(BF16)[:, 0:F]
                c4 = t2.bitcast(BF16)[:, F:2 * F]

                def cmp(dst, thr2, w):
                    V.tensor_scalar(out=dst, in0=t1[:, :F], scalar1=thr2,
                                    scalar2=w, op0=AL.is_gt, op1=AL.mult)

                guard(V, "scr3g")
                cmp(c1, 0.25, 1.0)
                guard(V, "scr2g")
                cmp(c3, 2.25, 1.0)
                cmp(c4, 6.25, 1.0)
                cmp(c2, 12.25, 1.0)
                V.drain()
                V.tensor_add(out=c1, in0=c1, in1=c3)
                V.tensor_add(out=c4, in0=c4, in1=c2)
                V.drain()
                cmp(c3, 25.0, 2.0)
                cmp(c2, 49.0, 2.0)
                V.drain()
                V.tensor_add(out=c1, in0=c1, in1=c4)
                V.tensor_add(out=c3, in0=c3, in1=c2)
                V.drain()
                cmp(c2, 100.0, 4.0)
                V.tensor_add(out=c1, in0=c1, in1=c3)
                V.drain()
                V.tensor_add(out=c1, in0=c1, in1=c2)
                V.drain()
                sh3 = ssb[:, :nb].broadcast_to([128, nb, 16])
                t2v = t2[:, :F].rearrange("p (b k) -> p b k", k=16)
                mm = V.tensor_tensor(out=t2v,
                                     in0=c1.rearrange("p (b k) -> p b k", k=16),
                                     in1=sh3, op=AL.mult)
                pv = inc(mm, "v")
                sgb = c2
                wait(A, "v", pv)
                wait(A, "tpm")
                A.copy(out=sp_h[:, :F], in_=t2[:, :F])
                ai = A.activation(out=sgb, in_=src, func=AF.Sign)
                pa = inc(ai, "a")
                if srckey is not None:
                    for k_ in ((srckey,) if isinstance(srckey, str) else srckey):
                        used(k_, "a")
                wait(G, "a", pa)
                G.tensor_tensor(out=sp_l[:, :F], in0=t2[:, :F], in1=sp_h[:, :F],
                                op=AL.subtract)
                G.drain()
                G.tensor_mul(out=sp_h[:, :F], in0=sp_h[:, :F], in1=sgb)
                gi = G.tensor_mul(out=sp_l[:, :F], in0=sp_l[:, :F], in1=sgb)
                inc(gi, "gq")
                used("scr2g", "gq")
                used("scr3g", "gq")
                return pv

            tp_state = {"n": 0}

            def transpose128(src_ap, dst_ap, ident=None):
                bk = 6 + (tp_state["n"] % 2)
                tp_state["n"] += 1
                key = f"bank{bk}"
                if ident is None:
                    pst = ps[bk][:, 0:128]
                else:
                    pst = ps[bk].bitcast(BF16)[:, 0:128]
                guard(T, key)
                mm = T.transpose(pst, src_ap, c_id if ident is None else ident)
                pt = inc(mm, "tpm")
                wait(A, "tpm", pt)
                inc(A.copy(out=dst_ap, in_=pst), "tpe")
                used(key, "tpe")
                return mm

            def transpose_hl(colslice, dsth, dstl):
                transpose128(sp_h[:, colslice], dsth, ident=c_idb)
                return transpose128(sp_l[:, colslice], dstl, ident=c_idb)

            def transpose_batch(pairs):
                assert len(pairs) <= 8
                bk = 6 + (tp_state["n"] % 2)
                tp_state["n"] += 1
                key = f"bank{bk}"
                pst = ps[bk].bitcast(BF16)
                guard(T, key)
                for i, (src, _) in enumerate(pairs):
                    mm = T.transpose(pst[:, i * 128:(i + 1) * 128], src, c_idb)
                pt = inc(mm, "tpm")
                wait(A, "tpm", pt)
                for i, (_, dst) in enumerate(pairs):
                    ai = A.copy(out=dst, in_=pst[:, i * 128:(i + 1) * 128])
                inc(ai, "tpe")
                used(key, "tpe")

            def transpose_batch_wide(srcs, dst):
                assert len(srcs) <= 8
                bk = 6 + (tp_state["n"] % 2)
                tp_state["n"] += 1
                key = f"bank{bk}"
                pst = ps[bk].bitcast(BF16)
                guard(T, key)
                for i, src in enumerate(srcs):
                    mm = T.transpose(pst[:, i * 128:(i + 1) * 128], src, c_idb)
                pt = inc(mm, "tpm")
                wait(A, "tpm", pt)
                inc(A.copy(out=dst, in_=pst[:, 0:len(srcs) * 128]), "tpe")
                used(key, "tpe")

            def ln_tile(src, out, srckey=None, outkey=None):
                V.tensor_reduce(out=sm_m[:, 0:1], in_=src[:, 0:D], axis=AX.X, op=AL.add)
                V.drain()
                V.tensor_scalar_mul(out=sm_m[:, 0:1], in0=sm_m[:, 0:1], scalar1=1.0 / D)
                V.drain()
                guard(V, "scr1sign")
                vi = V.tensor_scalar(out=scr1[:, 0:D], in0=src[:, 0:D],
                                     scalar1=sm_m[:, 0:1],
                                     scalar2=None, op0=AL.subtract)
                pv = inc(vi, "v")
                V.drain()
                if srckey is not None:
                    used(srckey, "v")
                wait(A, "v", pv)
                guard(A, "scr2g")
                A.activation(out=scr2[:, 0:D], in_=scr1[:, 0:D], func=AF.Square,
                             accum_out=sm_s[:, 0:1])
                A.drain()
                ai = A.activation(out=sm_r[:, 0:1], in_=sm_s[:, 0:1], func=AF.Sqrt,
                                  scale=1.0 / D, bias=c_eps)
                pa = inc(ai, "a")
                wait(V, "a", pa)
                V.reciprocal(out=sm_r[:, 1:2], in_=sm_r[:, 0:1])
                V.drain()
                if outkey is not None:
                    guard(V, outkey)
                vi = V.tensor_scalar_mul(out=out[:, 0:D], in0=scr1[:, 0:D],
                                         scalar1=sm_r[:, 1:2])
                pv = inc(vi, "v")
                V.drain()
                return pv

            stgb = [stg1[h].bitcast(BF16) for h in range(2)]

            def prep_group(segs, rt2):
                b = xin[0]
                lvs = "ld_xin0"
                off = 0
                for dram_ap, Fq, _ in segs:
                    lv = load(b[:, off:off + Fq], dram_ap, key="xin0")
                    off += Fq
                Ftot = off
                wait(V, lvs, lv)
                quant(b[:, :Ftot], b[:, :Ftot], Ftot, sm_m, sm_s, sm_r,
                      scr1, scr2, scr3, srckey="xin0")
                wait(T, "gq")
                off = 0
                for _, Fq, wdst in segs:
                    nbk = Fq // 128
                    for h in range(2):
                        sp = sp_h if h == 0 else sp_l
                        for p0 in range(0, nbk, 8):
                            pw = min(8, nbk - p0)
                            sgb = stgb[h]
                            guard(A, f"stg{h}")
                            transpose_batch_wide(
                                [sp[:, off + (p0 + j) * 128:off + (p0 + j + 1) * 128]
                                 for j in range(pw)],
                                sgb[:, 0:pw * 128])
                            wait(A, "tpe")
                            store(wdst(h, p0, pw),
                                  sgb[:, 0:pw * 128]
                                  .rearrange("p (b n) -> p b n", n=128), h)
                            used(f"stg{h}", f"st{h}")
                    off += Fq

            rt2 = [0]
            wg_st = {}
            wd_st = None
            if "prep" in phases:
                for rt in range(DCH):
                    c0 = 0
                    while c0 < I:
                        cw = min(2 * D, I - c0)
                        prep_group([
                            (d_wd[rt * 128:(rt + 1) * 128, c0:c0 + cw], cw,
                             lambda h, p0, pw, rt=rt, c0=c0:
                             d_std[h][c0 // 128 + p0:c0 // 128 + p0 + pw, rt]
                             .rearrange("b p n -> p b n"))], rt2[0])
                        rt2[0] += 1
                        c0 += cw
                wd_st = (cnt.get("st0", 0), cnt.get("st1", 0))

                def emit_wgwu(it0):
                    for stash, wsrc in ((d_stg, d_wg), (d_stu, d_wu)):
                        prep_group([
                            (wsrc[rt * 128:(rt + 1) * 128, :], D,
                             lambda h, p0, pw, rt=rt, stash=stash:
                             stash[h][p0:p0 + pw, rt].rearrange("b p n -> p b n"))
                            for rt in (2 * it0, 2 * it0 + 1) if rt < ICH], rt2[0])
                        rt2[0] += 1
                    wg_st[it0] = (cnt.get("st0", 0), cnt.get("st1", 0))

                def sweep0_side(it):
                    todo = (0, 1) if it == 0 else (it + 1,)
                    for it0 in todo:
                        if it0 < NIT and it0 not in wg_st:
                            emit_wgwu(it0)
            else:
                def sweep0_side(it):
                    pass

            def emit_ffn(tiles, src_x, src_res, dst_of, side=None):
                nsw = (len(tiles) + SWEEP - 1) // SWEEP

                def emit_pre(stiles):
                    wait(A, "t")
                    assert len(stiles) % 2 == 0
                    for p0 in range(0, len(stiles), 2):
                        ia, ib = p0 % 2, (p0 + 1) % 2
                        ta, tb = stiles[p0], stiles[p0 + 1]
                        lva = load(xin[ia][:, 0:D], src_x(ta), key=f"xin{ia}")
                        lvb = load(xin[ib][:, 0:D], src_x(tb), key=f"xin{ib}")
                        wait(V, f"ld_xin{ia}", lva)
                        ln_tile(xin[ia], hb_all[:, 0:D], srckey=f"xin{ia}",
                                outkey="hb0")
                        wait(V, f"ld_xin{ib}", lvb)
                        ln_tile(xin[ib], hb_all[:, D:2 * D], srckey=f"xin{ib}",
                                outkey="hb1")
                        quant(hb_all[:, 0:2 * D], hb_all[:, 0:2 * D], 2 * D,
                              sm_m, sm_s, sm_r, scr1, scr2, scr3,
                              srckey=("hb0", "hb1"))
                        wait(T, "gq")
                        for h, sp in ((0, sp_h), (1, sp_l)):
                            for k, si in ((0, p0), (1, p0 + 1)):
                                transpose_batch_wide(
                                    [sp[:, k * D + dc * 128:k * D + (dc + 1) * 128]
                                     for dc in range(DCH)],
                                    xqT[si][h].rearrange("p a n -> p (a n)"))
                    wait(T, "tpe")

                def emit_gateup(stiles, side_work=None):
                    ns = len(stiles)
                    pending_it = []

                    def flush_items(items):
                        hs0 = items[0][1]
                        F0 = sum(nsX * ITILE for _, _, nsX in items)
                        src = hb_all[:, hs0 * SLOTW:hs0 * SLOTW + F0]
                        wait(V, "g")
                        wait(A, "g")
                        quant(src, src, F0, hq_m, hq_s, hq_r, scr1, scr2, scr3)
                        wait(T, "gq")
                        pairs_h, pairs_l = [], []
                        for k, (itX, hsX, nsX) in enumerate(items):
                            for si0 in range(nsX):
                                for bi in range(IBLK):
                                    col = k * SLOTW + si0 * ITILE + bi * 128
                                    pairs_h.append((sp_h[:, col:col + 128],
                                                    hqT[si0][0][:, itX * IBLK + bi, :]))
                                    pairs_l.append((sp_l[:, col:col + 128],
                                                    hqT[si0][1][:, itX * IBLK + bi, :]))
                        for pp_ in (pairs_h, pairs_l):
                            for k0 in range(0, len(pp_), 8):
                                transpose_batch(pp_[k0:k0 + 8])
                        for _, hsX, _ in items:
                            used(f"hb{hsX}", "tpm")

                    def load_wcol(slot, it0):
                        if it0 in wg_st:
                            wait(SY, "st0", wg_st[it0][0])
                            wait(SY, "st1", wg_st[it0][1])
                        for mi, stash in enumerate((d_stg, d_stu)):
                            for h in range(2):
                                v5 = wcol[slot][h].rearrange(
                                    "p m d (b n) -> p m d b n", b=IBLK)
                                for bb in range(IBLK):
                                    load(v5[:, mi, :, bb, :],
                                         stash[h][:, it0 * IBLK + bb]
                                         .rearrange("d p n -> p d n"),
                                         s=f"wld{slot}", key=f"wcol{slot}")

                    for it in range(NIT):
                        if side_work is not None:
                            side_work(it)
                        if it == 0:
                            load_wcol(0, 0)
                        if it + 1 < NIT:
                            load_wcol((it + 1) % 2, it + 1)
                        wb = wcol[it % 2]
                        wait(T, f"wld{it % 2}")
                        hs = it % 4
                        for si in range(ns):
                            pp = ps[si % 2]
                            guard(T, f"bank{si % 2}")
                            passes = ((0, 0), (0, 1), (1, 0))
                            np_ = len(passes)
                            for mi2 in range(2):
                                reg = pp[:, mi2 * ITILE:(mi2 + 1) * ITILE]
                                k = 0
                                for xh_, wh_ in passes:
                                    for dc in range(DCH):
                                        mm = T.matmul(
                                            reg, xqT[si][xh_][:, dc, :],
                                            wb[wh_][:, mi2, dc, :],
                                            start=(k == 0),
                                            stop=(k == np_ * DCH - 1))
                                        k += 1
                            pt = inc(mm, "t")
                            if si == ns - 1:
                                used(f"wcol{it % 2}", "t")
                            wait(A, "t", pt)
                            hp = si % 2
                            guard(A, f"hsilu{hp}")
                            A.activation(out=hsilu[hp], in_=pp[:, 0:ITILE],
                                         func=AF.Silu)
                            hu = xres[hp]
                            guard(A, f"xres{hp}")
                            ai = A.copy(out=hu[:, 0:ITILE], in_=pp[:, ITILE:2 * ITILE])
                            pa = inc(ai, "a")
                            used(f"bank{si % 2}", "a")
                            wait(G, "a", pa)
                            if si == 0:
                                guard(G, f"hb{hs}")
                            gi = G.tensor_mul(out=hb[hs][:, si * ITILE:(si + 1) * ITILE],
                                              in0=hsilu[hp], in1=hu[:, 0:ITILE])
                            pg = inc(gi, "g")
                            used(f"xres{hp}", "g")
                            used(f"hsilu{hp}", "g")
                        pending_it.append((it, hs, ns))
                        if len(pending_it) >= 4:
                            flush_items([pending_it.pop(0), pending_it.pop(0)])
                    while pending_it:
                        if (len(pending_it) >= 2 and pending_it[0][1] % 2 == 0
                                and pending_it[1][1] == pending_it[0][1] + 1):
                            flush_items([pending_it.pop(0), pending_it.pop(0)])
                        else:
                            flush_items([pending_it.pop(0)])
                    wait(T, "tpe")

                def emit_down(stiles):
                    ns = len(stiles)
                    if wd_st is not None:
                        wait(SY, "st0", wd_st[0])
                        wait(SY, "st1", wd_st[1])
                    if True:
                        htiles = stiles
                        for dt in range(D // 512):
                            pts = []

                            def load_wdn(ic0):
                                sl = ic0 % 3
                                for h in range(2):
                                    load(wdn[sl][h].rearrange("p (b n) -> p b n", b=4),
                                         d_std[h][ic0, dt * 4:(dt + 1) * 4]
                                         .rearrange("b p n -> p b n"),
                                         s=f"dld{sl}", key=f"wdn{sl}")

                            load_wdn(0)
                            load_wdn(1)
                            for ic in range(ICH):
                                wbD = wdn[ic % 3]
                                lv = cnt[f"dld{ic % 3}"]
                                if ic + 2 < ICH:
                                    load_wdn(ic + 2)
                                wait(T, f"dld{ic % 3}", lv)
                                mms = []
                                for si2 in range(len(htiles)):
                                    si = si2
                                    pd = ps[2 + si2]
                                    if ic == 0:
                                        guard(T, f"bank{2 + si2}")
                                    for pi, (xh_, wh_) in enumerate(
                                            ((0, 0), (0, 1), (1, 0))):
                                        mms.append(T.matmul(
                                            pd[:, :512], hqT[si][xh_][:, ic, :],
                                            wbD[wh_], start=(ic == 0 and pi == 0),
                                            stop=(ic == ICH - 1 and pi == 2)))
                                inc(mms[-1], f"dfree{ic % 3}")
                                used(f"wdn{ic % 3}", f"dfree{ic % 3}")
                                if ic == ICH - 1:
                                    for si2 in range(len(htiles)):
                                        m_ = mms[si2 * 3 + 2]
                                        if si2 == len(htiles) - 1:
                                            pts.append((f"dfree{ic % 3}",
                                                        cnt[f"dfree{ic % 3}"]))
                                        else:
                                            pts.append(("t", inc(m_, "t")))
                            for si2, t in enumerate(htiles):
                                pd = ps[2 + si2]
                                xr = xres[si2 % 2]
                                lvs = f"ld_xres{si2 % 2}"
                                lv = load(xr, src_res(t)[:, dt * 512:(dt + 1) * 512],
                                          key=f"xres{si2 % 2}")
                                wait(A, pts[si2][0], pts[si2][1])
                                sl = si2
                                sgb = stg1[sl]
                                guard(A, f"stg{sl}")
                                ai = A.activation(out=sgb, in_=pd[:, :512],
                                                  func=AF.Copy, scale=0.5)
                                pa = inc(ai, "a")
                                used(f"bank{2 + si2}", "a")
                                wait(G, "a", pa)
                                wait(G, lvs, lv)
                                pg = inc(G.tensor_add(out=sgb, in0=sgb, in1=xr), "g")
                                used(f"xres{si2 % 2}", "g")
                                wait(A, "g", pg)
                                store(dst_of(t)[:, dt * 512:(dt + 1) * 512], sgb, sl)
                                used(f"stg{sl}", f"st{sl}")

                for sw in range(nsw):
                    stiles = stiles_of = tiles[sw * SWEEP:(sw + 1) * SWEEP]
                    if sw == 0:
                        emit_pre(stiles)
                    emit_gateup(stiles, side if sw == 0 else None)
                    if sw + 1 < nsw:
                        emit_pre(tiles[(sw + 1) * SWEEP:(sw + 2) * SWEEP])
                    emit_down(stiles)
                wait_all_stores(SY)

            if "ffn1" in phases: emit_ffn(
                tiles=list(range(NT)),
                side=sweep0_side,
                src_x=lambda t: d_x[t * 128:(t + 1) * 128, :],
                src_res=lambda t: d_x[t * 128:(t + 1) * 128, :],
                dst_of=lambda t: (d_x1ctx[t * 128:(t + 1) * 128, :] if t < OWN0 else
                                  d_x1own[(t - OWN0) * 128:(t - OWN0 + 1) * 128, :]),
            )

            def x1_src(t):
                return (d_x1ctx[t * 128:(t + 1) * 128, :] if t < OWN0 else
                        d_x1own[(t - OWN0) * 128:(t - OWN0 + 1) * 128, :])

            for nm_ in ("t", "dfree0", "dfree1", "dfree2"):
                wait(V, nm_)
            V.memset(a_M[0], 0.0)
            V.memset(a_MT[0], 0.0)
            V.drain()
            inc(V.engine_nop(), "v")
            mcur = {"i": 0}

            cs_state = {"n": 0}

            def cslot():
                bk = 2 + (cs_state["n"] % 2)
                cs_state["n"] += 1
                key = f"bank{bk}"
                pst = ps[bk][:, 0:128]
                guard(T, key)
                return pst, key

            def cmm(lhsT, rhs, dst, extra=None, n=128):
                pst, key = cslot()
                pm = inc(T.matmul(pst[:, 0:n], lhsT, rhs, start=True, stop=True), "csm")
                wait(V, "csm", pm)
                if extra is None:
                    vi = V.tensor_copy(out=dst, in_=pst[:, 0:n])
                else:
                    vi = extra(dst, pst[:, 0:n])
                if vi is None:
                    V.drain()
                    vi = V.engine_nop()
                inc(vi, "v")
                used(key, "v")

            def emit_chunk(si, own, t_own):
                nx = 1 - mcur["i"]
                Mc, MTc = a_M[mcur["i"]], a_MT[mcur["i"]]
                Mn, MTn = a_M[nx], a_MT[nx]
                acol = a_acol[:, si:si + 1]
                bcol = a_bcol[:, si:si + 1]
                Kt = a_K[:, si, :]
                Vt = a_V[:, si, :]
                QTt = a_QT[:, si, :]
                GTt = a_GT[:, si, :]

                wait(A, "v", cnt["v"])
                ai = ai = ai = ai = ai = ai = ai = A.activation(out=a_la, in_=acol, func=AF.Ln)
                pa = inc(ai, "a")
                wait(T, "a", pa)
                cmm(c_ut, a_la, a_cum, n=1)
                wait(T, "v")
                pst, key = cslot()
                ptr = inc(T.transpose(pst[0:1, 0:128], a_cum, c_id), "csm")
                wait(V, "csm", ptr)
                V.tensor_copy(out=a_cumrow, in_=pst[0:1, 0:128])
                V.drain()
                pv = inc(V.engine_nop(), "v")
                used(key, "v")
                wait(T, "v", pv)
                cmm(c_ones, a_cumrow, a_cumrep)
                pv = cnt["v"]
                wait(A, "v", pv)
                A.activation(out=a_gam, in_=a_cum, func=AF.Exp)
                ai = A.activation(out=a_gamL, in_=a_cumrep[:, 127:128], func=AF.Exp)
                pa = inc(ai, "a")
                wait(V, "a", pa)
                V.tensor_sub(out=a_wgt, in0=a_cumrep[:, 127:128], in1=a_cum)
                V.drain()
                pv = inc(V.engine_nop(), "v")
                wait(A, "v", pv)
                ai = A.activation(out=a_wgt, in_=a_wgt, func=AF.Exp)
                pa = inc(ai, "a")
                wait(V, "a", pa)
                V.tensor_mul(out=a_wgt, in0=a_wgt, in1=bcol)
                V.tensor_scalar_mul(out=a_negb, in0=bcol, scalar1=-1.0)
                V.drain()
                V.tensor_scalar(out=a_t1, in0=a_cumrep, scalar1=a_cum, scalar2=None,
                                op0=AL.subtract)
                V.drain()
                V.tensor_add(out=a_t1, in0=a_t1, in1=c_aiu)
                V.drain()
                pv = inc(V.engine_nop(), "v")
                wait(A, "v", pv)
                ai = A.activation(out=a_gtincl, in_=a_t1, func=AF.Exp)
                if own:
                    ai = A.activation(out=a_gamrep, in_=a_cumrep, func=AF.Exp)
                pa = inc(ai, "a")
                wait(V, "a", pa)
                V.tensor_sub(out=a_t1, in0=a_gtincl, in1=c_id)
                V.drain()
                pv = inc(V.engine_nop(), "v")

                wait(T, "v", pv)
                transpose128(Kt, a_KT)
                if own:
                    transpose128(Vt, a_VT)
                wait(T, "tpe")
                cmm(a_KT, a_KT, a_kk)
                V.drain()
                V.tensor_mul(out=s_nb[0], in0=a_t1, in1=a_kk)
                V.drain()
                V.tensor_scalar_mul(out=s_nb[0], in0=s_nb[0], scalar1=a_negb)
                V.tensor_scalar_mul(out=a_gk, in0=Kt, scalar1=a_gam)
                V.tensor_scalar_mul(out=a_kw, in0=Kt, scalar1=a_wgt)
                V.drain()
                pv = inc(V.engine_nop(), "v")
                wait(T, "v", pv)
                transpose128(s_nb[0], s_na[0])
                wait(V, "tpe")

                V.tensor_copy(out=a_Z, in_=a_kw)
                if own:
                    V.tensor_copy(out=a_Wm, in_=a_gk)
                V.drain()
                pv = inc(V.engine_nop(), "v")
                if own:
                    wait(T, "v", pv)
                    wait(T, "tpe")
                    cmm(a_VT, QTt, a_PF)
                cur = 0
                for m in range(7):
                    wait(T, "v")
                    cmm(s_na[cur], a_Z, a_Z,
                        extra=lambda d, p: V.tensor_add(out=d, in0=d, in1=p))
                    if own:
                        cmm(s_nb[cur], a_Wm, a_Wm,
                            extra=lambda d, p: V.tensor_add(out=d, in0=d, in1=p))
                        cmm(s_nb[cur], a_PF, a_PF,
                            extra=lambda d, p: V.tensor_add(out=d, in0=d, in1=p))
                    if m < 6:
                        wait(T, "v")
                        cmm(s_nb[cur], s_na[cur], s_na[1 - cur])
                        cmm(s_na[cur], s_nb[cur], s_nb[1 - cur])
                        cur = 1 - cur

                wait(T, "v")
                cmm(a_gk, a_Z, a_ET)
                cmm(a_Z, Vt, a_F)
                cmm(Vt, a_Z, a_FT)
                if own:
                    transpose128(a_Wm, a_WT)

                if own:
                    wait(T, "v")
                    cmm(MTc, QTt, a_MQ)
                    wait(T, "v")
                    wait(T, "tpe")
                    cmm(a_WT, a_MQ, a_BT,
                        extra=lambda d, p: V.scalar_tensor_tensor(
                            out=d, in0=p, scalar=-1.0, in1=a_PF,
                            op0=AL.mult, op1=AL.add))
                    V.drain()
                    V.tensor_mul(out=a_BT, in0=a_BT, in1=a_gtincl)
                    V.drain()
                    V.tensor_scalar_mul(out=a_BT, in0=a_BT, scalar1=bcol)
                    V.drain()
                    pv = inc(V.engine_nop(), "v")
                    wait(T, "v", pv)
                    cmm(Kt, a_BT, a_t2)
                    V.tensor_mul(out=a_t1, in0=a_MQ, in1=a_gamrep)
                    V.drain()
                    V.tensor_add(out=a_t1, in0=a_t1, in1=a_t2)
                    V.drain()
                    V.tensor_mul(out=a_OT, in0=a_t1, in1=GTt)
                    V.drain()
                    pv = inc(V.engine_nop(), "v")
                    wait(T, "v", pv)
                    for half in range(D // 512):
                        guard(T, "bank5")
                        mmO = T.matmul(ps[5][:, 0:512], a_OT,
                                       w_awo[:, half * 512:(half + 1) * 512],
                                       start=True, stop=True)
                        pt = inc(mmO, "t")
                        xr = xres[half % 2]
                        lvs = f"ld_xres{half % 2}"
                        lv = load(xr, d_x1own[t_own * 128:(t_own + 1) * 128,
                                              half * 512:(half + 1) * 512],
                                  key=f"xres{half % 2}")
                        wait(V, lvs, lv)
                        wait(V, "t", pt)
                        sgb = stg1[half]
                        guard(V, f"stg{half}")
                        V.tensor_add(out=sgb, in0=ps[5][:, 0:512], in1=xr)
                        V.drain()
                        V.tensor_add(out=sgb, in0=sgb,
                                     in1=b_o[:, half * 512:(half + 1) * 512])
                        V.drain()
                        pv = inc(V.engine_nop(), "v")
                        used("bank5", "v")
                        used(f"xres{half % 2}", "v")
                        wait(A, "v", pv)
                        store(d_x2own[t_own * 128:(t_own + 1) * 128,
                                      half * 512:(half + 1) * 512], sgb, half)
                        used(f"stg{half}", f"st{half}")

                wait(T, "v")
                cmm(a_ET, Mc, a_t1,
                    extra=lambda d, p: V.scalar_tensor_tensor(
                        out=d, in0=p, scalar=-1.0, in1=a_F, op0=AL.mult, op1=AL.add))
                cmm(Mc, a_ET, a_t2,
                    extra=lambda d, p: V.scalar_tensor_tensor(
                        out=d, in0=p, scalar=-1.0, in1=a_FT, op0=AL.mult, op1=AL.add))
                V.drain()
                V.scalar_tensor_tensor(out=Mn, in0=Mc, scalar=a_gamL, in1=a_t1,
                                       op0=AL.mult, op1=AL.add)
                V.scalar_tensor_tensor(out=MTn, in0=MTc, scalar=a_gamL, in1=a_t2,
                                       op0=AL.mult, op1=AL.add)
                V.drain()
                inc(V.engine_nop(), "v")
                mcur["i"] = nx

            ngroups = NT // GROUP if "attn" in phases else 0
            for g in range(ngroups):
                gtiles = list(range(g * GROUP, (g + 1) * GROUP))
                own_g = gtiles[0] >= OWN0
                wait(A, "t")
                wait(A, "tpm")
                wait(V, "t")
                for si, t in enumerate(gtiles):
                    b = xin[si % 2]
                    lvs = f"ld_xin{si % 2}"
                    lv = load(b[:, 0:D], x1_src(t), key=f"xin{si % 2}")
                    wait(V, lvs, lv)
                    ln_tile(b, scr4, srckey=f"xin{si % 2}")
                    wait(T, "v")
                    for dc in range(DCH):
                        transpose128(scr4[:, dc * 128:(dc + 1) * 128],
                                     a_xh2T[:, dc, si * 128:(si + 1) * 128])
                    wait(V, "tpe")
                wait(T, "tpe")

                for si, t in enumerate(gtiles):
                    pp = ps[si % 2]
                    guard(T, f"bank{si % 2}")
                    for dc in range(DCH):
                        mm = T.matmul(pp[:, 0:258],
                                      a_xh2T[:, dc, si * 128:(si + 1) * 128],
                                      w_awkvab[:, dc, :], start=(dc == 0),
                                      stop=(dc == DCH - 1))
                    pt = inc(mm, "t")
                    wait(V, "t", pt)
                    V.tensor_add(out=scr1[:, 0:256], in0=pp[:, 0:256],
                                 in1=b_kvab[:, 0:256])
                    V.drain()
                    pv = inc(V.engine_nop(), "v")
                    wait(A, "v", pv)
                    A.activation(out=scr2[:, 0:128], in_=scr1[:, 0:128], func=AF.Square,
                                 accum_out=sm_s[:, 0:1])
                    A.drain()
                    A.activation(out=sm_s[:, 1:2], in_=sm_s[:, 0:1], func=AF.Sqrt)
                    A.activation(out=a_acol[:, si:si + 1], in_=pp[:, 256:257],
                                 func=AF.Sigmoid, bias=b_a)
                    ai = A.activation(out=a_bcol[:, si:si + 1], in_=pp[:, 257:258],
                                      func=AF.Sigmoid, bias=b_b)
                    pa = inc(ai, "a")
                    used(f"bank{si % 2}", "a")
                    wait(V, "a", pa)
                    V.tensor_scalar_max(out=sm_s[:, 1:2], in0=sm_s[:, 1:2],
                                        scalar1=1e-12)
                    V.drain()
                    pv = inc(V.engine_nop(), "v")
                    wait(A, "v", pv)
                    V.reciprocal(out=sm_r[:, 0:1], in_=sm_s[:, 1:2])
                    V.drain()
                    V.tensor_scalar_mul(out=a_K[:, si, :], in0=scr1[:, 0:128],
                                        scalar1=sm_r[:, 0:1])
                    V.tensor_copy(out=a_V[:, si, :], in_=scr1[:, 128:256])
                    V.drain()
                    inc(V.engine_nop(), "v")
                if own_g:
                    for si, t in enumerate(gtiles):
                        pp = ps[si % 2]
                        guard(T, f"bank{si % 2}")
                        for dc in range(DCH):
                            T.matmul(pp[:, 0:128], w_awq[:, dc, :],
                                     a_xh2T[:, dc, si * 128:(si + 1) * 128],
                                     start=(dc == 0), stop=(dc == DCH - 1))
                        for dc in range(DCH):
                            mm = T.matmul(pp[:, 128:256], w_awg[:, dc, :],
                                          a_xh2T[:, dc, si * 128:(si + 1) * 128],
                                          start=(dc == 0), stop=(dc == DCH - 1))
                        pt = inc(mm, "t")
                        wait(V, "t", pt)
                        V.tensor_scalar(out=a_QT[:, si, :], in0=pp[:, 0:128],
                                        scalar1=b_q, scalar2=None, op0=AL.add)
                        V.drain()
                        pv = inc(V.engine_nop(), "v")
                        wait(A, "v", pv)
                        ai = A.activation(out=a_GT[:, si, :], in_=pp[:, 128:256],
                                          func=AF.Sigmoid, bias=b_g)
                        pa = inc(ai, "a")
                        used(f"bank{si % 2}", "a")
                        wait(V, "a", pa)

                for si, t in enumerate(gtiles):
                    emit_chunk(si, own_g, (t - OWN0) if own_g else 0)

            wait_all_stores(SY)
            wait(V, "t")
            if "ffn2" in phases: emit_ffn(
                tiles=list(range(NOWN)),
                src_x=lambda t: d_x2own[t * 128:(t + 1) * 128, :],
                src_res=lambda t: d_x2own[t * 128:(t + 1) * 128, :],
                dst_of=lambda t: d_out[t * 128:(t + 1) * 128, :],
            )

            if "ffn2" not in phases:
                src_dump = d_x2own if "attn" in phases else d_x1own
                for t in range(0, NOWN * 128, 128):
                    load(xin[0][:, 0:D], src_dump[t:t + 128, :],
                         s="dbg_ld", key="xin0")
                    wait(A, "dbg_ld")
                    store(d_out[t:t + 128, :], xin[0][:, 0:D], 0)
                    used("xin0", "st0")
                    wait(SY, "st0")
            wait_all_stores(SY)

    return nc


def make_host_inputs(cfg, ins):
    NT, OWN0, D, I = cfg["NTILES"], cfg["OWN0"], cfg["D"], cfg["I"]
    NOWN = NT - OWN0
    x = np.ascontiguousarray(np.asarray(ins["L0_attn_x"], dtype=np.float32))
    B = x.shape[0]
    Wk = np.asarray(ins["L0_attn_Wk"], np.float32)
    Wv = np.asarray(ins["L0_attn_Wv"], np.float32)
    Wa = np.asarray(ins["L0_attn_Wa"], np.float32)
    Wb = np.asarray(ins["L0_attn_Wb"], np.float32)
    wkvab = np.ascontiguousarray(np.concatenate([Wk, Wv, Wa, Wb], axis=1))
    bk = np.asarray(ins["L0_attn_bk"], np.float32)
    bv = np.asarray(ins["L0_attn_bv"], np.float32)
    bkvab = np.ascontiguousarray(
        np.broadcast_to(np.concatenate([bk, bv, [0.0, 0.0]]).astype(np.float32)[None, :],
                        (128, 258)))
    common = {
        "wg": np.ascontiguousarray(np.asarray(ins["L0_ffn_w_gate"], np.float32)),
        "wu": np.ascontiguousarray(np.asarray(ins["L0_ffn_w_up"], np.float32)),
        "wd": np.ascontiguousarray(np.asarray(ins["L0_ffn_w_down"], np.float32)),
        "awq": np.ascontiguousarray(np.asarray(ins["L0_attn_Wq"], np.float32)),
        "awg": np.ascontiguousarray(np.asarray(ins["L0_attn_Wg"], np.float32)),
        "awkvab": wkvab,
        "awo": np.ascontiguousarray(np.asarray(ins["L0_attn_Wo"], np.float32)),
        "bkvab": bkvab,
        "bqcol": np.ascontiguousarray(np.asarray(ins["L0_attn_bq"], np.float32)[:, None]),
        "bgcol": np.ascontiguousarray(np.asarray(ins["L0_attn_bg"], np.float32)[:, None]),
        "bacol": np.ascontiguousarray(
            np.broadcast_to(np.asarray(ins["L0_attn_ba"], np.float32), (128,))[:, None]),
        "bbcol": np.ascontiguousarray(
            np.broadcast_to(np.asarray(ins["L0_attn_bb"], np.float32), (128,))[:, None]),
        "borep": np.ascontiguousarray(
            np.broadcast_to(np.asarray(ins["L0_attn_bo"], np.float32)[None, :], (128, D))),
        "c_ut": np.triu(np.ones((128, 128), np.float32)),
        "c_id": np.eye(128, dtype=np.float32),
        "c_idb": np.eye(128, dtype=np.float32).astype(ml_dtypes.bfloat16),
        "c_aiu": ((np.triu(np.ones((128, 128), np.float32)) - 1.0) * 1e9).astype(np.float32),
        "c_ones1": np.ones((1, 128), np.float32),
    }
    half = OWN0 * 128
    in_maps = []
    for c in range(2 * B):
        bi, hi = c // 2, c % 2
        xa = np.zeros((NT * 128, D), np.float32)
        if hi == 0:
            xa[half:half + NOWN * 128] = x[bi, 0:NOWN * 128]
        else:
            xa[0:half] = x[bi, 0:half]
            xa[half:half + NOWN * 128] = x[bi, half:half + NOWN * 128]
        in_maps.append(dict(common, xall=np.ascontiguousarray(xa)))
    return in_maps


def assemble(cfg, results, B, T):
    NOWN = cfg["NTILES"] - cfg["OWN0"]
    D = cfg["D"]
    out = np.zeros((B, T, D), np.float32)
    half = cfg["OWN0"] * 128
    for c, r in enumerate(results):
        bi, hi = c // 2, c % 2
        if hi == 0:
            out[bi, 0:NOWN * 128] = r["yout"]
        else:
            out[bi, half:half + NOWN * 128] = r["yout"]
    return out


from concourse.bass_utils import run_bass_kernel_spmd

_NC_CACHE = {}


def kernel(**inputs):
    cfg = FULL
    if "nc" not in _NC_CACHE:
        _NC_CACHE["nc"] = build_nc(cfg)
    nc = _NC_CACHE["nc"]
    in_maps = make_host_inputs(cfg, inputs)
    res = run_bass_kernel_spmd(nc, in_maps, core_ids=list(range(8)))
    x = np.asarray(inputs["L0_attn_x"])
    return assemble(cfg, res.results, x.shape[0], x.shape[1])
